# version 04
# baseline (speedup 1.0000x reference)
import contextlib
import numpy as np
import ml_dtypes
import concourse.bass as bass
import concourse.mybir as mybir
from concourse.bass_utils import run_bass_kernel_spmd

F32 = mybir.dt.float32
BF16 = mybir.dt.bfloat16
I32 = mybir.dt.int32
AF = mybir.ActivationFunctionType
ALU = mybir.AluOpType
AX = mybir.AxisListType

NCORES = 8
B = 4
L = 2048
D = 1024
H = 16
HP = 8
DH = 64
DS = 16
INNER = 4096
CH = D // 128
MCH = INNER // 128
TT = 512
NT = (B * L) // TT
NSUB = TT // 128
EPS = 1e-5
TOK = B * L


SIM_SMALL = False


def build_nc(dbg=False):
    NTs = 2 if SIM_SMALL else NT
    NITs = 2 if SIM_SMALL else B * HP
    nc = bass.Bass("TRN2", target_bir_lowering=False)

    x_h = nc.dram_tensor("x", [B, L, D], F32, kind="ExternalInput").ap()
    acols_h = nc.dram_tensor("acols", [128, 128], F32, kind="ExternalInput").ap()
    bdt_h = nc.dram_tensor("bdt", [64, 1], F32, kind="ExternalInput").ap()
    wx_h = nc.dram_tensor("wx", [96, 64], F32, kind="ExternalInput").ap()
    wdtT_h = nc.dram_tensor("wdtT", [64, 64], F32, kind="ExternalInput").ap()
    wxbcT_h = nc.dram_tensor("wxbcT", [64, 32], F32, kind="ExternalInput").ap()
    woutT_h = nc.dram_tensor("woutT", [D, D], F32, kind="ExternalInput").ap()
    bout_h = nc.dram_tensor("boutc", [128, CH], F32, kind="ExternalInput").ap()
    wgT_h = nc.dram_tensor("wgT", [D, INNER], F32, kind="ExternalInput").ap()
    wuT_h = nc.dram_tensor("wuT", [D, INNER], F32, kind="ExternalInput").ap()
    wdT_h = nc.dram_tensor("wdT", [INNER, D], F32, kind="ExternalInput").ap()
    onesr_h = nc.dram_tensor("onesr", [1, 128], BF16, kind="ExternalInput").ap()
    onesc_h = nc.dram_tensor("onesc", [128, 1], BF16, kind="ExternalInput").ap()
    idm_h = nc.dram_tensor("idm", [128, 128], BF16, kind="ExternalInput").ap()
    out_h = nc.dram_tensor("out", [B, L, D], F32, kind="ExternalOutput").ap()

    KD = "ExternalOutput" if dbg else "Internal"
    s_dt = nc.dram_tensor("s_dt", [B, H, DH, L], F32, kind=KD).ap()
    s_z = nc.dram_tensor("s_z", [B, H, DH, L], F32, kind=KD).ap()
    s_bc = nc.dram_tensor("s_bc", [B, H, 2, DS, L], BF16, kind=KD).ap()
    s_y = nc.dram_tensor("s_y", [CH, 128, TOK], BF16, kind=KD).ap()
    s_x2 = nc.dram_tensor("s_x2", [TOK // 128, 128, D], F32, kind=KD).ap()
    s_xq = nc.dram_tensor("s_xq", [CH, 128, TOK], BF16, kind=KD).ap()
    s_gu = nc.dram_tensor("s_gu", [MCH, 128, TOK], BF16, kind=KD).ap()
    s_wgq = nc.dram_tensor("s_wgq", [D, INNER], BF16, kind=KD).ap()
    s_wuq = nc.dram_tensor("s_wuq", [D, INNER], BF16, kind=KD).ap()
    s_wdq = nc.dram_tensor("s_wdq", [INNER, D], BF16, kind=KD).ap()
    s_stage = nc.dram_tensor("s_stage", [1, 16], F32, kind=KD).ap()
    s_rg = nc.dram_tensor("s_rg", [NT, 8, 128], F32, kind=KD).ap()
    s_am = nc.dram_tensor("s_am", [2, MCH, TOK], F32, kind=KD).ap()
    s_dbg = nc.dram_tensor("s_dbg", [NT, 128, 40], F32, kind=KD).ap()
    s_dbg2 = nc.dram_tensor("s_dbg2", [NT, 128, 4096], BF16, kind=KD).ap()
    s_dbg3 = nc.dram_tensor("s_dbg3", [NT, 128, 4096], BF16, kind=KD).ap()

    g_st = contextlib.ExitStack()

    def mk(stk):
        return (lambda n, s, d: stk.enter_context(nc.sbuf_tensor(n, s, d)),
                lambda n, s, d: stk.enter_context(nc.psum_tensor(n, s, d)))

    sbuf, psum = mk(g_st)

    ascale = sbuf("ascale", [128, 128], F32)
    bdt_c = sbuf("bdt_c", [64, 1], F32)
    lhsT_f = sbuf("lhsT_f", [64, 96], F32)
    lhsT_b = sbuf("lhsT_b", [64, 96], BF16)
    bout_c = sbuf("bout_c", [128, CH], F32)
    onesr = sbuf("onesr_s", [1, 128], BF16)
    onesc = sbuf("onesc_s", [128, 1], BF16)
    idm = sbuf("idm_s", [128, 128], BF16)
    idm_f = sbuf("idmf_s", [128, 128], F32)
    wq_cols = sbuf("wq_cols", [128, 8], F32)
    stat = sbuf("stat", [128, 40], F32)
    tmpI = sbuf("tmpI", [128, 2 * D], I32)
    cc = sbuf("cc", [128, 4], F32)
    junk = sbuf("junk", [128, 512], F32)
    rstdw = sbuf("rstdw", [128, 512], F32)

    qs_h = g_st.enter_context(nc.semaphore("qs"))
    qg_h = g_st.enter_context(nc.semaphore("qg"))
    ev_h = g_st.enter_context(nc.semaphore("ev"))
    ea_h = g_st.enter_context(nc.semaphore("ea"))
    eo_h = g_st.enter_context(nc.semaphore("eo"))
    ep_h = g_st.enter_context(nc.semaphore("ep"))
    ed_h = g_st.enter_context(nc.semaphore("ed"))
    qn = {"qs": 0, "qg": 0}
    evn = [0]
    ean = [0]
    eon = [0]
    epn = [0]
    edn = [0]

    def dmark(inst):
        inst.then_inc(ed_h, 1)
        edn[0] += 1

    def dmarkd(eng):
        eng.drain().then_inc(ed_h, 1)
        edn[0] += 1

    def pmark(inst):
        inst.then_inc(ep_h, 1)
        epn[0] += 1

    def amark(inst):
        inst.then_inc(ea_h, 1)
        ean[0] += 1

    def omark(inst):
        inst.then_inc(eo_h, 1)
        eon[0] += 1

    def dma(eng, dst, src, q):
        h = qs_h if q == "qs" else qg_h
        eng.wait_ge(h, qn[q])
        eng.dma_start(out=dst, in_=src).then_inc(h, 16)
        qn[q] += 16

    su_st = contextlib.ExitStack()
    s_sbuf, s_psum = mk(su_st)
    tmpA = s_sbuf("tmpA", [128, 2048], F32)
    tmpC = s_sbuf("tmpC", [128, 2048], BF16)
    tmpC2 = s_sbuf("tmpC2", [128, 2048], BF16)
    wld = [s_sbuf(f"wld{i}", [128, 2048], F32) for i in range(2)]
    wabs = s_sbuf("wabs", [128, 2048], F32)
    onesc32 = s_sbuf("onesc32", [128, 1], F32)
    p_small = s_psum("p_small", [96, 512], F32)
    p_row = s_psum("p_row", [1, 512], F32)

    with nc.Block() as blk:
        @blk.sync
        def _(s):
            dma(s, ascale[:], acols_h[:], "qs")
            dma(s, bdt_c[:], bdt_h[:], "qs")
            dma(s, tmpA[0:96, 0:64], wx_h[:], "qs")
            dma(s, tmpA[0:64, 64:128], wdtT_h[:], "qs")
            dma(s, lhsT_f[:, 64:96], wxbcT_h[:], "qs")
            dma(s, bout_c[:], bout_h[:], "qs")
            dma(s, onesr[:], onesr_h[:], "qs")
            dma(s, onesc[:], onesc_h[:], "qs")
            dma(s, idm[:], idm_h[:], "qs")
            s.wait_ge(qs_h, qn["qs"])
        @blk.gpsimd
        def _(g):
            g.memset(cc[:, 0:1], EPS)
            g.memset(cc[:, 1:2], -0.5)
            g.memset(cc[:, 2:3], -1.0)
            g.memset(junk[:], 0.0)
            g.memset(stat[:], 0.0)
            g.memset(rstdw[:], 0.0)
    with nc.Block() as blk:
        @blk.scalar
        def _(sc):
            sc.activation(tmpA[0:128, 128:256], ascale[:], AF.Exp)
            amark(sc.activation(junk[:], junk[:], AF.Copy))
        @blk.tensor
        def _(t):
            pmark(t.matmul(p_small[0:64, 0:64], tmpA[0:64, 0:64], tmpA[0:64, 64:128],
                           start=True, stop=True))
    with nc.Block() as blk:
        @blk.vector
        def _(v):
            v.wait_ge(ea_h, ean[0])
            v.wait_ge(ep_h, epn[0])
            v.tensor_scalar_mul(ascale[:], tmpA[0:128, 128:256], -1.0)
            v.tensor_copy(lhsT_f[:, 0:64], p_small[0:64, 0:64])
    with nc.Block() as blk:
        @blk.vector
        def _(v):
            v.tensor_copy(lhsT_b[:], lhsT_f[:])
            v.tensor_copy(onesc32[:], onesc[:])
            dmark(v.tensor_copy(idm_f[:], idm[:]))

    w_list = [(wgT_h, s_wgq, D, INNER, 0), (wuT_h, s_wuq, D, INNER, 1),
              (wdT_h, s_wdq, INNER, D, 2)]
    for (src, _d, R, Cc, idx) in w_list:
        CW = min(2048, Cc)
        cpr = Cc // CW
        ntile = (R // 128) * cpr
        if SIM_SMALL:
            ntile = 2
        njr = CW // 512
        for it in range(ntile + 1):
            with nc.Block() as blk:
                @blk.sync
                def _(s):
                    if it < ntile:
                        r, c = it // cpr, it % cpr
                        dma(s, wld[it % 2][:, 0:CW],
                            src[r * 128:(r + 1) * 128, c * CW:(c + 1) * CW], "qs")
                    s.wait_ge(qs_h, qn["qs"])
                @blk.scalar
                def _(sc):
                    if it >= 1:
                        amark(sc.activation(wabs[:, 0:CW], wld[(it - 1) % 2][:, 0:CW],
                                            AF.Abs))
            if it >= 1:
                with nc.Block() as blk:
                    @blk.tensor
                    def _(t):
                        t.wait_ge(ea_h, ean[0])
                        mm = None
                        for j in range(njr):
                            mm = t.matmul(p_row[0:1, 0:512], onesc32[:],
                                          wabs[:, j * 512:(j + 1) * 512],
                                          start=(it == 1 and j == 0),
                                          stop=(it == ntile and j == njr - 1))
                        if it == ntile:
                            pmark(mm)
        with nc.Block() as blk:
            @blk.vector
            def _(v):
                v.wait_ge(ep_h, epn[0])
                v.reduce_sum(tmpA[0:1, 0:1], p_row[0:1, 0:512], axis=AX.X)
        with nc.Block() as blk:
            @blk.vector
            def _(v):
                v.tensor_scalar(tmpA[0:1, 1:2], tmpA[0:1, 0:1],
                                1.0 / (R * Cc), EPS, ALU.mult, ALU.max)
        with nc.Block() as blk:
            @blk.vector
            def _(v):
                dmark(v.reciprocal(tmpA[0:1, 2:3], tmpA[0:1, 1:2]))
        with nc.Block() as blk:
            @blk.gpsimd
            def _(g):
                g.wait_ge(ed_h, edn[0])
                dma(g, s_stage[0:1, idx * 2:idx * 2 + 2], tmpA[0:1, 1:3], "qg")
                g.wait_ge(qg_h, qn["qg"])
    with nc.Block() as blk:
        @blk.sync
        def _(s):
            for idx in range(3):
                dma(s, wq_cols[:, 4 + idx:5 + idx],
                    bass.AP(s_stage.tensor, idx * 2, [[0, 128], [1, 1]]), "qs")
                dma(s, wq_cols[:, idx:idx + 1],
                    bass.AP(s_stage.tensor, idx * 2 + 1, [[0, 128], [1, 1]]), "qs")
            s.wait_ge(qs_h, qn["qs"])

    for (src, dst, R, Cc, idx) in w_list:
        CW = min(2048, Cc)
        cpr = Cc // CW
        ntile = (R // 128) * cpr
        if SIM_SMALL:
            ntile = 2
        for it in range(ntile + 2):
            with nc.Block() as blk:
                @blk.sync
                def _(s):
                    if it < ntile:
                        r, c = it // cpr, it % cpr
                        dma(s, wld[it % 2][:, 0:CW],
                            src[r * 128:(r + 1) * 128, c * CW:(c + 1) * CW], "qs")
                    s.wait_ge(qs_h, qn["qs"])
                @blk.vector
                def _(v):
                    if 1 <= it <= ntile:
                        j = it - 1
                        qb = tmpC if j % 2 == 0 else tmpC2
                        v.tensor_scalar_mul(tmpA[:, 0:CW], wld[j % 2][:, 0:CW],
                                            wq_cols[:, idx:idx + 1])
                        v.drain()
                        v.tensor_copy(tmpI[:, 0:CW], tmpA[:, 0:CW])
                        v.drain()
                        v.tensor_scalar(tmpI[:, 0:CW], tmpI[:, 0:CW], 1, -1,
                                        ALU.min, ALU.max)
                        v.drain()
                        dmark(v.tensor_copy(qb[:, 0:CW], tmpI[:, 0:CW]))
                @blk.gpsimd
                def _(g):
                    if it >= 2:
                        g.wait_ge(ed_h, edn[0])
                        j = it - 2
                        r, c = j // cpr, j % cpr
                        qb = tmpC if j % 2 == 0 else tmpC2
                        dma(g, dst[r * 128:(r + 1) * 128, c * CW:(c + 1) * CW],
                            qb[:, 0:CW], "qg")
                        g.wait_ge(qg_h, qn["qg"])
    su_st.close()

    p1_st = contextlib.ExitStack()
    p_sbuf, p_psum = mk(p1_st)
    xt = [p_sbuf(f"xt{i}", [128, NSUB * D], F32) for i in range(2)]
    sqb = p_sbuf("sqb", [128, 4 * D], F32)
    xn1T = p_sbuf("xn1T", [128, NSUB * D], BF16)
    xf = p_sbuf("xf", [64, NSUB * 2048], BF16)
    dt_p1 = [p_sbuf(f"dt_p1{i}", [64, 2048], F32) for i in range(2)]
    z_p1 = [p_sbuf(f"z_p1{i}", [64, 2048], F32) for i in range(2)]
    bc_p1 = [p_sbuf(f"bc_p1{i}", [32, 2048], BF16) for i in range(2)]
    sfp = [p_sbuf(f"sfp{i}", [64, 2048], F32) for i in range(3)]
    p_trA = p_psum("p_trA", [128, 2048], BF16)
    p_trB = p_psum("p_trB", [128, 2048], BF16)
    p_pj = p_psum("p_pj", [96, 2048], F32)

    x_flat = x_h.rearrange("b l d -> (b l) d")

    for g_i in range(NTs):
        with nc.Block() as blk:
            @blk.sync
            def _(s):
                for ssi in range(NSUB):
                    t0 = g_i * TT + ssi * 128
                    dma(s, xt[g_i % 2][:, ssi * D:(ssi + 1) * D],
                        x_flat[t0:t0 + 128, :], "qs")
                s.wait_ge(qs_h, qn["qs"])
        xcur = xt[g_i % 2]
        b_i = g_i // 4
        tbase = (g_i % 4) * TT
        with nc.Block() as blk:
            @blk.scalar
            def _(sc):
                amark(sc.activation(sqb[:], xcur[:], AF.Square))
            @blk.vector
            def _(v):
                v.reduce_sum(stat[:, 0:4],
                             xcur.ap().rearrange("p (s d) -> p s d", s=NSUB),
                             axis=AX.X)
        with nc.Block() as blk:
            @blk.vector
            def _(v):
                v.wait_ge(ea_h, ean[0])
                v.reduce_sum(stat[:, 4:8],
                             sqb.ap().rearrange("p (s d) -> p s d", s=NSUB),
                             axis=AX.X)
        with nc.Block() as blk:
            @blk.vector
            def _(v):
                v.tensor_scalar_mul(stat[:, 0:4], stat[:, 0:4], 1.0 / D)
                v.tensor_scalar_mul(stat[:, 4:8], stat[:, 4:8], 1.0 / D)
                v.drain()
                v.tensor_mul(stat[:, 8:12], stat[:, 0:4], stat[:, 0:4])
                v.drain()
                dmark(v.tensor_sub(stat[:, 8:12], stat[:, 4:8], stat[:, 8:12]))
        with nc.Block() as blk:
            @blk.scalar
            def _(sc):
                sc.wait_ge(ed_h, edn[0])
                lnsrc = bass.AP(stat.ap().tensor, 8, [[40, 128], [0, 128], [1, 4]])
                sc.activation(junk.ap().rearrange("p (r c) -> p r c", c=4), lnsrc,
                              AF.Ln, bias=cc[:, 0:1])
                sc.drain()
                amark(sc.activation(rstdw[:], junk[:], AF.Exp, scale=cc[:, 1:2]))
        with nc.Block() as blk:
            @blk.vector
            def _(v):
                v.wait_ge(ea_h, ean[0])
                li = None
                for ssi in range(NSUB):
                    li = v.tensor_scalar(xn1T[:, ssi * D:(ssi + 1) * D],
                                         xcur[:, ssi * D:(ssi + 1) * D],
                                         stat[:, ssi:ssi + 1], rstdw[:, ssi:ssi + 1],
                                         ALU.subtract, ALU.mult)
                dmark(li)
            @blk.gpsimd
            def _(g):
                dma(g, s_dbg[g_i, :, 0:16], stat[:, 0:16], "qg")
                g.wait_ge(qg_h, qn["qg"])
        for half in range(2):
            subs = (0, 1) if half == 0 else (2, 3)
            with nc.Block() as blk:
                @blk.tensor
                def _(t):
                    t.wait_ge(ed_h, edn[0])
                    tr = None
                    for j, ssi in enumerate(subs):
                        for c in range(CH):
                            pt = p_trA if c < 4 else p_trB
                            col = (j * 4 + (c % 4)) * 128
                            tr = t.transpose(pt[:, col:col + 128],
                                             xn1T[:, ssi * D + c * 128:ssi * D + (c + 1) * 128],
                                             idm[:])
                    pmark(tr)
            with nc.Block() as blk:
                @blk.scalar
                def _(sc):
                    sc.wait_ge(ep_h, epn[0])
                    for j, ssi in enumerate(subs):
                        for lo, pt in ((0, p_trA), (4, p_trB)):
                            src0 = pt.ap()[0:64, j * 512:(j + 1) * 512]\
                                .rearrange("p (c t) -> p c t", c=4)
                            src1 = pt.ap()[64:128, j * 512:(j + 1) * 512]\
                                .rearrange("p (c t) -> p c t", c=4)
                            dsts = xf.ap()[0:64, ssi * 2048:(ssi + 1) * 2048]\
                                .rearrange("p (h t) -> p h t", h=16)
                            sc.activation(dsts[:, 2 * lo:2 * lo + 7:2, :], src0, AF.Copy)
                            amark(sc.activation(dsts[:, 2 * lo + 1:2 * lo + 8:2, :],
                                                src1, AF.Copy))
        for ssi in range(NSUB):
            with nc.Block() as blk:
                @blk.tensor
                def _(t):
                    t.wait_ge(ea_h, ean[0])
                    mm = None
                    for h_i in range(16):
                        mm = t.matmul(p_pj[:, h_i * 128:(h_i + 1) * 128], lhsT_b[:],
                                      xf[0:64, ssi * 2048 + h_i * 128:ssi * 2048 + (h_i + 1) * 128],
                                      start=True, stop=True)
                    pmark(mm)
            with nc.Block() as blk:
                @blk.scalar
                def _(sc):
                    sc.wait_ge(ep_h, epn[0])
                    sc.activation(sfp[0][:], p_pj[0:64, :], AF.Abs, bias=bdt_c[:])
                @blk.vector
                def _(v):
                    v.wait_ge(ep_h, epn[0])
                    v.tensor_scalar(sfp[1][:], p_pj[0:64, :], bdt_c[:], 0.0,
                                    ALU.add, ALU.max)
                    dmark(v.tensor_copy(bc_p1[ssi % 2][:], p_pj[64:96, :]))
            with nc.Block() as blk:
                @blk.scalar
                def _(sc):
                    sc.activation(sfp[2][:], sfp[0][:], AF.Exp, scale=cc[0:64, 2:3])
                    sc.drain()
                    amark(sc.activation(sfp[0][:], sfp[2][:], AF.Ln, bias=1.0))
            with nc.Block() as blk:
                @blk.vector
                def _(v):
                    v.wait_ge(ea_h, ean[0])
                    v.tensor_add(dt_p1[ssi % 2][:], sfp[0][:], sfp[1][:])
                    v.drain()
                    dmark(v.tensor_mul(z_p1[ssi % 2][:], dt_p1[ssi % 2][:],
                                       xf[0:64, ssi * 2048:(ssi + 1) * 2048]))
            with nc.Block() as blk:
                @blk.gpsimd
                def _(g):
                    g.wait_ge(ed_h, edn[0])
                    t0 = tbase + ssi * 128
                    dma(g, bass.AP(s_dt.tensor, b_i * (H * DH * L) + t0,
                                   [[L, 64], [DH * L, 16], [1, 128]]),
                        dt_p1[ssi % 2].ap().rearrange("p (h t) -> p h t", h=16), "qg")
                    dma(g, bass.AP(s_z.tensor, b_i * (H * DH * L) + t0,
                                   [[L, 64], [DH * L, 16], [1, 128]]),
                        z_p1[ssi % 2].ap().rearrange("p (h t) -> p h t", h=16), "qg")
                    dma(g, bass.AP(s_bc.tensor, b_i * (H * 2 * DS * L) + t0,
                                   [[L, 16], [2 * DS * L, 16], [1, 128]]),
                        bc_p1[ssi % 2].ap()[0:16, :].rearrange("p (h t) -> p h t", h=16),
                        "qg")
                    dma(g, bass.AP(s_bc.tensor, b_i * (H * 2 * DS * L) + DS * L + t0,
                                   [[L, 16], [2 * DS * L, 16], [1, 128]]),
                        bc_p1[ssi % 2].ap()[16:32, :].rearrange("p (h t) -> p h t", h=16),
                        "qg")
                    g.wait_ge(qg_h, qn["qg"])
    p1_st.close()

    p2_st = contextlib.ExitStack()
    p_sbuf, p_psum = mk(p2_st)
    dt2 = [p_sbuf(f"dt2_{i}", [128, L], F32) for i in range(2)]
    z2 = [p_sbuf(f"z2_{i}", [128, L], F32) for i in range(2)]
    Bb = [p_sbuf(f"Bb{i}", [128, L], BF16) for i in range(4)]
    Cb = [p_sbuf(f"Cb{i}", [128, L], BF16) for i in range(4)]
    a_s = [p_sbuf(f"a_s{i}", [128, L], F32) for i in range(4)]
    u_s = p_sbuf("u_s", [128, L], F32)
    h_s = p_sbuf("h_s", [128, L], F32)
    hc_s = p_sbuf("hc_s", [128, L], F32)
    y_s = [p_sbuf(f"y_s{i}", [128, L], F32) for i in range(2)]
    ybf = [p_sbuf(f"ybf{i}", [128, L], BF16) for i in range(2)]

    def bc_ap(b_i, hp, cb, n):
        off = b_i * (H * 2 * DS * L) + (2 * hp) * (2 * DS * L) + cb * (DS * L) + n * L
        return bass.AP(s_bc.tensor, off, [[2 * DS * L, 2], [0, 64], [1, L]])

    def dtz_ap(tns, b_i, hp):
        off = b_i * (H * DH * L) + (2 * hp) * (DH * L)
        return bass.AP(tns.tensor, off, [[DH * L, 2], [L, 64], [1, L]])

    NIT = NITs
    NPAIR = NIT * 8
    rec_b = {}
    rec_c = {}
    rec_dtz = {}
    rec_y = {}
    rec_a = {}

    def pair_of(j):
        return j // 8, j % 8

    with nc.Block() as blk:
        @blk.sync
        def _(s):
            dma(s, dt2[0][:], dtz_ap(s_dt, 0, 0), "qs")
            dma(s, z2[0][:], dtz_ap(s_z, 0, 0), "qs")
            for n in (0, 1):
                dma(s, Bb[n % 4][:], bc_ap(0, 0, 0, n), "qs")
            rec_dtz[0] = qn["qs"]
            rec_b[0] = qn["qs"]
        @blk.gpsimd
        def _(g):
            for n in (0, 1):
                dma(g, Cb[n % 4][:], bc_ap(0, 0, 1, n), "qg")
            rec_c[0] = qn["qg"]

    for j in range(NPAIR + 1):
        it_a, pr_a = pair_of(j) if j < NPAIR else (None, None)
        with nc.Block() as blk:
            @blk.scalar
            def _(sc):
                if j < NPAIR:
                    it, pr = it_a, pr_a
                    if pr == 0:
                        sc.wait_ge(qs_h, rec_dtz[it])
                    for n in (2 * pr, 2 * pr + 1):
                        col = (it % HP) * 16 + n
                        ai = sc.activation(a_s[n % 4][:], dt2[it % 2][:], AF.Exp,
                                           scale=ascale[:, col:col + 1])
                    amark(ai)
                    rec_a[j] = ean[0]
            @blk.vector
            def _(v):
                if j >= 1:
                    jj = j - 1
                    it, pr = pair_of(jj)
                    v.wait_ge(qs_h, rec_b[jj])
                    v.wait_ge(qg_h, rec_c[jj])
                    v.wait_ge(ea_h, rec_a[jj])
                    if pr == 0 and it >= 2 and (it - 2) in rec_y:
                        v.wait_ge(qg_h, rec_y[it - 2])
                    last = None
                    for n in (2 * pr, 2 * pr + 1):
                        v.tensor_mul(u_s[:], z2[it % 2][:], Bb[n % 4][:])
                        v.drain()
                        v.tensor_tensor_scan(h_s[:], a_s[n % 4][:], u_s[:],
                                             0.0, ALU.mult, ALU.add)
                        v.drain()
                        if n == 0:
                            last = v.tensor_mul(y_s[it % 2][:], h_s[:], Cb[n % 4][:])
                        elif n == 15:
                            v.tensor_mul(hc_s[:], h_s[:], Cb[n % 4][:])
                            v.drain()
                            last = v.tensor_add(ybf[it % 2][:], y_s[it % 2][:], hc_s[:])
                        else:
                            v.tensor_mul(hc_s[:], h_s[:], Cb[n % 4][:])
                            v.drain()
                            last = v.tensor_add(y_s[it % 2][:], y_s[it % 2][:], hc_s[:])
                        v.drain()
                    last.then_inc(ev_h, 1)
                    evn[0] += 1
            @blk.sync
            def _(s):
                if j < NPAIR and pr_a == 1 and it_a + 1 < NIT:
                    nb, nh = it_a + 1, 0
                    dma(s, dt2[(it_a + 1) % 2][:],
                        dtz_ap(s_dt, (it_a + 1) // HP, (it_a + 1) % HP), "qs")
                    dma(s, z2[(it_a + 1) % 2][:],
                        dtz_ap(s_z, (it_a + 1) // HP, (it_a + 1) % HP), "qs")
                    rec_dtz[it_a + 1] = qn["qs"]
                jn = j + 1
                if jn < NPAIR:
                    s.wait_ge(ev_h, j)
                    itn, prn = pair_of(jn)
                    for n in (2 * prn, 2 * prn + 1):
                        dma(s, Bb[n % 4][:], bc_ap(itn // HP, itn % HP, 0, n), "qs")
                    rec_b[jn] = qn["qs"]
            @blk.gpsimd
            def _(g):
                if j < NPAIR and pr_a == 1 and it_a >= 1:
                    pit = it_a - 1
                    pb, ph = pit // HP, pit % HP
                    g.wait_ge(ev_h, (pit + 1) * 8)
                    dma(g, bass.AP(s_y.tensor, ph * (128 * TOK) + pb * L,
                                   [[TOK, 128], [1, L]]), ybf[pit % 2][:], "qg")
                    rec_y[pit] = qn["qg"]
                jn = j + 1
                if jn < NPAIR:
                    g.wait_ge(ev_h, j)
                    itn, prn = pair_of(jn)
                    for n in (2 * prn, 2 * prn + 1):
                        dma(g, Cb[n % 4][:], bc_ap(itn // HP, itn % HP, 1, n), "qg")
                    rec_c[jn] = qn["qg"]
    with nc.Block() as blk:
        @blk.gpsimd
        def _(g):
            pit = NIT - 1
            g.wait_ge(ev_h, NPAIR)
            dma(g, bass.AP(s_y.tensor, (pit % HP) * (128 * TOK) + (pit // HP) * L,
                           [[TOK, 128], [1, L]]), ybf[pit % 2][:], "qg")
            g.wait_ge(qg_h, qn["qg"])
    p2_st.close()

    a1_st = contextlib.ExitStack()
    p_sbuf, p_psum = mk(a1_st)
    tmpB = p_sbuf("tmpB", [128, 2048], F32)
    sqb = p_sbuf("sqb_a1", [128, 4 * D], F32)
    wout_sb = p_sbuf("wout_sb", [128, CH * D], BF16)
    ytl = [p_sbuf(f"ytl{i}", [128, CH * TT], BF16) for i in range(2)]
    xTt = [p_sbuf(f"xTt{i}", [128, NSUB * D], F32) for i in range(2)]
    aout_f = p_sbuf("aout_f", [128, CH * TT], BF16)
    aoutT = p_sbuf("aoutT", [128, NSUB * D], BF16)
    x2T = p_sbuf("x2T", [128, NSUB * D], F32)
    xcT = p_sbuf("xcT", [128, NSUB * D], BF16)
    xqT = p_sbuf("xqT", [128, NSUB * D], BF16)
    xq_f = p_sbuf("xq_f", [128, CH * TT], BF16)
    rgr = p_sbuf("rgr", [8, 128], F32)
    p_wo = [p_psum(f"p_wo{i}", [128, TT], F32) for i in range(4)]
    p3tr = p_psum("p3tr", [128, 2048], BF16)
    p3x = p_psum("p3x", [128, 128], F32)

    for k in range(CH):
        with nc.Block() as blk:
            @blk.sync
            def _(s):
                dma(s, tmpB[:, 0:1024], woutT_h[k * 128:(k + 1) * 128, :], "qs")
                s.wait_ge(qs_h, qn["qs"])
        with nc.Block() as blk:
            @blk.vector
            def _(v):
                dmark(v.tensor_copy(wout_sb[:, k * D:(k + 1) * D], tmpB[:, 0:1024]))

    rec_ed_wout = edn[0]
    rec_ytl = {}
    rec_xt = {}
    with nc.Block() as blk:
        @blk.sync
        def _(s):
            for k in range(CH):
                dma(s, ytl[0][:, k * TT:(k + 1) * TT],
                    bass.AP(s_y.tensor, k * (128 * TOK), [[TOK, 128], [1, TT]]), "qs")
            rec_ytl[0] = qn["qs"]
        @blk.gpsimd
        def _(g):
            for ssi in range(NSUB):
                dma(g, xTt[0][:, ssi * D:(ssi + 1) * D],
                    x_flat[ssi * 128:(ssi + 1) * 128, :], "qg")
            rec_xt[0] = qn["qg"]

    for g_i in range(NTs):
        ycur = ytl[g_i % 2]
        xcurT = xTt[g_i % 2]
        for rnd in range(2):
            with nc.Block() as blk:
                @blk.tensor
                def _(t):
                    if rnd == 0:
                        t.wait_ge(qs_h, rec_ytl[g_i])
                        t.wait_ge(ed_h, rec_ed_wout)
                    mm = None
                    for dmi in range(4):
                        dm = rnd * 4 + dmi
                        for k in range(CH):
                            mm = t.matmul(p_wo[dmi][:],
                                          wout_sb[:, k * D + dm * 128:k * D + (dm + 1) * 128],
                                          ycur[:, k * TT:(k + 1) * TT],
                                          start=(k == 0), stop=(k == CH - 1))
                    pmark(mm)
                @blk.sync
                def _(s):
                    if rnd == 0 and g_i + 1 < NT:
                        for k in range(CH):
                            dma(s, ytl[(g_i + 1) % 2][:, k * TT:(k + 1) * TT],
                                bass.AP(s_y.tensor, k * (128 * TOK) + (g_i + 1) * TT,
                                        [[TOK, 128], [1, TT]]), "qs")
                        rec_ytl[g_i + 1] = qn["qs"]
                @blk.gpsimd
                def _(g):
                    if rnd == 0 and g_i + 1 < NT:
                        for ssi in range(NSUB):
                            t0 = (g_i + 1) * TT + ssi * 128
                            dma(g, xTt[(g_i + 1) % 2][:, ssi * D:(ssi + 1) * D],
                                x_flat[t0:t0 + 128, :], "qg")
                        rec_xt[g_i + 1] = qn["qg"]
            with nc.Block() as blk:
                @blk.scalar
                def _(sc):
                    sc.wait_ge(ep_h, epn[0])
                    last = None
                    for dmi in range(4):
                        dm = rnd * 4 + dmi
                        last = sc.activation(aout_f[:, dm * TT:(dm + 1) * TT],
                                             p_wo[dmi][:], AF.Identity,
                                             bias=bout_c[:, dm:dm + 1])
                    amark(last)
        for rnd in range(2):
            subs = (0, 1) if rnd == 0 else (2, 3)
            with nc.Block() as blk:
                @blk.tensor
                def _(t):
                    t.wait_ge(ea_h, ean[0])
                    tr = None
                    for j, ssi in enumerate(subs):
                        for dm in range(CH):
                            col = (j * 8 + dm) * 128
                            tr = t.transpose(p3tr[:, col:col + 128],
                                             aout_f[:, dm * TT + ssi * 128:dm * TT + (ssi + 1) * 128],
                                             idm[:])
                    pmark(tr)
            with nc.Block() as blk:
                @blk.scalar
                def _(sc):
                    sc.wait_ge(ep_h, epn[0])
                    last = None
                    for j, ssi in enumerate(subs):
                        last = sc.activation(aoutT[:, ssi * D:(ssi + 1) * D],
                                             p3tr[:, j * D:(j + 1) * D], AF.Copy)
                    amark(last)
        with nc.Block() as blk:
            @blk.scalar
            def _(sc):
                amark(sc.activation(sqb[:], aoutT[:], AF.Square))
            @blk.vector
            def _(v):
                v.reduce_sum(stat[:, 0:4],
                             aoutT.ap().rearrange("p (s d) -> p s d", s=NSUB), axis=AX.X)
        with nc.Block() as blk:
            @blk.vector
            def _(v):
                v.wait_ge(ea_h, ean[0])
                v.reduce_sum(stat[:, 4:8],
                             sqb.ap().rearrange("p (s d) -> p s d", s=NSUB), axis=AX.X)
        with nc.Block() as blk:
            @blk.vector
            def _(v):
                v.tensor_scalar_mul(stat[:, 0:4], stat[:, 0:4], 1.0 / D)
                v.tensor_scalar_mul(stat[:, 4:8], stat[:, 4:8], 1.0 / D)
                v.drain()
                v.tensor_mul(stat[:, 8:12], stat[:, 0:4], stat[:, 0:4])
                v.drain()
                dmark(v.tensor_sub(stat[:, 8:12], stat[:, 4:8], stat[:, 8:12]))
        with nc.Block() as blk:
            @blk.scalar
            def _(sc):
                sc.wait_ge(ed_h, edn[0])
                lnsrc = bass.AP(stat.ap().tensor, 8, [[40, 128], [0, 128], [1, 4]])
                sc.activation(junk.ap().rearrange("p (r c) -> p r c", c=4), lnsrc,
                              AF.Ln, bias=cc[:, 0:1])
                sc.drain()
                amark(sc.activation(rstdw[:], junk[:], AF.Exp, scale=cc[:, 1:2]))
        with nc.Block() as blk:
            @blk.vector
            def _(v):
                v.wait_ge(qg_h, rec_xt[g_i])
                v.wait_ge(ea_h, ean[0])
                for ssi in range(NSUB):
                    v.tensor_scalar(aoutT[:, ssi * D:(ssi + 1) * D],
                                    aoutT[:, ssi * D:(ssi + 1) * D],
                                    stat[:, ssi:ssi + 1], rstdw[:, ssi:ssi + 1],
                                    ALU.subtract, ALU.mult)
                v.drain()
                dmark(v.tensor_add(x2T[:], xcurT[:], aoutT[:]))
        with nc.Block() as blk:
            @blk.gpsimd
            def _(g):
                g.wait_ge(ed_h, edn[0])
                for ssi in range(NSUB):
                    dma(g, s_x2[g_i * NSUB + ssi, :, :], x2T[:, ssi * D:(ssi + 1) * D],
                        "qg")
                g.wait_ge(qg_h, qn["qg"])
            @blk.scalar
            def _(sc):
                sc.wait_ge(ed_h, edn[0])
                amark(sc.activation(sqb[:], x2T[:], AF.Square))
            @blk.vector
            def _(v):
                v.reduce_sum(stat[:, 0:4],
                             x2T.ap().rearrange("p (s d) -> p s d", s=NSUB), axis=AX.X)
        with nc.Block() as blk:
            @blk.vector
            def _(v):
                v.wait_ge(ea_h, ean[0])
                v.reduce_sum(stat[:, 4:8],
                             sqb.ap().rearrange("p (s d) -> p s d", s=NSUB), axis=AX.X)
        with nc.Block() as blk:
            @blk.vector
            def _(v):
                v.tensor_scalar_mul(stat[:, 0:4], stat[:, 0:4], 1.0 / D)
                v.tensor_scalar_mul(stat[:, 4:8], stat[:, 4:8], 1.0 / D)
                v.drain()
                v.tensor_mul(stat[:, 8:12], stat[:, 0:4], stat[:, 0:4])
                v.drain()
                dmark(v.tensor_sub(stat[:, 8:12], stat[:, 4:8], stat[:, 8:12]))
        with nc.Block() as blk:
            @blk.scalar
            def _(sc):
                sc.wait_ge(ed_h, edn[0])
                lnsrc = bass.AP(stat.ap().tensor, 8, [[40, 128], [0, 128], [1, 4]])
                sc.activation(junk.ap().rearrange("p (r c) -> p r c", c=4), lnsrc,
                              AF.Ln, bias=cc[:, 0:1])
                sc.drain()
                amark(sc.activation(rstdw[:], junk[:], AF.Exp, scale=cc[:, 1:2]))
            @blk.vector
            def _(v):
                for ssi in range(NSUB):
                    v.tensor_single_scalar(xcT[:, ssi * D:(ssi + 1) * D],
                                           x2T[:, ssi * D:(ssi + 1) * D],
                                           stat[:, ssi:ssi + 1], ALU.subtract)
        with nc.Block() as blk:
            @blk.vector
            def _(v):
                v.reduce_max(stat[:, 16:20],
                             xcT.ap().rearrange("p (s d) -> p s d", s=NSUB),
                             axis=AX.X, apply_absolute_value=True)
        with nc.Block() as blk:
            @blk.vector
            def _(v):
                v.tensor_scalar_max(stat[:, 16:20], stat[:, 16:20], EPS)
        with nc.Block() as blk:
            @blk.vector
            def _(v):
                v.reciprocal(stat[:, 20:24], stat[:, 16:20])
        with nc.Block() as blk:
            @blk.vector
            def _(v):
                v.wait_ge(ea_h, ean[0])
                v.tensor_scalar_mul(stat[:, 20:24], stat[:, 20:24], 127.0)
                v.tensor_scalar_mul(stat[:, 16:20], stat[:, 16:20], 1.0 / 127.0)
                v.drain()
                v.wait_ge(ea_h, ean[0])
                v.tensor_mul(stat[:, 16:20], stat[:, 16:20], rstdw[:, 0:4])
                v.drain()
                v.tensor_scalar_mul(stat[:, 24:28], stat[:, 16:20], wq_cols[:, 4:5])
                v.tensor_scalar_mul(stat[:, 28:32], stat[:, 16:20], wq_cols[:, 5:6])
                for ssi in range(NSUB):
                    v.tensor_scalar_mul(sqb[:, ssi * D:(ssi + 1) * D],
                                        xcT[:, ssi * D:(ssi + 1) * D],
                                        stat[:, 20 + ssi:21 + ssi])
                v.drain()
                for hh in range(2):
                    v.tensor_copy(tmpI[:], sqb[:, hh * 2048:(hh + 1) * 2048])
                    v.drain()
                    li2 = v.tensor_copy(xqT[:, hh * 2048:(hh + 1) * 2048], tmpI[:])
                    if hh == 0:
                        v.drain()
                dmark(li2)
        with nc.Block() as blk:
            @blk.tensor
            def _(t):
                t.wait_ge(ed_h, edn[0])
                pmark(t.transpose(p3x[0:8, :], stat[:, 24:32], idm_f[:]))
        with nc.Block() as blk:
            @blk.vector
            def _(v):
                v.wait_ge(ep_h, epn[0])
                dmark(v.tensor_copy(rgr[0:8, :], p3x[0:8, :]))
        with nc.Block() as blk:
            @blk.gpsimd
            def _(g):
                g.wait_ge(ed_h, edn[0])
                dma(g, s_rg[g_i, :, :], rgr[0:8, :], "qg")
                g.wait_ge(qg_h, qn["qg"])
        for rnd in range(2):
            cs = range(4) if rnd == 0 else range(4, 8)
            with nc.Block() as blk:
                @blk.tensor
                def _(t):
                    t.wait_ge(ed_h, edn[0])
                    tr = None
                    for jc, c in enumerate(cs):
                        for ssi in range(NSUB):
                            col = (jc * 4 + ssi) * 128
                            tr = t.transpose(p3tr[:, col:col + 128],
                                             xqT[:, ssi * D + c * 128:ssi * D + (c + 1) * 128],
                                             idm[:])
                    pmark(tr)
            with nc.Block() as blk:
                @blk.scalar
                def _(sc):
                    sc.wait_ge(ep_h, epn[0])
                    last = None
                    for jc, c in enumerate(cs):
                        last = sc.activation(xq_f[:, c * TT:(c + 1) * TT],
                                             p3tr[:, jc * TT:(jc + 1) * TT], AF.Copy)
                    amark(last)
        with nc.Block() as blk:
            @blk.gpsimd
            def _(g):
                g.wait_ge(ea_h, ean[0])
                for c in range(CH):
                    dma(g, bass.AP(s_xq.tensor, c * (128 * TOK) + g_i * TT,
                                   [[TOK, 128], [1, TT]]),
                        xq_f[:, c * TT:(c + 1) * TT], "qg")
                g.wait_ge(qg_h, qn["qg"])
    a1_st.close()

    a2_st = contextlib.ExitStack()
    p_sbuf, p_psum = mk(a2_st)
    HTOK = TOK // 2
    NTH = 2 if SIM_SMALL else NT // 2
    xq_all = p_sbuf("xq_all", [128, CH * HTOK], BF16)
    rgb_all = p_sbuf("rgb_all", [128, HTOK], BF16)
    rub_all = p_sbuf("rub_all", [128, HTOK], BF16)
    tmpA2 = p_sbuf("tmpA2", [128, 1024], F32)
    wslc = [[p_sbuf(f"wslc{w}_{i}", [128, 1024], BF16) for i in range(2)]
            for w in range(2)]
    gat = [p_sbuf(f"gat{i}", [128, 1024], BF16) for i in range(2)]
    gat2 = [p_sbuf(f"gat2_{i}", [128, 1024], BF16) for i in range(2)]
    upv = [p_sbuf(f"upv{i}", [128, 1024], BF16) for i in range(4)]
    guv = [p_sbuf(f"guv{i}", [128, 1024], BF16) for i in range(2)]
    absgu = [p_sbuf(f"absgu{i}", [128, 1024], BF16) for i in range(2)]
    amtmp = [p_sbuf(f"amtmp{i}", [1, 1024], F32) for i in range(2)]
    amtmp2 = [p_sbuf(f"amtmp2_{i}", [1, 1024], F32) for i in range(2)]
    p_g = [p_psum(f"p_g{i}", [128, 1024], F32) for i in range(2)]
    p_u = [p_psum(f"p_u{i}", [128, 1024], F32) for i in range(2)]

    def wslc_ap(tns, m):
        return bass.AP(tns.tensor, m * 128,
                       [[INNER, 128], [128 * INNER, 8], [1, 128]])

    rec_w = {}
    rec_si = {}
    rec_pe = {}
    rec_ga = {}
    rec_gv = {}
    rec_am = {}
    rec_ab = {}
    for half in range(1 if SIM_SMALL else 2):
        tb = half * NTH
        with nc.Block() as blk:
            @blk.sync
            def _(s):
                for c in range(CH):
                    for gg in range(NTH):
                        dma(s, xq_all[:, c * HTOK + gg * TT:c * HTOK + (gg + 1) * TT],
                            bass.AP(s_xq.tensor, c * (128 * TOK) + (tb + gg) * TT,
                                    [[TOK, 128], [1, TT]]), "qs")
                s.wait_ge(qs_h, qn["qs"])
        for gg in range(NTH):
            with nc.Block() as blk:
                @blk.sync
                def _(s):
                    for ssi in range(NSUB):
                        dma(s, tmpA2[:, ssi * 128:(ssi + 1) * 128],
                            bass.AP(s_rg.tensor, ((tb + gg) * 8 + ssi) * 128,
                                    [[0, 128], [1, 128]]), "qs")
                        dma(s, tmpB2_ap(tmpA2, ssi),
                            bass.AP(s_rg.tensor, ((tb + gg) * 8 + 4 + ssi) * 128,
                                    [[0, 128], [1, 128]]), "qs")
                    s.wait_ge(qs_h, qn["qs"])
            with nc.Block() as blk:
                @blk.vector
                def _(v):
                    v.tensor_copy(rgb_all[:, gg * TT:(gg + 1) * TT], tmpA2[:, 0:512])
                    v.tensor_copy(rub_all[:, gg * TT:(gg + 1) * TT], tmpA2[:, 512:1024])
        MCHs = 2 if SIM_SMALL else MCH
        NP = MCHs * (NTH // 2)
        def unit(J):
            return J // (NTH // 2), (J % (NTH // 2)) * 2
        with nc.Block() as blk:
            @blk.sync
            def _(s):
                dma(s, wslc[0][0][:], wslc_ap(s_wgq, 0), "qs")
                dma(s, wslc[1][0][:], wslc_ap(s_wuq, 0), "qs")
                rec_w[(half, 0)] = qn["qs"]
        for J in range(NP + 7):
            with nc.Block() as blk:
                @blk.sync
                def _(s):
                    if J < NP and J % (NTH // 2) == 0:
                        m = unit(J)[0]
                        if m + 1 < MCH:
                            dma(s, wslc[0][(m + 1) % 2][:], wslc_ap(s_wgq, m + 1), "qs")
                            dma(s, wslc[1][(m + 1) % 2][:], wslc_ap(s_wuq, m + 1), "qs")
                            rec_w[(half, m + 1)] = qn["qs"]
                @blk.tensor
                def _(t):
                    if J < NP:
                        m, gg = unit(J)
                        t.wait_ge(qs_h, rec_w[(half, m)])
                        mm = None
                        for pgp, wsl in ((p_g, wslc[0]), (p_u, wslc[1])):
                            for tt2 in range(2):
                                for k in range(CH):
                                    mm = t.matmul(pgp[J % 2][:, tt2 * TT:(tt2 + 1) * TT],
                                                  wsl[m % 2][:, k * 128:(k + 1) * 128],
                                                  xq_all[:, k * HTOK + (gg + tt2) * TT:
                                                         k * HTOK + (gg + tt2 + 1) * TT],
                                                  start=(k == 0), stop=(k == CH - 1))
                        pmark(mm)
                        rec_pe[J] = epn[0]
                @blk.vector
                def _(v):
                    li = None
                    if 1 <= J < NP + 1:
                        Jp = J - 1
                        v.wait_ge(ep_h, rec_pe[Jp])
                        li = v.tensor_mul(gat[Jp % 2][:], p_g[Jp % 2][:],
                                          rgb_all[:, rgslice(Jp, NTH)])
                        v.tensor_mul(upv[Jp % 4][:], p_u[Jp % 2][:],
                                     rub_all[:, rgslice(Jp, NTH)])
                        rec_ga[Jp] = edn[0] + 1
                    if 3 <= J < NP + 3:
                        Jp = J - 3
                        v.wait_ge(ea_h, rec_si[Jp])
                        li = v.tensor_mul(guv[Jp % 2][:], gat2[Jp % 2][:], upv[Jp % 4][:])
                        rec_gv[Jp] = edn[0] + 1
                    if li is not None:
                        dmark(li)
                @blk.scalar
                def _(sc):
                    if 2 <= J < NP + 2:
                        Jp = J - 2
                        sc.wait_ge(ed_h, rec_ga[Jp])
                        amark(sc.activation(gat2[Jp % 2][:], gat[Jp % 2][:], AF.Silu))
                        rec_si[Jp] = ean[0]
                    if 4 <= J < NP + 4:
                        Jp = J - 4
                        sc.wait_ge(ed_h, rec_gv[Jp])
                        amark(sc.activation(absgu[Jp % 2][:], guv[Jp % 2][:], AF.Abs))
                        rec_ab[Jp] = ean[0]
                @blk.gpsimd
                def _(g):
                    if 6 <= J < NP + 6:
                        Jp = J - 6
                        m, gg = unit(Jp)
                        tok0 = (half * NTH + gg) * TT
                        g.wait_ge(eo_h, rec_am[Jp])
                        dma(g, bass.AP(s_am.tensor, m * TOK + tok0,
                                       [[1, 1], [1, 1024]]), amtmp[Jp % 2][:], "qg")
                    if 4 <= J < NP + 4:
                        Jp = J - 4
                        m, gg = unit(Jp)
                        g.wait_ge(ed_h, rec_gv[Jp])
                        dma(g, bass.AP(s_gu.tensor,
                                       m * (128 * TOK) + (half * NTH + gg) * TT,
                                       [[TOK, 128], [1, 2 * TT]]), guv[Jp % 2][:], "qg")
                        g.wait_ge(qg_h, qn["qg"])
                    if 5 <= J < NP + 5:
                        Jp = J - 5
                        g.wait_ge(ea_h, rec_ab[Jp])
                        omark(g.tensor_reduce(amtmp[Jp % 2][:], absgu[Jp % 2][:],
                                              axis=AX.C, op=ALU.max))
                        rec_am[Jp] = eon[0]
    a2_st.close()

    b_st = contextlib.ExitStack()
    p_sbuf, p_psum = mk(b_st)
    wd_sb = p_sbuf("wd_sb", [128, MCH * D], BF16)
    gu_all = p_sbuf("gu_all", [128, MCH * TT], BF16)
    guq_all = p_sbuf("guq_all", [128, MCH * TT], BF16)
    am_ld = p_sbuf("am_ld", [32, TT], F32)
    rowmx = p_sbuf("rowmx", [1, TT], F32)
    rowmn = p_sbuf("rowmn", [1, TT], F32)
    rowsc = p_sbuf("rowsc", [1, TT], BF16)
    rowrd = p_sbuf("rowrd", [1, TT], BF16)
    scb = p_sbuf("scb", [128, TT], F32)
    rdb = p_sbuf("rdb", [128, TT], F32)
    ffn_f = p_sbuf("ffn_f", [128, CH * TT], BF16)
    ffnT = p_sbuf("ffnT", [128, NSUB * D], BF16)
    x2l = p_sbuf("x2l", [128, NSUB * D], F32)
    outT = p_sbuf("outT", [128, NSUB * D], F32)
    p_dn = [p_psum(f"p_dn{i}", [128, TT], F32) for i in range(4)]
    p3trb = p_psum("p3trb", [128, 2048], BF16)
    p_b2 = p_psum("p_b2", [128, 2 * TT], F32)

    out_flat = out_h.rearrange("b l d -> (b l) d")

    with nc.Block() as blk:
        @blk.sync
        def _(s):
            for k in range(MCH):
                dma(s, wd_sb[:, k * D:(k + 1) * D],
                    bass.AP(s_wdq.tensor, k * 128 * D, [[D, 128], [1, D]]), "qs")
            s.wait_ge(qs_h, qn["qs"])

    for g_i in range(NTs):
        with nc.Block() as blk:
            @blk.sync
            def _(s):
                for k in range(MCH):
                    dma(s, gu_all[:, k * TT:(k + 1) * TT],
                        bass.AP(s_gu.tensor, k * (128 * TOK) + g_i * TT,
                                [[TOK, 128], [1, TT]]), "qs")
                s.wait_ge(qs_h, qn["qs"])
            @blk.gpsimd
            def _(g):
                for ssi in range(NSUB):
                    dma(g, x2l[:, ssi * D:(ssi + 1) * D],
                        s_x2[g_i * NSUB + ssi, :, :], "qg")
                dma(g, am_ld[:], bass.AP(s_am.tensor, g_i * TT,
                                         [[TOK, 32], [1, TT]]), "qg")
                g.wait_ge(qg_h, qn["qg"])
        with nc.Block() as blk:
            @blk.gpsimd
            def _(g):
                omark(g.tensor_reduce(rowmx[:], am_ld[0:32, :], axis=AX.C, op=ALU.max))
        with nc.Block() as blk:
            @blk.vector
            def _(v):
                v.wait_ge(eo_h, eon[0])
                v.tensor_scalar_max(rowmx[:], rowmx[:], EPS)
                v.drain()
        with nc.Block() as blk:
            @blk.vector
            def _(v):
                v.reciprocal(rowmn[:], rowmx[:])
        with nc.Block() as blk:
            @blk.vector
            def _(v):
                v.tensor_scalar_mul(rowmn[:], rowmn[:], 127.0)
                v.drain()
                v.tensor_copy(rowsc[:], rowmn[:])
                v.tensor_scalar_mul(rowmx[:], rowmx[:], 1.0 / 127.0)
                v.drain()
                v.tensor_scalar_mul(rowmx[:], rowmx[:], wq_cols[0:1, 6:7])
                v.drain()
                dmark(v.tensor_copy(rowrd[:], rowmx[:]))
        with nc.Block() as blk:
            @blk.tensor
            def _(t):
                t.wait_ge(ed_h, edn[0])
                t.matmul(p_b2[:, 0:TT], onesr[:], rowsc[:], start=True, stop=True)
                pmark(t.matmul(p_b2[:, TT:2 * TT], onesr[:], rowrd[:],
                               start=True, stop=True))
        with nc.Block() as blk:
            @blk.vector
            def _(v):
                v.wait_ge(ep_h, epn[0])
                v.tensor_copy(scb[:], p_b2[:, 0:TT])
                dmark(v.tensor_copy(rdb[:], p_b2[:, TT:2 * TT]))
        for q in range(18):
            with nc.Block() as blk:
                @blk.gpsimd
                def _(g):
                    if q < 16:
                        g.wait_ge(ed_h, edn[0])
                        last = None
                        for kk in range(2):
                            k = q * 2 + kk
                            last = g.tensor_tensor(outT[:, (q % 2) * 2048 + kk * TT:
                                                        (q % 2) * 2048 + (kk + 1) * TT],
                                                   gu_all[:, k * TT:(k + 1) * TT],
                                                   scb[:], ALU.mult)
                        omark(last)
                @blk.vector
                def _(v):
                    if 1 <= q < 17:
                        v.wait_ge(eo_h, eon[0] - (1 if q < 16 else 0))
                        for kk in range(2):
                            v.tensor_copy(tmpI[:, ((q - 1) % 2) * 1024 + kk * TT:
                                               ((q - 1) % 2) * 1024 + (kk + 1) * TT],
                                          outT[:, ((q - 1) % 2) * 2048 + kk * TT:
                                               ((q - 1) % 2) * 2048 + (kk + 1) * TT])
                    if q >= 2:
                        for kk in range(2):
                            k = (q - 2) * 2 + kk
                            v.tensor_copy(guq_all[:, k * TT:(k + 1) * TT],
                                          tmpI[:, ((q - 2) % 2) * 1024 + kk * TT:
                                               ((q - 2) % 2) * 1024 + (kk + 1) * TT])
                        dmarkd(v)
        for rnd in range(2):
            with nc.Block() as blk:
                @blk.tensor
                def _(t):
                    t.wait_ge(ed_h, edn[0])
                    mm = None
                    for dmi in range(4):
                        dm = rnd * 4 + dmi
                        for k in range(MCH):
                            mm = t.matmul(p_dn[dmi][:],
                                          wd_sb[:, k * D + dm * 128:k * D + (dm + 1) * 128],
                                          guq_all[:, k * TT:(k + 1) * TT],
                                          start=(k == 0), stop=(k == MCH - 1))
                    pmark(mm)
            with nc.Block() as blk:
                @blk.vector
                def _(v):
                    v.wait_ge(ep_h, epn[0])
                    for dmi in range(4):
                        dm = rnd * 4 + dmi
                        v.tensor_mul(ffn_f[:, dm * TT:(dm + 1) * TT],
                                     p_dn[dmi][:], rdb[:])
                    dmarkd(v)
        for rnd in range(2):
            subs = (0, 1) if rnd == 0 else (2, 3)
            with nc.Block() as blk:
                @blk.tensor
                def _(t):
                    t.wait_ge(ed_h, edn[0])
                    tr = None
                    for j, ssi in enumerate(subs):
                        for dm in range(CH):
                            col = (j * 8 + dm) * 128
                            tr = t.transpose(p3trb[:, col:col + 128],
                                             ffn_f[:, dm * TT + ssi * 128:dm * TT + (ssi + 1) * 128],
                                             idm[:])
                    pmark(tr)
            with nc.Block() as blk:
                @blk.scalar
                def _(sc):
                    sc.wait_ge(ep_h, epn[0])
                    last = None
                    for j, ssi in enumerate(subs):
                        last = sc.activation(ffnT[:, ssi * D:(ssi + 1) * D],
                                             p3trb[:, j * D:(j + 1) * D], AF.Copy)
                    amark(last)
        with nc.Block() as blk:
            @blk.scalar
            def _(sc):
                amark(sc.activation(gu_all.ap()[:, 0:4 * D], ffnT[:], AF.Square))
            @blk.vector
            def _(v):
                v.wait_ge(ea_h, ean[0] - 1)
                v.reduce_sum(stat[:, 0:4],
                             ffnT.ap().rearrange("p (s d) -> p s d", s=NSUB),
                             axis=AX.X)
        with nc.Block() as blk:
            @blk.vector
            def _(v):
                v.wait_ge(ea_h, ean[0])
                v.reduce_sum(stat[:, 4:8],
                             gu_all.ap()[:, 0:4 * D].rearrange("p (s d) -> p s d", s=NSUB),
                             axis=AX.X)
        with nc.Block() as blk:
            @blk.vector
            def _(v):
                v.tensor_scalar_mul(stat[:, 0:4], stat[:, 0:4], 1.0 / D)
                v.tensor_scalar_mul(stat[:, 4:8], stat[:, 4:8], 1.0 / D)
                v.drain()
                v.tensor_mul(stat[:, 8:12], stat[:, 0:4], stat[:, 0:4])
                v.drain()
                dmark(v.tensor_sub(stat[:, 8:12], stat[:, 4:8], stat[:, 8:12]))
        with nc.Block() as blk:
            @blk.scalar
            def _(sc):
                sc.wait_ge(ed_h, edn[0])
                lnsrc = bass.AP(stat.ap().tensor, 8, [[40, 128], [0, 128], [1, 4]])
                sc.activation(junk.ap().rearrange("p (r c) -> p r c", c=4), lnsrc,
                              AF.Ln, bias=cc[:, 0:1])
                sc.drain()
                amark(sc.activation(rstdw[:], junk[:], AF.Exp, scale=cc[:, 1:2]))
        with nc.Block() as blk:
            @blk.vector
            def _(v):
                v.wait_ge(ea_h, ean[0])
                for ssi in range(NSUB):
                    v.tensor_scalar(ffnT[:, ssi * D:(ssi + 1) * D],
                                    ffnT[:, ssi * D:(ssi + 1) * D],
                                    stat[:, ssi:ssi + 1], rstdw[:, ssi:ssi + 1],
                                    ALU.subtract, ALU.mult)
                v.drain()
                dmark(v.tensor_add(outT[:], x2l[:], ffnT[:]))
        with nc.Block() as blk:
            @blk.gpsimd
            def _(g):
                g.wait_ge(ed_h, edn[0])
                for ssi in range(NSUB):
                    t0 = g_i * TT + ssi * 128
                    dma(g, out_flat[t0:t0 + 128, :], outT[:, ssi * D:(ssi + 1) * D],
                        "qg")
                if dbg:
                    dma(g, s_dbg2[g_i, :, :], ffn_f[:], "qg")
                    dma(g, s_dbg3[g_i, :, :], ffnT[:], "qg")
                g.wait_ge(qg_h, qn["qg"])

    with nc.Block() as blk:
        @blk.gpsimd
        def _(g):
            g.wait_ge(qg_h, qn["qg"])
            g.wait_ge(qs_h, qn["qs"])
        @blk.vector
        def _(v):
            v.wait_ge(qg_h, qn["qg"])
    b_st.close()

    g_st.close()
    return nc


def tmpB2_ap(tmpA2, ssi):
    return tmpA2[:, 512 + ssi * 128:512 + (ssi + 1) * 128]


def rgslice(Jp, NTH):
    gg = (Jp % (NTH // 2)) * 2
    return slice(gg * TT, (gg + 2) * TT)


def kernel(**inputs):
    x = np.ascontiguousarray(inputs["L0_attn_x"])
    A_log = np.asarray(inputs["L0_attn_A_log"])
    Wx = np.ascontiguousarray(inputs["L0_attn_Wx"]).astype(np.float32, copy=False)
    Wdt = np.asarray(inputs["L0_attn_Wdt"])
    bdt = np.asarray(inputs["L0_attn_bdt"])
    Wout = np.asarray(inputs["L0_attn_Wout"])
    bout = np.asarray(inputs["L0_attn_bout"])
    wg = np.asarray(inputs["L0_ffn_w_gate"])
    wu = np.asarray(inputs["L0_ffn_w_up"])
    wd = np.asarray(inputs["L0_ffn_w_down"])

    acols = np.empty((128, 128), dtype=np.float32)
    for hp in range(HP):
        for n in range(16):
            acols[0:64, hp * 16 + n] = A_log[2 * hp, n]
            acols[64:128, hp * 16 + n] = A_log[2 * hp + 1, n]
    in_common = {
        "acols": acols,
        "bdt": np.ascontiguousarray(bdt.reshape(64, 1), dtype=np.float32),
        "wx": Wx,
        "wdtT": np.ascontiguousarray(Wdt.T),
        "wxbcT": np.ascontiguousarray(Wx[64:96, :].T),
        "woutT": np.ascontiguousarray(Wout.T),
        "boutc": np.ascontiguousarray(bout.reshape(CH, 128).T),
        "wgT": np.ascontiguousarray(wg.T),
        "wuT": np.ascontiguousarray(wu.T),
        "wdT": np.ascontiguousarray(wd.T),
        "onesr": np.ones((1, 128), dtype=ml_dtypes.bfloat16),
        "onesc": np.ones((128, 1), dtype=ml_dtypes.bfloat16),
        "idm": np.eye(128, dtype=ml_dtypes.bfloat16),
    }
    nc = build_nc()
    in_maps = [dict(in_common, x=np.ascontiguousarray(x[c * B:(c + 1) * B]))
               for c in range(NCORES)]
    res = run_bass_kernel_spmd(nc, in_maps, core_ids=list(range(NCORES)))
    return np.concatenate([r["out"] for r in res.results], axis=0)


# version 05
# speedup vs baseline: 1.9756x; 1.9756x over previous
import contextlib
import numpy as np
import ml_dtypes
import concourse.bass as bass
import concourse.mybir as mybir
from concourse.bass_utils import run_bass_kernel_spmd

F32 = mybir.dt.float32
BF16 = mybir.dt.bfloat16
I32 = mybir.dt.int32
AF = mybir.ActivationFunctionType
ALU = mybir.AluOpType
AX = mybir.AxisListType

NCORES = 8
B = 4
L = 2048
D = 1024
H = 16
HP = 8
DH = 64
DS = 16
INNER = 4096
CH = D // 128
MCH = INNER // 128
TT = 512
NT = (B * L) // TT
NSUB = TT // 128
EPS = 1e-5
TOK = B * L


SIM_SMALL = False


def build_nc(dbg=False):
    NTs = 2 if SIM_SMALL else NT
    NITs = 2 if SIM_SMALL else B * HP
    nc = bass.Bass("TRN2", target_bir_lowering=False)

    x_h = nc.dram_tensor("x", [B, L, D], F32, kind="ExternalInput").ap()
    acols_h = nc.dram_tensor("acols", [128, 128], F32, kind="ExternalInput").ap()
    bdt_h = nc.dram_tensor("bdt", [64, 1], F32, kind="ExternalInput").ap()
    wx_h = nc.dram_tensor("wx", [96, 64], F32, kind="ExternalInput").ap()
    wdtT_h = nc.dram_tensor("wdtT", [64, 64], F32, kind="ExternalInput").ap()
    wxbcT_h = nc.dram_tensor("wxbcT", [64, 32], F32, kind="ExternalInput").ap()
    woutT_h = nc.dram_tensor("woutT", [D, D], F32, kind="ExternalInput").ap()
    bout_h = nc.dram_tensor("boutc", [128, CH], F32, kind="ExternalInput").ap()
    wgT_h = nc.dram_tensor("wgT", [D, INNER], F32, kind="ExternalInput").ap()
    wuT_h = nc.dram_tensor("wuT", [D, INNER], F32, kind="ExternalInput").ap()
    wdT_h = nc.dram_tensor("wdT", [INNER, D], F32, kind="ExternalInput").ap()
    onesr_h = nc.dram_tensor("onesr", [1, 128], BF16, kind="ExternalInput").ap()
    onesc_h = nc.dram_tensor("onesc", [128, 1], BF16, kind="ExternalInput").ap()
    idm_h = nc.dram_tensor("idm", [128, 128], BF16, kind="ExternalInput").ap()
    out_h = nc.dram_tensor("out", [B, L, D], F32, kind="ExternalOutput").ap()

    KD = "ExternalOutput" if dbg else "Internal"
    s_dt = nc.dram_tensor("s_dt", [B, H, DH, L], F32, kind=KD).ap()
    s_z = nc.dram_tensor("s_z", [B, H, DH, L], F32, kind=KD).ap()
    s_bc = nc.dram_tensor("s_bc", [B, H, 2, DS, L], BF16, kind=KD).ap()
    s_y = nc.dram_tensor("s_y", [CH, 128, TOK], BF16, kind=KD).ap()
    s_x2 = nc.dram_tensor("s_x2", [TOK // 128, 128, D], F32, kind=KD).ap()
    s_xq = nc.dram_tensor("s_xq", [CH, 128, TOK], BF16, kind=KD).ap()
    s_gu = nc.dram_tensor("s_gu", [MCH, 128, TOK], BF16, kind=KD).ap()
    s_wgq = nc.dram_tensor("s_wgq", [D, INNER], BF16, kind=KD).ap()
    s_wuq = nc.dram_tensor("s_wuq", [D, INNER], BF16, kind=KD).ap()
    s_wdq = nc.dram_tensor("s_wdq", [INNER, D], BF16, kind=KD).ap()
    s_stage = nc.dram_tensor("s_stage", [1, 16], F32, kind=KD).ap()
    s_rg = nc.dram_tensor("s_rg", [NT, 8, 128], F32, kind=KD).ap()
    s_am = nc.dram_tensor("s_am", [1, TOK], F32, kind=KD).ap()
    s_dbg = nc.dram_tensor("s_dbg", [NT, 128, 40], F32, kind=KD).ap()
    s_dbg2 = nc.dram_tensor("s_dbg2", [NT, 128, 4096], BF16, kind=KD).ap()
    s_dbg3 = nc.dram_tensor("s_dbg3", [NT, 128, 4096], BF16, kind=KD).ap()

    g_st = contextlib.ExitStack()

    def mk(stk):
        return (lambda n, s, d: stk.enter_context(nc.sbuf_tensor(n, s, d)),
                lambda n, s, d: stk.enter_context(nc.psum_tensor(n, s, d)))

    sbuf, psum = mk(g_st)

    ascale = sbuf("ascale", [128, 128], F32)
    bdt_c = sbuf("bdt_c", [64, 1], F32)
    lhsT_f = sbuf("lhsT_f", [64, 96], F32)
    lhsT_b = sbuf("lhsT_b", [64, 96], BF16)
    bout_c = sbuf("bout_c", [128, CH], F32)
    onesr = sbuf("onesr_s", [1, 128], BF16)
    onesc = sbuf("onesc_s", [128, 1], BF16)
    idm = sbuf("idm_s", [128, 128], BF16)
    idm_f = sbuf("idmf_s", [128, 128], F32)
    wq_cols = sbuf("wq_cols", [128, 8], F32)
    stat = sbuf("stat", [128, 40], F32)
    tmpI = sbuf("tmpI", [128, 2 * D], I32)
    cc = sbuf("cc", [128, 4], F32)
    junk = sbuf("junk", [128, 512], F32)
    rstdw = sbuf("rstdw", [128, 512], F32)

    qs_h = g_st.enter_context(nc.semaphore("qs"))
    qg_h = g_st.enter_context(nc.semaphore("qg"))
    ev_h = g_st.enter_context(nc.semaphore("ev"))
    ea_h = g_st.enter_context(nc.semaphore("ea"))
    eo_h = g_st.enter_context(nc.semaphore("eo"))
    ep_h = g_st.enter_context(nc.semaphore("ep"))
    ed_h = g_st.enter_context(nc.semaphore("ed"))
    qn = {"qs": 0, "qg": 0}
    evn = [0]
    ean = [0]
    eon = [0]
    epn = [0]
    edn = [0]

    def dmark(inst):
        inst.then_inc(ed_h, 1)
        edn[0] += 1

    def dmarkd(eng):
        eng.drain().then_inc(ed_h, 1)
        edn[0] += 1

    def pmark(inst):
        inst.then_inc(ep_h, 1)
        epn[0] += 1

    def amark(inst):
        inst.then_inc(ea_h, 1)
        ean[0] += 1

    def omark(inst):
        inst.then_inc(eo_h, 1)
        eon[0] += 1

    def dma(eng, dst, src, q):
        h = qs_h if q == "qs" else qg_h
        eng.wait_ge(h, qn[q])
        eng.dma_start(out=dst, in_=src).then_inc(h, 16)
        qn[q] += 16

    su_st = contextlib.ExitStack()
    s_sbuf, s_psum = mk(su_st)
    tmpA = s_sbuf("tmpA", [128, 2048], F32)
    tmpC = s_sbuf("tmpC", [128, 2048], BF16)
    tmpC2 = s_sbuf("tmpC2", [128, 2048], BF16)
    wld = [s_sbuf(f"wld{i}", [128, 2048], F32) for i in range(2)]
    wabs = s_sbuf("wabs", [128, 2048], F32)
    onesc32 = s_sbuf("onesc32", [128, 1], F32)
    p_small = s_psum("p_small", [96, 512], F32)
    p_row = s_psum("p_row", [1, 512], F32)

    with nc.Block(no_gpsimd_drain=True) as blk:
        @blk.sync
        def _(s):
            dma(s, ascale[:], acols_h[:], "qs")
            dma(s, bdt_c[:], bdt_h[:], "qs")
            dma(s, tmpA[0:96, 0:64], wx_h[:], "qs")
            dma(s, tmpA[0:64, 64:128], wdtT_h[:], "qs")
            dma(s, lhsT_f[:, 64:96], wxbcT_h[:], "qs")
            dma(s, bout_c[:], bout_h[:], "qs")
            dma(s, onesr[:], onesr_h[:], "qs")
            dma(s, onesc[:], onesc_h[:], "qs")
            dma(s, idm[:], idm_h[:], "qs")
            s.wait_ge(qs_h, qn["qs"])
        @blk.gpsimd
        def _(g):
            g.memset(cc[:, 0:1], EPS)
            g.memset(cc[:, 1:2], -0.5)
            g.memset(cc[:, 2:3], -1.0)
            g.memset(junk[:], 0.0)
            g.memset(stat[:], 0.0)
            omark(g.memset(rstdw[:], 0.0))
    with nc.Block(no_gpsimd_drain=True) as blk:
        @blk.scalar
        def _(sc):
            sc.wait_ge(eo_h, eon[0])
            sc.activation(tmpA[0:128, 128:256], ascale[:], AF.Exp)
            amark(sc.activation(junk[:], junk[:], AF.Copy))
        @blk.tensor
        def _(t):
            pmark(t.matmul(p_small[0:64, 0:64], tmpA[0:64, 0:64], tmpA[0:64, 64:128],
                           start=True, stop=True))
    with nc.Block(no_gpsimd_drain=True) as blk:
        @blk.vector
        def _(v):
            v.wait_ge(ea_h, ean[0])
            v.wait_ge(ep_h, epn[0])
            v.tensor_scalar_mul(ascale[:], tmpA[0:128, 128:256], -1.0)
            v.tensor_copy(lhsT_f[:, 0:64], p_small[0:64, 0:64])
    with nc.Block(no_gpsimd_drain=True) as blk:
        @blk.vector
        def _(v):
            v.tensor_copy(lhsT_b[:], lhsT_f[:])
            v.tensor_copy(onesc32[:], onesc[:])
            dmark(v.tensor_copy(idm_f[:], idm[:]))

    w_list = [(wgT_h, s_wgq, D, INNER, 0), (wuT_h, s_wuq, D, INNER, 1),
              (wdT_h, s_wdq, INNER, D, 2)]
    for (src, _d, R, Cc, idx) in w_list:
        CW = min(2048, Cc)
        cpr = Cc // CW
        ntile = (R // 128) * cpr
        if SIM_SMALL:
            ntile = 2
        njr = CW // 512
        for it in range(ntile + 1):
            with nc.Block(no_gpsimd_drain=True) as blk:
                @blk.sync
                def _(s):
                    if it < ntile:
                        r, c = it // cpr, it % cpr
                        dma(s, wld[it % 2][:, 0:CW],
                            src[r * 128:(r + 1) * 128, c * CW:(c + 1) * CW], "qs")
                    s.wait_ge(qs_h, qn["qs"])
                @blk.scalar
                def _(sc):
                    if it >= 1:
                        amark(sc.activation(wabs[:, 0:CW], wld[(it - 1) % 2][:, 0:CW],
                                            AF.Abs))
            if it >= 1:
                with nc.Block(no_gpsimd_drain=True) as blk:
                    @blk.tensor
                    def _(t):
                        t.wait_ge(ea_h, ean[0])
                        mm = None
                        for j in range(njr):
                            mm = t.matmul(p_row[0:1, 0:512], onesc32[:],
                                          wabs[:, j * 512:(j + 1) * 512],
                                          start=(it == 1 and j == 0),
                                          stop=(it == ntile and j == njr - 1))
                        if it == ntile:
                            pmark(mm)
        with nc.Block(no_gpsimd_drain=True) as blk:
            @blk.vector
            def _(v):
                v.wait_ge(ep_h, epn[0])
                v.reduce_sum(tmpA[0:1, 0:1], p_row[0:1, 0:512], axis=AX.X)
        with nc.Block(no_gpsimd_drain=True) as blk:
            @blk.vector
            def _(v):
                v.tensor_scalar(tmpA[0:1, 1:2], tmpA[0:1, 0:1],
                                1.0 / (R * Cc), EPS, ALU.mult, ALU.max)
        with nc.Block(no_gpsimd_drain=True) as blk:
            @blk.vector
            def _(v):
                dmark(v.reciprocal(tmpA[0:1, 2:3], tmpA[0:1, 1:2]))
        with nc.Block(no_gpsimd_drain=True) as blk:
            @blk.gpsimd
            def _(g):
                g.wait_ge(ed_h, edn[0])
                dma(g, s_stage[0:1, idx * 2:idx * 2 + 2], tmpA[0:1, 1:3], "qg")
                g.wait_ge(qg_h, qn["qg"])
    with nc.Block(no_gpsimd_drain=True) as blk:
        @blk.sync
        def _(s):
            for idx in range(3):
                dma(s, wq_cols[:, 4 + idx:5 + idx],
                    bass.AP(s_stage.tensor, idx * 2, [[0, 128], [1, 1]]), "qs")
                dma(s, wq_cols[:, idx:idx + 1],
                    bass.AP(s_stage.tensor, idx * 2 + 1, [[0, 128], [1, 1]]), "qs")
            s.wait_ge(qs_h, qn["qs"])

    for (src, dst, R, Cc, idx) in w_list:
        CW = min(2048, Cc)
        cpr = Cc // CW
        ntile = (R // 128) * cpr
        if SIM_SMALL:
            ntile = 2
        for it in range(ntile + 2):
            with nc.Block(no_gpsimd_drain=True) as blk:
                @blk.sync
                def _(s):
                    if it < ntile:
                        r, c = it // cpr, it % cpr
                        dma(s, wld[it % 2][:, 0:CW],
                            src[r * 128:(r + 1) * 128, c * CW:(c + 1) * CW], "qs")
                    s.wait_ge(qs_h, qn["qs"])
                @blk.vector
                def _(v):
                    if 1 <= it <= ntile:
                        j = it - 1
                        qb = tmpC if j % 2 == 0 else tmpC2
                        v.tensor_scalar_mul(tmpA[:, 0:CW], wld[j % 2][:, 0:CW],
                                            wq_cols[:, idx:idx + 1])
                        v.drain()
                        v.tensor_copy(tmpI[:, 0:CW], tmpA[:, 0:CW])
                        v.drain()
                        v.tensor_scalar(tmpI[:, 0:CW], tmpI[:, 0:CW], 1, -1,
                                        ALU.min, ALU.max)
                        v.drain()
                        dmark(v.tensor_copy(qb[:, 0:CW], tmpI[:, 0:CW]))
                @blk.gpsimd
                def _(g):
                    if it >= 2:
                        g.wait_ge(ed_h, edn[0])
                        j = it - 2
                        r, c = j // cpr, j % cpr
                        qb = tmpC if j % 2 == 0 else tmpC2
                        dma(g, dst[r * 128:(r + 1) * 128, c * CW:(c + 1) * CW],
                            qb[:, 0:CW], "qg")
                        g.wait_ge(qg_h, qn["qg"])
    su_st.close()

    p1_st = contextlib.ExitStack()
    p_sbuf, p_psum = mk(p1_st)
    xt = [p_sbuf(f"xt{i}", [128, NSUB * D], F32) for i in range(2)]
    sqb = p_sbuf("sqb", [128, 4 * D], F32)
    xn1T = p_sbuf("xn1T", [128, NSUB * D], BF16)
    xf = p_sbuf("xf", [64, NSUB * 2048], BF16)
    dt_p1 = [p_sbuf(f"dt_p1{i}", [64, 2048], F32) for i in range(2)]
    z_p1 = [p_sbuf(f"z_p1{i}", [64, 2048], F32) for i in range(2)]
    bc_p1 = [p_sbuf(f"bc_p1{i}", [32, 2048], BF16) for i in range(2)]
    sfp = [p_sbuf(f"sfp{i}", [64, 2048], F32) for i in range(3)]
    p_trA = p_psum("p_trA", [128, 2048], BF16)
    p_trB = p_psum("p_trB", [128, 2048], BF16)
    p_pj = p_psum("p_pj", [96, 2048], F32)

    x_flat = x_h.rearrange("b l d -> (b l) d")

    for g_i in range(NTs):
        with nc.Block(no_gpsimd_drain=True) as blk:
            @blk.sync
            def _(s):
                for ssi in range(NSUB):
                    t0 = g_i * TT + ssi * 128
                    dma(s, xt[g_i % 2][:, ssi * D:(ssi + 1) * D],
                        x_flat[t0:t0 + 128, :], "qs")
                s.wait_ge(qs_h, qn["qs"])
        xcur = xt[g_i % 2]
        b_i = g_i // 4
        tbase = (g_i % 4) * TT
        with nc.Block(no_gpsimd_drain=True) as blk:
            @blk.scalar
            def _(sc):
                amark(sc.activation(sqb[:], xcur[:], AF.Square))
            @blk.vector
            def _(v):
                v.reduce_sum(stat[:, 0:4],
                             xcur.ap().rearrange("p (s d) -> p s d", s=NSUB),
                             axis=AX.X)
        with nc.Block(no_gpsimd_drain=True) as blk:
            @blk.vector
            def _(v):
                v.wait_ge(ea_h, ean[0])
                v.reduce_sum(stat[:, 4:8],
                             sqb.ap().rearrange("p (s d) -> p s d", s=NSUB),
                             axis=AX.X)
        with nc.Block(no_gpsimd_drain=True) as blk:
            @blk.vector
            def _(v):
                v.tensor_scalar_mul(stat[:, 0:4], stat[:, 0:4], 1.0 / D)
                v.tensor_scalar_mul(stat[:, 4:8], stat[:, 4:8], 1.0 / D)
                v.drain()
                v.tensor_mul(stat[:, 8:12], stat[:, 0:4], stat[:, 0:4])
                v.drain()
                dmark(v.tensor_sub(stat[:, 8:12], stat[:, 4:8], stat[:, 8:12]))
        with nc.Block(no_gpsimd_drain=True) as blk:
            @blk.scalar
            def _(sc):
                sc.wait_ge(ed_h, edn[0])
                lnsrc = bass.AP(stat.ap().tensor, 8, [[40, 128], [0, 128], [1, 4]])
                sc.activation(junk.ap().rearrange("p (r c) -> p r c", c=4), lnsrc,
                              AF.Ln, bias=cc[:, 0:1])
                sc.drain()
                amark(sc.activation(rstdw[:], junk[:], AF.Exp, scale=cc[:, 1:2]))
        with nc.Block(no_gpsimd_drain=True) as blk:
            @blk.vector
            def _(v):
                v.wait_ge(ea_h, ean[0])
                li = None
                for ssi in range(NSUB):
                    li = v.tensor_scalar(xn1T[:, ssi * D:(ssi + 1) * D],
                                         xcur[:, ssi * D:(ssi + 1) * D],
                                         stat[:, ssi:ssi + 1], rstdw[:, ssi:ssi + 1],
                                         ALU.subtract, ALU.mult)
                dmark(li)
            @blk.gpsimd
            def _(g):
                dma(g, s_dbg[g_i, :, 0:16], stat[:, 0:16], "qg")
                g.wait_ge(qg_h, qn["qg"])
        for half in range(2):
            subs = (0, 1) if half == 0 else (2, 3)
            with nc.Block(no_gpsimd_drain=True) as blk:
                @blk.tensor
                def _(t):
                    t.wait_ge(ed_h, edn[0])
                    tr = None
                    for j, ssi in enumerate(subs):
                        for c in range(CH):
                            pt = p_trA if c < 4 else p_trB
                            col = (j * 4 + (c % 4)) * 128
                            tr = t.transpose(pt[:, col:col + 128],
                                             xn1T[:, ssi * D + c * 128:ssi * D + (c + 1) * 128],
                                             idm[:])
                    pmark(tr)
            with nc.Block(no_gpsimd_drain=True) as blk:
                @blk.scalar
                def _(sc):
                    sc.wait_ge(ep_h, epn[0])
                    for j, ssi in enumerate(subs):
                        for lo, pt in ((0, p_trA), (4, p_trB)):
                            src0 = pt.ap()[0:64, j * 512:(j + 1) * 512]\
                                .rearrange("p (c t) -> p c t", c=4)
                            src1 = pt.ap()[64:128, j * 512:(j + 1) * 512]\
                                .rearrange("p (c t) -> p c t", c=4)
                            dsts = xf.ap()[0:64, ssi * 2048:(ssi + 1) * 2048]\
                                .rearrange("p (h t) -> p h t", h=16)
                            sc.activation(dsts[:, 2 * lo:2 * lo + 7:2, :], src0, AF.Copy)
                            amark(sc.activation(dsts[:, 2 * lo + 1:2 * lo + 8:2, :],
                                                src1, AF.Copy))
        for ssi in range(NSUB):
            with nc.Block(no_gpsimd_drain=True) as blk:
                @blk.tensor
                def _(t):
                    t.wait_ge(ea_h, ean[0])
                    mm = None
                    for h_i in range(16):
                        mm = t.matmul(p_pj[:, h_i * 128:(h_i + 1) * 128], lhsT_b[:],
                                      xf[0:64, ssi * 2048 + h_i * 128:ssi * 2048 + (h_i + 1) * 128],
                                      start=True, stop=True)
                    pmark(mm)
            with nc.Block(no_gpsimd_drain=True) as blk:
                @blk.scalar
                def _(sc):
                    sc.wait_ge(ep_h, epn[0])
                    sc.activation(sfp[0][:], p_pj[0:64, :], AF.Abs, bias=bdt_c[:])
                @blk.vector
                def _(v):
                    v.wait_ge(ep_h, epn[0])
                    v.tensor_scalar(sfp[1][:], p_pj[0:64, :], bdt_c[:], 0.0,
                                    ALU.add, ALU.max)
                    dmark(v.tensor_copy(bc_p1[ssi % 2][:], p_pj[64:96, :]))
            with nc.Block(no_gpsimd_drain=True) as blk:
                @blk.scalar
                def _(sc):
                    sc.activation(sfp[2][:], sfp[0][:], AF.Exp, scale=cc[0:64, 2:3])
                    sc.drain()
                    amark(sc.activation(sfp[0][:], sfp[2][:], AF.Ln, bias=1.0))
            with nc.Block(no_gpsimd_drain=True) as blk:
                @blk.vector
                def _(v):
                    v.wait_ge(ea_h, ean[0])
                    v.tensor_add(dt_p1[ssi % 2][:], sfp[0][:], sfp[1][:])
                    v.drain()
                    dmark(v.tensor_mul(z_p1[ssi % 2][:], dt_p1[ssi % 2][:],
                                       xf[0:64, ssi * 2048:(ssi + 1) * 2048]))
            with nc.Block(no_gpsimd_drain=True) as blk:
                @blk.gpsimd
                def _(g):
                    g.wait_ge(ed_h, edn[0])
                    t0 = tbase + ssi * 128
                    dma(g, bass.AP(s_dt.tensor, b_i * (H * DH * L) + t0,
                                   [[L, 64], [DH * L, 16], [1, 128]]),
                        dt_p1[ssi % 2].ap().rearrange("p (h t) -> p h t", h=16), "qg")
                    dma(g, bass.AP(s_z.tensor, b_i * (H * DH * L) + t0,
                                   [[L, 64], [DH * L, 16], [1, 128]]),
                        z_p1[ssi % 2].ap().rearrange("p (h t) -> p h t", h=16), "qg")
                    dma(g, bass.AP(s_bc.tensor, b_i * (H * 2 * DS * L) + t0,
                                   [[L, 16], [2 * DS * L, 16], [1, 128]]),
                        bc_p1[ssi % 2].ap()[0:16, :].rearrange("p (h t) -> p h t", h=16),
                        "qg")
                    dma(g, bass.AP(s_bc.tensor, b_i * (H * 2 * DS * L) + DS * L + t0,
                                   [[L, 16], [2 * DS * L, 16], [1, 128]]),
                        bc_p1[ssi % 2].ap()[16:32, :].rearrange("p (h t) -> p h t", h=16),
                        "qg")
                    g.wait_ge(qg_h, qn["qg"])
    p1_st.close()

    p2_st = contextlib.ExitStack()
    p_sbuf, p_psum = mk(p2_st)
    dt2 = [p_sbuf(f"dt2_{i}", [128, L], F32) for i in range(2)]
    z2 = [p_sbuf(f"z2_{i}", [128, L], F32) for i in range(2)]
    Bb = [p_sbuf(f"Bb{i}", [128, L], BF16) for i in range(4)]
    Cb = [p_sbuf(f"Cb{i}", [128, L], BF16) for i in range(4)]
    a_s = [p_sbuf(f"a_s{i}", [128, L], F32) for i in range(4)]
    u_s = p_sbuf("u_s", [128, L], F32)
    h_s = p_sbuf("h_s", [128, L], F32)
    hc_s = p_sbuf("hc_s", [128, L], F32)
    y_s = [p_sbuf(f"y_s{i}", [128, L], F32) for i in range(2)]
    ybf = [p_sbuf(f"ybf{i}", [128, L], BF16) for i in range(2)]

    def bc_ap(b_i, hp, cb, n):
        off = b_i * (H * 2 * DS * L) + (2 * hp) * (2 * DS * L) + cb * (DS * L) + n * L
        return bass.AP(s_bc.tensor, off, [[2 * DS * L, 2], [0, 64], [1, L]])

    def dtz_ap(tns, b_i, hp):
        off = b_i * (H * DH * L) + (2 * hp) * (DH * L)
        return bass.AP(tns.tensor, off, [[DH * L, 2], [L, 64], [1, L]])

    NIT = NITs
    NPAIR = NIT * 8
    rec_b = {}
    rec_c = {}
    rec_dtz = {}
    rec_y = {}
    rec_a = {}

    def pair_of(j):
        return j // 8, j % 8

    with nc.Block(no_gpsimd_drain=True) as blk:
        @blk.sync
        def _(s):
            dma(s, dt2[0][:], dtz_ap(s_dt, 0, 0), "qs")
            dma(s, z2[0][:], dtz_ap(s_z, 0, 0), "qs")
            for n in (0, 1):
                dma(s, Bb[n % 4][:], bc_ap(0, 0, 0, n), "qs")
            rec_dtz[0] = qn["qs"]
            rec_b[0] = qn["qs"]
        @blk.gpsimd
        def _(g):
            for n in (0, 1):
                dma(g, Cb[n % 4][:], bc_ap(0, 0, 1, n), "qg")
            rec_c[0] = qn["qg"]

    for j in range(NPAIR + 1):
        it_a, pr_a = pair_of(j) if j < NPAIR else (None, None)
        with nc.Block(no_gpsimd_drain=True) as blk:
            @blk.scalar
            def _(sc):
                if j < NPAIR:
                    it, pr = it_a, pr_a
                    if pr == 0:
                        sc.wait_ge(qs_h, rec_dtz[it])
                    for n in (2 * pr, 2 * pr + 1):
                        col = (it % HP) * 16 + n
                        ai = sc.activation(a_s[n % 4][:], dt2[it % 2][:], AF.Exp,
                                           scale=ascale[:, col:col + 1])
                    amark(ai)
                    rec_a[j] = ean[0]
            @blk.vector
            def _(v):
                if j >= 1:
                    jj = j - 1
                    it, pr = pair_of(jj)
                    v.wait_ge(qs_h, rec_b[jj])
                    v.wait_ge(qg_h, rec_c[jj])
                    v.wait_ge(ea_h, rec_a[jj])
                    if pr == 0 and it >= 2 and (it - 2) in rec_y:
                        v.wait_ge(qg_h, rec_y[it - 2])
                    last = None
                    for n in (2 * pr, 2 * pr + 1):
                        v.tensor_mul(u_s[:], z2[it % 2][:], Bb[n % 4][:])
                        v.drain()
                        v.tensor_tensor_scan(h_s[:], a_s[n % 4][:], u_s[:],
                                             0.0, ALU.mult, ALU.add)
                        v.drain()
                        if n == 0:
                            last = v.tensor_mul(y_s[it % 2][:], h_s[:], Cb[n % 4][:])
                        elif n == 15:
                            v.tensor_mul(hc_s[:], h_s[:], Cb[n % 4][:])
                            v.drain()
                            last = v.tensor_add(ybf[it % 2][:], y_s[it % 2][:], hc_s[:])
                        else:
                            v.tensor_mul(hc_s[:], h_s[:], Cb[n % 4][:])
                            v.drain()
                            last = v.tensor_add(y_s[it % 2][:], y_s[it % 2][:], hc_s[:])
                        v.drain()
                    last.then_inc(ev_h, 1)
                    evn[0] += 1
            @blk.sync
            def _(s):
                if j < NPAIR and pr_a == 1 and it_a + 1 < NIT:
                    nb, nh = it_a + 1, 0
                    dma(s, dt2[(it_a + 1) % 2][:],
                        dtz_ap(s_dt, (it_a + 1) // HP, (it_a + 1) % HP), "qs")
                    dma(s, z2[(it_a + 1) % 2][:],
                        dtz_ap(s_z, (it_a + 1) // HP, (it_a + 1) % HP), "qs")
                    rec_dtz[it_a + 1] = qn["qs"]
                jn = j + 1
                if jn < NPAIR:
                    s.wait_ge(ev_h, j)
                    itn, prn = pair_of(jn)
                    for n in (2 * prn, 2 * prn + 1):
                        dma(s, Bb[n % 4][:], bc_ap(itn // HP, itn % HP, 0, n), "qs")
                    rec_b[jn] = qn["qs"]
            @blk.gpsimd
            def _(g):
                if j < NPAIR and pr_a == 1 and it_a >= 1:
                    pit = it_a - 1
                    pb, ph = pit // HP, pit % HP
                    g.wait_ge(ev_h, (pit + 1) * 8)
                    dma(g, bass.AP(s_y.tensor, ph * (128 * TOK) + pb * L,
                                   [[TOK, 128], [1, L]]), ybf[pit % 2][:], "qg")
                    rec_y[pit] = qn["qg"]
                jn = j + 1
                if jn < NPAIR:
                    g.wait_ge(ev_h, j)
                    itn, prn = pair_of(jn)
                    for n in (2 * prn, 2 * prn + 1):
                        dma(g, Cb[n % 4][:], bc_ap(itn // HP, itn % HP, 1, n), "qg")
                    rec_c[jn] = qn["qg"]
    with nc.Block(no_gpsimd_drain=True) as blk:
        @blk.gpsimd
        def _(g):
            pit = NIT - 1
            g.wait_ge(ev_h, NPAIR)
            dma(g, bass.AP(s_y.tensor, (pit % HP) * (128 * TOK) + (pit // HP) * L,
                           [[TOK, 128], [1, L]]), ybf[pit % 2][:], "qg")
            g.wait_ge(qg_h, qn["qg"])
    p2_st.close()

    a1_st = contextlib.ExitStack()
    p_sbuf, p_psum = mk(a1_st)
    tmpB = p_sbuf("tmpB", [128, 2048], F32)
    sqb = p_sbuf("sqb_a1", [128, 4 * D], F32)
    wout_sb = p_sbuf("wout_sb", [128, CH * D], BF16)
    ytl = [p_sbuf(f"ytl{i}", [128, CH * TT], BF16) for i in range(2)]
    xTt = [p_sbuf(f"xTt{i}", [128, NSUB * D], F32) for i in range(2)]
    aout_f = p_sbuf("aout_f", [128, CH * TT], BF16)
    aoutT = p_sbuf("aoutT", [128, NSUB * D], BF16)
    x2T = p_sbuf("x2T", [128, NSUB * D], F32)
    xcT = p_sbuf("xcT", [128, NSUB * D], BF16)
    xqT = p_sbuf("xqT", [128, NSUB * D], BF16)
    xq_f = p_sbuf("xq_f", [128, CH * TT], BF16)
    rgr = p_sbuf("rgr", [8, 128], F32)
    p_wo = [p_psum(f"p_wo{i}", [128, TT], F32) for i in range(4)]
    p3tr = p_psum("p3tr", [128, 2048], BF16)
    p3x = p_psum("p3x", [128, 128], F32)

    for k in range(CH):
        with nc.Block(no_gpsimd_drain=True) as blk:
            @blk.sync
            def _(s):
                dma(s, tmpB[:, 0:1024], woutT_h[k * 128:(k + 1) * 128, :], "qs")
                s.wait_ge(qs_h, qn["qs"])
        with nc.Block(no_gpsimd_drain=True) as blk:
            @blk.vector
            def _(v):
                dmark(v.tensor_copy(wout_sb[:, k * D:(k + 1) * D], tmpB[:, 0:1024]))

    rec_ed_wout = edn[0]
    rec_ytl = {}
    rec_xt = {}
    with nc.Block(no_gpsimd_drain=True) as blk:
        @blk.sync
        def _(s):
            for k in range(CH):
                dma(s, ytl[0][:, k * TT:(k + 1) * TT],
                    bass.AP(s_y.tensor, k * (128 * TOK), [[TOK, 128], [1, TT]]), "qs")
            rec_ytl[0] = qn["qs"]
        @blk.gpsimd
        def _(g):
            for ssi in range(NSUB):
                dma(g, xTt[0][:, ssi * D:(ssi + 1) * D],
                    x_flat[ssi * 128:(ssi + 1) * 128, :], "qg")
            rec_xt[0] = qn["qg"]

    for g_i in range(NTs):
        ycur = ytl[g_i % 2]
        xcurT = xTt[g_i % 2]
        for rnd in range(2):
            with nc.Block(no_gpsimd_drain=True) as blk:
                @blk.tensor
                def _(t):
                    if rnd == 0:
                        t.wait_ge(qs_h, rec_ytl[g_i])
                        t.wait_ge(ed_h, rec_ed_wout)
                    mm = None
                    for dmi in range(4):
                        dm = rnd * 4 + dmi
                        for k in range(CH):
                            mm = t.matmul(p_wo[dmi][:],
                                          wout_sb[:, k * D + dm * 128:k * D + (dm + 1) * 128],
                                          ycur[:, k * TT:(k + 1) * TT],
                                          start=(k == 0), stop=(k == CH - 1))
                    pmark(mm)
                @blk.sync
                def _(s):
                    if rnd == 0 and g_i + 1 < NT:
                        for k in range(CH):
                            dma(s, ytl[(g_i + 1) % 2][:, k * TT:(k + 1) * TT],
                                bass.AP(s_y.tensor, k * (128 * TOK) + (g_i + 1) * TT,
                                        [[TOK, 128], [1, TT]]), "qs")
                        rec_ytl[g_i + 1] = qn["qs"]
                @blk.gpsimd
                def _(g):
                    if rnd == 0 and g_i + 1 < NT:
                        for ssi in range(NSUB):
                            t0 = (g_i + 1) * TT + ssi * 128
                            dma(g, xTt[(g_i + 1) % 2][:, ssi * D:(ssi + 1) * D],
                                x_flat[t0:t0 + 128, :], "qg")
                        rec_xt[g_i + 1] = qn["qg"]
            with nc.Block(no_gpsimd_drain=True) as blk:
                @blk.scalar
                def _(sc):
                    sc.wait_ge(ep_h, epn[0])
                    last = None
                    for dmi in range(4):
                        dm = rnd * 4 + dmi
                        last = sc.activation(aout_f[:, dm * TT:(dm + 1) * TT],
                                             p_wo[dmi][:], AF.Identity,
                                             bias=bout_c[:, dm:dm + 1])
                    amark(last)
        for rnd in range(2):
            subs = (0, 1) if rnd == 0 else (2, 3)
            with nc.Block(no_gpsimd_drain=True) as blk:
                @blk.tensor
                def _(t):
                    t.wait_ge(ea_h, ean[0])
                    tr = None
                    for j, ssi in enumerate(subs):
                        for dm in range(CH):
                            col = (j * 8 + dm) * 128
                            tr = t.transpose(p3tr[:, col:col + 128],
                                             aout_f[:, dm * TT + ssi * 128:dm * TT + (ssi + 1) * 128],
                                             idm[:])
                    pmark(tr)
            with nc.Block(no_gpsimd_drain=True) as blk:
                @blk.scalar
                def _(sc):
                    sc.wait_ge(ep_h, epn[0])
                    last = None
                    for j, ssi in enumerate(subs):
                        last = sc.activation(aoutT[:, ssi * D:(ssi + 1) * D],
                                             p3tr[:, j * D:(j + 1) * D], AF.Copy)
                    amark(last)
        with nc.Block(no_gpsimd_drain=True) as blk:
            @blk.scalar
            def _(sc):
                amark(sc.activation(sqb[:], aoutT[:], AF.Square))
            @blk.vector
            def _(v):
                v.reduce_sum(stat[:, 0:4],
                             aoutT.ap().rearrange("p (s d) -> p s d", s=NSUB), axis=AX.X)
        with nc.Block(no_gpsimd_drain=True) as blk:
            @blk.vector
            def _(v):
                v.wait_ge(ea_h, ean[0])
                v.reduce_sum(stat[:, 4:8],
                             sqb.ap().rearrange("p (s d) -> p s d", s=NSUB), axis=AX.X)
        with nc.Block(no_gpsimd_drain=True) as blk:
            @blk.vector
            def _(v):
                v.tensor_scalar_mul(stat[:, 0:4], stat[:, 0:4], 1.0 / D)
                v.tensor_scalar_mul(stat[:, 4:8], stat[:, 4:8], 1.0 / D)
                v.drain()
                v.tensor_mul(stat[:, 8:12], stat[:, 0:4], stat[:, 0:4])
                v.drain()
                dmark(v.tensor_sub(stat[:, 8:12], stat[:, 4:8], stat[:, 8:12]))
        with nc.Block(no_gpsimd_drain=True) as blk:
            @blk.scalar
            def _(sc):
                sc.wait_ge(ed_h, edn[0])
                lnsrc = bass.AP(stat.ap().tensor, 8, [[40, 128], [0, 128], [1, 4]])
                sc.activation(junk.ap().rearrange("p (r c) -> p r c", c=4), lnsrc,
                              AF.Ln, bias=cc[:, 0:1])
                sc.drain()
                amark(sc.activation(rstdw[:], junk[:], AF.Exp, scale=cc[:, 1:2]))
        with nc.Block(no_gpsimd_drain=True) as blk:
            @blk.vector
            def _(v):
                v.wait_ge(qg_h, rec_xt[g_i])
                v.wait_ge(ea_h, ean[0])
                for ssi in range(NSUB):
                    v.tensor_scalar(aoutT[:, ssi * D:(ssi + 1) * D],
                                    aoutT[:, ssi * D:(ssi + 1) * D],
                                    stat[:, ssi:ssi + 1], rstdw[:, ssi:ssi + 1],
                                    ALU.subtract, ALU.mult)
                v.drain()
                dmark(v.tensor_add(x2T[:], xcurT[:], aoutT[:]))
        with nc.Block(no_gpsimd_drain=True) as blk:
            @blk.gpsimd
            def _(g):
                g.wait_ge(ed_h, edn[0])
                for ssi in range(NSUB):
                    dma(g, s_x2[g_i * NSUB + ssi, :, :], x2T[:, ssi * D:(ssi + 1) * D],
                        "qg")
                g.wait_ge(qg_h, qn["qg"])
            @blk.scalar
            def _(sc):
                sc.wait_ge(ed_h, edn[0])
                amark(sc.activation(sqb[:], x2T[:], AF.Square))
            @blk.vector
            def _(v):
                v.reduce_sum(stat[:, 0:4],
                             x2T.ap().rearrange("p (s d) -> p s d", s=NSUB), axis=AX.X)
        with nc.Block(no_gpsimd_drain=True) as blk:
            @blk.vector
            def _(v):
                v.wait_ge(ea_h, ean[0])
                v.reduce_sum(stat[:, 4:8],
                             sqb.ap().rearrange("p (s d) -> p s d", s=NSUB), axis=AX.X)
        with nc.Block(no_gpsimd_drain=True) as blk:
            @blk.vector
            def _(v):
                v.tensor_scalar_mul(stat[:, 0:4], stat[:, 0:4], 1.0 / D)
                v.tensor_scalar_mul(stat[:, 4:8], stat[:, 4:8], 1.0 / D)
                v.drain()
                v.tensor_mul(stat[:, 8:12], stat[:, 0:4], stat[:, 0:4])
                v.drain()
                dmark(v.tensor_sub(stat[:, 8:12], stat[:, 4:8], stat[:, 8:12]))
        with nc.Block(no_gpsimd_drain=True) as blk:
            @blk.scalar
            def _(sc):
                sc.wait_ge(ed_h, edn[0])
                lnsrc = bass.AP(stat.ap().tensor, 8, [[40, 128], [0, 128], [1, 4]])
                sc.activation(junk.ap().rearrange("p (r c) -> p r c", c=4), lnsrc,
                              AF.Ln, bias=cc[:, 0:1])
                sc.drain()
                amark(sc.activation(rstdw[:], junk[:], AF.Exp, scale=cc[:, 1:2]))
            @blk.vector
            def _(v):
                for ssi in range(NSUB):
                    v.tensor_single_scalar(xcT[:, ssi * D:(ssi + 1) * D],
                                           x2T[:, ssi * D:(ssi + 1) * D],
                                           stat[:, ssi:ssi + 1], ALU.subtract)
        with nc.Block(no_gpsimd_drain=True) as blk:
            @blk.vector
            def _(v):
                v.reduce_max(stat[:, 16:20],
                             xcT.ap().rearrange("p (s d) -> p s d", s=NSUB),
                             axis=AX.X, apply_absolute_value=True)
        with nc.Block(no_gpsimd_drain=True) as blk:
            @blk.vector
            def _(v):
                v.tensor_scalar_max(stat[:, 16:20], stat[:, 16:20], EPS)
        with nc.Block(no_gpsimd_drain=True) as blk:
            @blk.vector
            def _(v):
                v.reciprocal(stat[:, 20:24], stat[:, 16:20])
        with nc.Block(no_gpsimd_drain=True) as blk:
            @blk.vector
            def _(v):
                v.wait_ge(ea_h, ean[0])
                v.tensor_scalar_mul(stat[:, 20:24], stat[:, 20:24], 127.0)
                v.tensor_scalar_mul(stat[:, 16:20], stat[:, 16:20], 1.0 / 127.0)
                v.drain()
                v.wait_ge(ea_h, ean[0])
                v.tensor_mul(stat[:, 16:20], stat[:, 16:20], rstdw[:, 0:4])
                v.drain()
                v.tensor_scalar_mul(stat[:, 24:28], stat[:, 16:20], wq_cols[:, 4:5])
                v.tensor_scalar_mul(stat[:, 28:32], stat[:, 16:20], wq_cols[:, 5:6])
                for ssi in range(NSUB):
                    v.tensor_scalar_mul(sqb[:, ssi * D:(ssi + 1) * D],
                                        xcT[:, ssi * D:(ssi + 1) * D],
                                        stat[:, 20 + ssi:21 + ssi])
                v.drain()
                for hh in range(2):
                    v.tensor_copy(tmpI[:], sqb[:, hh * 2048:(hh + 1) * 2048])
                    v.drain()
                    li2 = v.tensor_copy(xqT[:, hh * 2048:(hh + 1) * 2048], tmpI[:])
                    if hh == 0:
                        v.drain()
                dmark(li2)
        with nc.Block(no_gpsimd_drain=True) as blk:
            @blk.tensor
            def _(t):
                t.wait_ge(ed_h, edn[0])
                pmark(t.transpose(p3x[0:8, :], stat[:, 24:32], idm_f[:]))
        with nc.Block(no_gpsimd_drain=True) as blk:
            @blk.vector
            def _(v):
                v.wait_ge(ep_h, epn[0])
                dmark(v.tensor_copy(rgr[0:8, :], p3x[0:8, :]))
        with nc.Block(no_gpsimd_drain=True) as blk:
            @blk.gpsimd
            def _(g):
                g.wait_ge(ed_h, edn[0])
                dma(g, s_rg[g_i, :, :], rgr[0:8, :], "qg")
                g.wait_ge(qg_h, qn["qg"])
        for rnd in range(2):
            cs = range(4) if rnd == 0 else range(4, 8)
            with nc.Block(no_gpsimd_drain=True) as blk:
                @blk.tensor
                def _(t):
                    t.wait_ge(ed_h, edn[0])
                    tr = None
                    for jc, c in enumerate(cs):
                        for ssi in range(NSUB):
                            col = (jc * 4 + ssi) * 128
                            tr = t.transpose(p3tr[:, col:col + 128],
                                             xqT[:, ssi * D + c * 128:ssi * D + (c + 1) * 128],
                                             idm[:])
                    pmark(tr)
            with nc.Block(no_gpsimd_drain=True) as blk:
                @blk.scalar
                def _(sc):
                    sc.wait_ge(ep_h, epn[0])
                    last = None
                    for jc, c in enumerate(cs):
                        last = sc.activation(xq_f[:, c * TT:(c + 1) * TT],
                                             p3tr[:, jc * TT:(jc + 1) * TT], AF.Copy)
                    amark(last)
        with nc.Block(no_gpsimd_drain=True) as blk:
            @blk.gpsimd
            def _(g):
                g.wait_ge(ea_h, ean[0])
                for c in range(CH):
                    dma(g, bass.AP(s_xq.tensor, c * (128 * TOK) + g_i * TT,
                                   [[TOK, 128], [1, TT]]),
                        xq_f[:, c * TT:(c + 1) * TT], "qg")
                g.wait_ge(qg_h, qn["qg"])
    a1_st.close()

    a2_st = contextlib.ExitStack()
    p_sbuf, p_psum = mk(a2_st)
    HTOK = TOK // 2
    NTH = 2 if SIM_SMALL else NT // 2
    xq_all = p_sbuf("xq_all", [128, CH * HTOK], BF16)
    rgb_all = p_sbuf("rgb_all", [128, HTOK], BF16)
    rub_all = p_sbuf("rub_all", [128, HTOK], BF16)
    tmpA2 = p_sbuf("tmpA2", [128, 1024], F32)
    wslc = [[p_sbuf(f"wslc{w}_{i}", [128, 1024], BF16) for i in range(2)]
            for w in range(2)]
    gat = [p_sbuf(f"gat{i}", [128, 1024], BF16) for i in range(2)]
    gat2 = [p_sbuf(f"gat2_{i}", [128, 1024], BF16) for i in range(2)]
    upv = [p_sbuf(f"upv{i}", [128, 1024], BF16) for i in range(4)]
    guv = [p_sbuf(f"guv{i}", [128, 1024], BF16) for i in range(2)]
    absgu = [p_sbuf(f"absgu{i}", [128, 1024], BF16) for i in range(2)]
    runmx = [p_sbuf(f"runmx{i}", [128, 1024], BF16) for i in range(NTH // 2)]
    trmx = p_sbuf("trmx", [128, 1024], BF16)
    red1 = p_sbuf("red1", [128, 32], F32)
    flmx = p_sbuf("flmx", [32, 128], F32)
    amfin = p_sbuf("amfin", [32, 32], F32)
    amfinT = p_sbuf("amfinT", [32, 32], F32)
    p_g = [p_psum(f"p_g{i}", [128, 1024], F32) for i in range(2)]
    p_u = [p_psum(f"p_u{i}", [128, 1024], F32) for i in range(2)]

    def wslc_ap(tns, m):
        return bass.AP(tns.tensor, m * 128,
                       [[INNER, 128], [128 * INNER, 8], [1, 128]])

    rec_w = {}
    rec_si = {}
    rec_pe = {}
    rec_ga = {}
    rec_gv = {}
    am_store_qg = {}
    rec_ab = {}
    for half in range(1 if SIM_SMALL else 2):
        tb = half * NTH
        with nc.Block(no_gpsimd_drain=True) as blk:
            @blk.sync
            def _(s):
                for c in range(CH):
                    for gg in range(NTH):
                        dma(s, xq_all[:, c * HTOK + gg * TT:c * HTOK + (gg + 1) * TT],
                            bass.AP(s_xq.tensor, c * (128 * TOK) + (tb + gg) * TT,
                                    [[TOK, 128], [1, TT]]), "qs")
                s.wait_ge(qs_h, qn["qs"])
        for gg in range(NTH):
            with nc.Block(no_gpsimd_drain=True) as blk:
                @blk.sync
                def _(s):
                    for ssi in range(NSUB):
                        dma(s, tmpA2[:, ssi * 128:(ssi + 1) * 128],
                            bass.AP(s_rg.tensor, ((tb + gg) * 8 + ssi) * 128,
                                    [[0, 128], [1, 128]]), "qs")
                        dma(s, tmpB2_ap(tmpA2, ssi),
                            bass.AP(s_rg.tensor, ((tb + gg) * 8 + 4 + ssi) * 128,
                                    [[0, 128], [1, 128]]), "qs")
                    s.wait_ge(qs_h, qn["qs"])
            with nc.Block(no_gpsimd_drain=True) as blk:
                @blk.vector
                def _(v):
                    v.tensor_copy(rgb_all[:, gg * TT:(gg + 1) * TT], tmpA2[:, 0:512])
                    v.tensor_copy(rub_all[:, gg * TT:(gg + 1) * TT], tmpA2[:, 512:1024])
        MCHs = 2 if SIM_SMALL else MCH
        NP = MCHs * (NTH // 2)
        def unit(J):
            return J // (NTH // 2), (J % (NTH // 2)) * 2
        with nc.Block(no_gpsimd_drain=True) as blk:
            @blk.sync
            def _(s):
                dma(s, wslc[0][0][:], wslc_ap(s_wgq, 0), "qs")
                dma(s, wslc[1][0][:], wslc_ap(s_wuq, 0), "qs")
                rec_w[(half, 0)] = qn["qs"]
        for J in range(NP + 8):
            with nc.Block(no_gpsimd_drain=True) as blk:
                @blk.sync
                def _(s):
                    if J < NP and J % (NTH // 2) == 0:
                        m = unit(J)[0]
                        if m + 1 < MCH:
                            dma(s, wslc[0][(m + 1) % 2][:], wslc_ap(s_wgq, m + 1), "qs")
                            dma(s, wslc[1][(m + 1) % 2][:], wslc_ap(s_wuq, m + 1), "qs")
                            rec_w[(half, m + 1)] = qn["qs"]
                @blk.tensor
                def _(t):
                    if J < NP:
                        m, gg = unit(J)
                        t.wait_ge(qs_h, rec_w[(half, m)])
                        mm = None
                        for pgp, wsl in ((p_g, wslc[0]), (p_u, wslc[1])):
                            for tt2 in range(2):
                                for k in range(CH):
                                    mm = t.matmul(pgp[J % 2][:, tt2 * TT:(tt2 + 1) * TT],
                                                  wsl[m % 2][:, k * 128:(k + 1) * 128],
                                                  xq_all[:, k * HTOK + (gg + tt2) * TT:
                                                         k * HTOK + (gg + tt2 + 1) * TT],
                                                  start=(k == 0), stop=(k == CH - 1))
                        pmark(mm)
                        rec_pe[J] = epn[0]
                @blk.vector
                def _(v):
                    li = None
                    if 1 <= J < NP + 1:
                        Jp = J - 1
                        v.wait_ge(ep_h, rec_pe[Jp])
                        li = v.tensor_mul(gat[Jp % 2][:], p_g[Jp % 2][:],
                                          rgb_all[:, rgslice(Jp, NTH)])
                        v.tensor_mul(upv[Jp % 4][:], p_u[Jp % 2][:],
                                     rub_all[:, rgslice(Jp, NTH)])
                        rec_ga[Jp] = edn[0] + 1
                    if 3 <= J < NP + 3:
                        Jp = J - 3
                        v.wait_ge(ea_h, rec_si[Jp])
                        li = v.tensor_mul(guv[Jp % 2][:], gat2[Jp % 2][:], upv[Jp % 4][:])
                        rec_gv[Jp] = edn[0] + 1
                    if li is not None:
                        dmark(li)
                    if 5 <= J < NP + 5:
                        Jp = J - 5
                        m, gg = unit(Jp)
                        gi = gg // 2
                        v.wait_ge(ea_h, rec_ab[Jp])
                        if m == 0:
                            v.tensor_copy(runmx[gi][:], absgu[Jp % 2][:])
                        else:
                            v.tensor_max(runmx[gi][:], runmx[gi][:],
                                         absgu[Jp % 2][:])
                @blk.scalar
                def _(sc):
                    if 2 <= J < NP + 2:
                        Jp = J - 2
                        sc.wait_ge(ed_h, rec_ga[Jp])
                        amark(sc.activation(gat2[Jp % 2][:], gat[Jp % 2][:], AF.Silu))
                        rec_si[Jp] = ean[0]
                    if 4 <= J < NP + 4:
                        Jp = J - 4
                        sc.wait_ge(ed_h, rec_gv[Jp])
                        amark(sc.activation(absgu[Jp % 2][:], guv[Jp % 2][:], AF.Abs))
                        rec_ab[Jp] = ean[0]
                @blk.gpsimd
                def _(g):
                    if 4 <= J < NP + 4:
                        Jp = J - 4
                        m, gg = unit(Jp)
                        g.wait_ge(ed_h, rec_gv[Jp])
                        dma(g, bass.AP(s_gu.tensor,
                                       m * (128 * TOK) + (half * NTH + gg) * TT,
                                       [[TOK, 128], [1, 2 * TT]]), guv[Jp % 2][:], "qg")
                        g.wait_ge(qg_h, qn["qg"])
        for gi in range(NTH // 2):
            fi = half * (NTH // 2) + gi
            with nc.Block(no_gpsimd_drain=True) as blk:
                red_mark = edn[0] + 1
                @blk.sync
                def _(s):
                    s.wait_ge(ed_h, red_mark)
                    for a in range(4):
                        dma(s, flmx[:, 32 * a:32 * (a + 1)],
                            red1[32 * a:32 * (a + 1), :], "qs")
                fl_tgt = qn["qs"]
                @blk.vector
                def _(v):
                    v.transpose(trmx[:], runmx[gi][:])
                    v.drain()
                    dmark(v.tensor_reduce(
                        red1[:], trmx.ap().rearrange("p (j v) -> p j v", v=32),
                        axis=AX.X, op=ALU.max))
                    v.wait_ge(qs_h, fl_tgt)
                    if fi >= 1:
                        v.wait_ge(qg_h, am_store_qg[fi - 1])
                    v.tensor_reduce(
                        amfin[:], flmx.ap().rearrange("p (a j) -> p j a", a=4),
                        axis=AX.X, op=ALU.max)
                    v.drain()
                    dmark(v.transpose(amfinT[:], amfin[:]))
                am_mark = edn[0]
                @blk.gpsimd
                def _(g):
                    g.wait_ge(ed_h, am_mark)
                    dma(g, bass.AP(s_am.tensor, (half * NTH + gi * 2) * TT,
                                   [[32, 32], [1, 32]]), amfinT[:], "qg")
                    am_store_qg[fi] = qn["qg"]
                    g.wait_ge(qg_h, qn["qg"])
    a2_st.close()

    b_st = contextlib.ExitStack()
    p_sbuf, p_psum = mk(b_st)
    wd_sb = p_sbuf("wd_sb", [128, MCH * D], BF16)
    gu_all = p_sbuf("gu_all", [128, MCH * TT], BF16)
    guq_all = p_sbuf("guq_all", [128, MCH * TT], BF16)
    rowmx = p_sbuf("rowmx", [1, TT], F32)
    rowmn = p_sbuf("rowmn", [1, TT], F32)
    rowsc = p_sbuf("rowsc", [1, TT], BF16)
    rowrd = p_sbuf("rowrd", [1, TT], BF16)
    scb = p_sbuf("scb", [128, TT], F32)
    rdb = p_sbuf("rdb", [128, TT], F32)
    ffn_f = p_sbuf("ffn_f", [128, CH * TT], BF16)
    ffnT = p_sbuf("ffnT", [128, NSUB * D], BF16)
    x2l = p_sbuf("x2l", [128, NSUB * D], F32)
    outT = p_sbuf("outT", [128, NSUB * D], F32)
    p_dn = [p_psum(f"p_dn{i}", [128, TT], F32) for i in range(4)]
    p3trb = p_psum("p3trb", [128, 2048], BF16)
    p_b2 = p_psum("p_b2", [128, 2 * TT], F32)

    out_flat = out_h.rearrange("b l d -> (b l) d")
    rec_addf = {}
    rec_amq = {}

    with nc.Block(no_gpsimd_drain=True) as blk:
        @blk.sync
        def _(s):
            for k in range(MCH):
                dma(s, wd_sb[:, k * D:(k + 1) * D],
                    bass.AP(s_wdq.tensor, k * 128 * D, [[D, 128], [1, D]]), "qs")
            s.wait_ge(qs_h, qn["qs"])

    for g_i in range(NTs):
        with nc.Block(no_gpsimd_drain=True) as blk:
            @blk.sync
            def _(s):
                for k in range(MCH):
                    dma(s, gu_all[:, k * TT:(k + 1) * TT],
                        bass.AP(s_gu.tensor, k * (128 * TOK) + g_i * TT,
                                [[TOK, 128], [1, TT]]), "qs")
                s.wait_ge(qs_h, qn["qs"])
            @blk.gpsimd
            def _(g):
                if g_i >= 1:
                    g.wait_ge(ed_h, rec_addf[g_i - 1])
                for ssi in range(NSUB):
                    dma(g, x2l[:, ssi * D:(ssi + 1) * D],
                        s_x2[g_i * NSUB + ssi, :, :], "qg")
                dma(g, rowmx[:], bass.AP(s_am.tensor, g_i * TT,
                                         [[TT, 1], [1, TT]]), "qg")
                rec_amq[g_i] = qn["qg"]
                g.wait_ge(qg_h, qn["qg"])
        with nc.Block(no_gpsimd_drain=True) as blk:
            @blk.vector
            def _(v):
                v.wait_ge(qg_h, rec_amq[g_i])
                v.tensor_scalar_max(rowmx[:], rowmx[:], EPS)
                v.drain()
        with nc.Block(no_gpsimd_drain=True) as blk:
            @blk.vector
            def _(v):
                v.reciprocal(rowmn[:], rowmx[:])
        with nc.Block(no_gpsimd_drain=True) as blk:
            @blk.vector
            def _(v):
                v.tensor_scalar_mul(rowmn[:], rowmn[:], 127.0)
                v.drain()
                v.tensor_copy(rowsc[:], rowmn[:])
                v.tensor_scalar_mul(rowmx[:], rowmx[:], 1.0 / 127.0)
                v.drain()
                v.tensor_scalar_mul(rowmx[:], rowmx[:], wq_cols[0:1, 6:7])
                v.drain()
                dmark(v.tensor_copy(rowrd[:], rowmx[:]))
        with nc.Block(no_gpsimd_drain=True) as blk:
            @blk.tensor
            def _(t):
                t.wait_ge(ed_h, edn[0])
                t.matmul(p_b2[:, 0:TT], onesr[:], rowsc[0:1, :],
                         start=True, stop=True)
                pmark(t.matmul(p_b2[:, TT:2 * TT], onesr[:], rowrd[0:1, :],
                               start=True, stop=True))
        with nc.Block(no_gpsimd_drain=True) as blk:
            @blk.vector
            def _(v):
                v.wait_ge(ep_h, epn[0])
                v.tensor_copy(scb[:], p_b2[:, 0:TT])
                dmark(v.tensor_copy(rdb[:], p_b2[:, TT:2 * TT]))
        for q in range(18):
            with nc.Block(no_gpsimd_drain=True) as blk:
                @blk.gpsimd
                def _(g):
                    if q < 16:
                        g.wait_ge(ed_h, edn[0])
                        last = None
                        for kk in range(2):
                            k = q * 2 + kk
                            last = g.tensor_tensor(outT[:, (q % 2) * 2048 + kk * TT:
                                                        (q % 2) * 2048 + (kk + 1) * TT],
                                                   gu_all[:, k * TT:(k + 1) * TT],
                                                   scb[:], ALU.mult)
                        omark(last)
                @blk.vector
                def _(v):
                    if 1 <= q < 17:
                        v.wait_ge(eo_h, eon[0] - (1 if q < 16 else 0))
                        for kk in range(2):
                            v.tensor_copy(tmpI[:, ((q - 1) % 2) * 1024 + kk * TT:
                                               ((q - 1) % 2) * 1024 + (kk + 1) * TT],
                                          outT[:, ((q - 1) % 2) * 2048 + kk * TT:
                                               ((q - 1) % 2) * 2048 + (kk + 1) * TT])
                    if q >= 2:
                        for kk in range(2):
                            k = (q - 2) * 2 + kk
                            v.tensor_copy(guq_all[:, k * TT:(k + 1) * TT],
                                          tmpI[:, ((q - 2) % 2) * 1024 + kk * TT:
                                               ((q - 2) % 2) * 1024 + (kk + 1) * TT])
                        dmarkd(v)
        for rnd in range(2):
            with nc.Block(no_gpsimd_drain=True) as blk:
                @blk.tensor
                def _(t):
                    t.wait_ge(ed_h, edn[0])
                    mm = None
                    for dmi in range(4):
                        dm = rnd * 4 + dmi
                        for k in range(MCH):
                            mm = t.matmul(p_dn[dmi][:],
                                          wd_sb[:, k * D + dm * 128:k * D + (dm + 1) * 128],
                                          guq_all[:, k * TT:(k + 1) * TT],
                                          start=(k == 0), stop=(k == MCH - 1))
                    pmark(mm)
            with nc.Block(no_gpsimd_drain=True) as blk:
                @blk.vector
                def _(v):
                    v.wait_ge(ep_h, epn[0])
                    for dmi in range(4):
                        dm = rnd * 4 + dmi
                        v.tensor_mul(ffn_f[:, dm * TT:(dm + 1) * TT],
                                     p_dn[dmi][:], rdb[:])
                    dmarkd(v)
        for rnd in range(2):
            subs = (0, 1) if rnd == 0 else (2, 3)
            with nc.Block(no_gpsimd_drain=True) as blk:
                @blk.tensor
                def _(t):
                    t.wait_ge(ed_h, edn[0])
                    tr = None
                    for j, ssi in enumerate(subs):
                        for dm in range(CH):
                            col = (j * 8 + dm) * 128
                            tr = t.transpose(p3trb[:, col:col + 128],
                                             ffn_f[:, dm * TT + ssi * 128:dm * TT + (ssi + 1) * 128],
                                             idm[:])
                    pmark(tr)
            with nc.Block(no_gpsimd_drain=True) as blk:
                @blk.scalar
                def _(sc):
                    sc.wait_ge(ep_h, epn[0])
                    last = None
                    for j, ssi in enumerate(subs):
                        last = sc.activation(ffnT[:, ssi * D:(ssi + 1) * D],
                                             p3trb[:, j * D:(j + 1) * D], AF.Copy)
                    amark(last)
        with nc.Block(no_gpsimd_drain=True) as blk:
            @blk.scalar
            def _(sc):
                amark(sc.activation(gu_all.ap()[:, 0:4 * D], ffnT[:], AF.Square))
            @blk.vector
            def _(v):
                v.wait_ge(ea_h, ean[0] - 1)
                v.reduce_sum(stat[:, 0:4],
                             ffnT.ap().rearrange("p (s d) -> p s d", s=NSUB),
                             axis=AX.X)
        with nc.Block(no_gpsimd_drain=True) as blk:
            @blk.vector
            def _(v):
                v.wait_ge(ea_h, ean[0])
                v.reduce_sum(stat[:, 4:8],
                             gu_all.ap()[:, 0:4 * D].rearrange("p (s d) -> p s d", s=NSUB),
                             axis=AX.X)
        with nc.Block(no_gpsimd_drain=True) as blk:
            @blk.vector
            def _(v):
                v.tensor_scalar_mul(stat[:, 0:4], stat[:, 0:4], 1.0 / D)
                v.tensor_scalar_mul(stat[:, 4:8], stat[:, 4:8], 1.0 / D)
                v.drain()
                v.tensor_mul(stat[:, 8:12], stat[:, 0:4], stat[:, 0:4])
                v.drain()
                dmark(v.tensor_sub(stat[:, 8:12], stat[:, 4:8], stat[:, 8:12]))
        with nc.Block(no_gpsimd_drain=True) as blk:
            @blk.scalar
            def _(sc):
                sc.wait_ge(ed_h, edn[0])
                lnsrc = bass.AP(stat.ap().tensor, 8, [[40, 128], [0, 128], [1, 4]])
                sc.activation(junk.ap().rearrange("p (r c) -> p r c", c=4), lnsrc,
                              AF.Ln, bias=cc[:, 0:1])
                sc.drain()
                amark(sc.activation(rstdw[:], junk[:], AF.Exp, scale=cc[:, 1:2]))
        with nc.Block(no_gpsimd_drain=True) as blk:
            @blk.vector
            def _(v):
                v.wait_ge(ea_h, ean[0])
                for ssi in range(NSUB):
                    v.tensor_scalar(ffnT[:, ssi * D:(ssi + 1) * D],
                                    ffnT[:, ssi * D:(ssi + 1) * D],
                                    stat[:, ssi:ssi + 1], rstdw[:, ssi:ssi + 1],
                                    ALU.subtract, ALU.mult)
                v.drain()
                dmark(v.tensor_add(outT[:], x2l[:], ffnT[:]))
                rec_addf[g_i] = edn[0]
        with nc.Block(no_gpsimd_drain=True) as blk:
            @blk.gpsimd
            def _(g):
                g.wait_ge(ed_h, edn[0])
                for ssi in range(NSUB):
                    t0 = g_i * TT + ssi * 128
                    dma(g, out_flat[t0:t0 + 128, :], outT[:, ssi * D:(ssi + 1) * D],
                        "qg")
                if dbg:
                    dma(g, s_dbg2[g_i, :, :], ffn_f[:], "qg")
                    dma(g, s_dbg3[g_i, :, :], ffnT[:], "qg")
                g.wait_ge(qg_h, qn["qg"])

    with nc.Block(no_gpsimd_drain=True) as blk:
        @blk.gpsimd
        def _(g):
            g.wait_ge(qg_h, qn["qg"])
            g.wait_ge(qs_h, qn["qs"])
        @blk.vector
        def _(v):
            v.wait_ge(qg_h, qn["qg"])
    b_st.close()

    g_st.close()
    return nc


def tmpB2_ap(tmpA2, ssi):
    return tmpA2[:, 512 + ssi * 128:512 + (ssi + 1) * 128]


def rgslice(Jp, NTH):
    gg = (Jp % (NTH // 2)) * 2
    return slice(gg * TT, (gg + 2) * TT)


def kernel(**inputs):
    x = np.ascontiguousarray(inputs["L0_attn_x"])
    A_log = np.asarray(inputs["L0_attn_A_log"])
    Wx = np.ascontiguousarray(inputs["L0_attn_Wx"]).astype(np.float32, copy=False)
    Wdt = np.asarray(inputs["L0_attn_Wdt"])
    bdt = np.asarray(inputs["L0_attn_bdt"])
    Wout = np.asarray(inputs["L0_attn_Wout"])
    bout = np.asarray(inputs["L0_attn_bout"])
    wg = np.asarray(inputs["L0_ffn_w_gate"])
    wu = np.asarray(inputs["L0_ffn_w_up"])
    wd = np.asarray(inputs["L0_ffn_w_down"])

    acols = np.empty((128, 128), dtype=np.float32)
    for hp in range(HP):
        for n in range(16):
            acols[0:64, hp * 16 + n] = A_log[2 * hp, n]
            acols[64:128, hp * 16 + n] = A_log[2 * hp + 1, n]
    in_common = {
        "acols": acols,
        "bdt": np.ascontiguousarray(bdt.reshape(64, 1), dtype=np.float32),
        "wx": Wx,
        "wdtT": np.ascontiguousarray(Wdt.T),
        "wxbcT": np.ascontiguousarray(Wx[64:96, :].T),
        "woutT": np.ascontiguousarray(Wout.T),
        "boutc": np.ascontiguousarray(bout.reshape(CH, 128).T),
        "wgT": np.ascontiguousarray(wg.T),
        "wuT": np.ascontiguousarray(wu.T),
        "wdT": np.ascontiguousarray(wd.T),
        "onesr": np.ones((1, 128), dtype=ml_dtypes.bfloat16),
        "onesc": np.ones((128, 1), dtype=ml_dtypes.bfloat16),
        "idm": np.eye(128, dtype=ml_dtypes.bfloat16),
    }
    nc = build_nc()
    in_maps = [dict(in_common, x=np.ascontiguousarray(x[c * B:(c + 1) * B]))
               for c in range(NCORES)]
    res = run_bass_kernel_spmd(nc, in_maps, core_ids=list(range(NCORES)))
    return np.concatenate([r["out"] for r in res.results], axis=0)


# version 06
# speedup vs baseline: 2.0134x; 1.0191x over previous
import contextlib
import numpy as np
import ml_dtypes
import concourse.bass as bass
import concourse.mybir as mybir
from concourse.bass_utils import run_bass_kernel_spmd

F32 = mybir.dt.float32
BF16 = mybir.dt.bfloat16
I32 = mybir.dt.int32
AF = mybir.ActivationFunctionType
ALU = mybir.AluOpType
AX = mybir.AxisListType

NCORES = 8
B = 4
L = 2048
D = 1024
H = 16
HP = 8
DH = 64
DS = 16
INNER = 4096
CH = D // 128
MCH = INNER // 128
TT = 512
NT = (B * L) // TT
NSUB = TT // 128
EPS = 1e-5
TOK = B * L


SIM_SMALL = False


def build_nc(dbg=False):
    NTs = 2 if SIM_SMALL else NT
    NITs = 2 if SIM_SMALL else B * HP
    nc = bass.Bass("TRN2", target_bir_lowering=False)

    x_h = nc.dram_tensor("x", [B, L, D], F32, kind="ExternalInput").ap()
    acols_h = nc.dram_tensor("acols", [128, 128], F32, kind="ExternalInput").ap()
    bdt_h = nc.dram_tensor("bdt", [64, 1], F32, kind="ExternalInput").ap()
    wx_h = nc.dram_tensor("wx", [96, 64], F32, kind="ExternalInput").ap()
    wdtT_h = nc.dram_tensor("wdtT", [64, 64], F32, kind="ExternalInput").ap()
    wxbcT_h = nc.dram_tensor("wxbcT", [64, 32], F32, kind="ExternalInput").ap()
    woutT_h = nc.dram_tensor("woutT", [D, D], F32, kind="ExternalInput").ap()
    bout_h = nc.dram_tensor("boutc", [128, CH], F32, kind="ExternalInput").ap()
    wgT_h = nc.dram_tensor("wgT", [D, INNER], F32, kind="ExternalInput").ap()
    wuT_h = nc.dram_tensor("wuT", [D, INNER], F32, kind="ExternalInput").ap()
    wdT_h = nc.dram_tensor("wdT", [INNER, D], F32, kind="ExternalInput").ap()
    onesr_h = nc.dram_tensor("onesr", [1, 128], BF16, kind="ExternalInput").ap()
    onesc_h = nc.dram_tensor("onesc", [128, 1], BF16, kind="ExternalInput").ap()
    idm_h = nc.dram_tensor("idm", [128, 128], BF16, kind="ExternalInput").ap()
    out_h = nc.dram_tensor("out", [B, L, D], F32, kind="ExternalOutput").ap()

    KD = "ExternalOutput" if dbg else "Internal"
    s_dt = nc.dram_tensor("s_dt", [B, H, DH, L], F32, kind=KD).ap()
    s_z = nc.dram_tensor("s_z", [B, H, DH, L], F32, kind=KD).ap()
    s_bc = nc.dram_tensor("s_bc", [B, H, 2, DS, L], BF16, kind=KD).ap()
    s_y = nc.dram_tensor("s_y", [CH, 128, TOK], BF16, kind=KD).ap()
    s_x2 = nc.dram_tensor("s_x2", [TOK // 128, 128, D], F32, kind=KD).ap()
    s_xq = nc.dram_tensor("s_xq", [CH, 128, TOK], BF16, kind=KD).ap()
    s_gu = nc.dram_tensor("s_gu", [MCH, 128, TOK], BF16, kind=KD).ap()
    s_wgq = nc.dram_tensor("s_wgq", [D, INNER], BF16, kind=KD).ap()
    s_wuq = nc.dram_tensor("s_wuq", [D, INNER], BF16, kind=KD).ap()
    s_wdq = nc.dram_tensor("s_wdq", [INNER, D], BF16, kind=KD).ap()
    s_stage = nc.dram_tensor("s_stage", [1, 16], F32, kind=KD).ap()
    s_rg = nc.dram_tensor("s_rg", [NT, 8, 128], F32, kind=KD).ap()
    s_am = nc.dram_tensor("s_am", [1, TOK], F32, kind=KD).ap()
    s_dbg = nc.dram_tensor("s_dbg", [NT, 128, 40], F32, kind=KD).ap()
    s_dbg2 = nc.dram_tensor("s_dbg2", [NT, 128, 4096], BF16, kind=KD).ap()
    s_dbg3 = nc.dram_tensor("s_dbg3", [NT, 128, 4096], BF16, kind=KD).ap()

    g_st = contextlib.ExitStack()

    def mk(stk):
        return (lambda n, s, d: stk.enter_context(nc.sbuf_tensor(n, s, d)),
                lambda n, s, d: stk.enter_context(nc.psum_tensor(n, s, d)))

    sbuf, psum = mk(g_st)

    ascale = sbuf("ascale", [128, 128], F32)
    bdt_c = sbuf("bdt_c", [64, 1], F32)
    lhsT_f = sbuf("lhsT_f", [64, 96], F32)
    lhsT_b = sbuf("lhsT_b", [64, 96], BF16)
    bout_c = sbuf("bout_c", [128, CH], F32)
    onesr = sbuf("onesr_s", [1, 128], BF16)
    onesc = sbuf("onesc_s", [128, 1], BF16)
    idm = sbuf("idm_s", [128, 128], BF16)
    idm_f = sbuf("idmf_s", [128, 128], F32)
    wq_cols = sbuf("wq_cols", [128, 8], F32)
    stat = sbuf("stat", [128, 40], F32)
    tmpI = sbuf("tmpI", [128, 2 * D], I32)
    cc = sbuf("cc", [128, 4], F32)
    junk = sbuf("junk", [128, 512], F32)
    rstdw = sbuf("rstdw", [128, 512], F32)

    qs_h = g_st.enter_context(nc.semaphore("qs"))
    qg_h = g_st.enter_context(nc.semaphore("qg"))
    ev_h = g_st.enter_context(nc.semaphore("ev"))
    ea_h = g_st.enter_context(nc.semaphore("ea"))
    eo_h = g_st.enter_context(nc.semaphore("eo"))
    ep_h = g_st.enter_context(nc.semaphore("ep"))
    ed_h = g_st.enter_context(nc.semaphore("ed"))
    qn = {"qs": 0, "qg": 0}
    evn = [0]
    ean = [0]
    eon = [0]
    epn = [0]
    edn = [0]

    def dmark(inst):
        inst.then_inc(ed_h, 1)
        edn[0] += 1

    def dmarkd(eng):
        eng.drain().then_inc(ed_h, 1)
        edn[0] += 1

    def pmark(inst):
        inst.then_inc(ep_h, 1)
        epn[0] += 1

    def amark(inst):
        inst.then_inc(ea_h, 1)
        ean[0] += 1

    def omark(inst):
        inst.then_inc(eo_h, 1)
        eon[0] += 1

    def dma(eng, dst, src, q):
        h = qs_h if q == "qs" else qg_h
        eng.wait_ge(h, qn[q])
        eng.dma_start(out=dst, in_=src).then_inc(h, 16)
        qn[q] += 16

    su_st = contextlib.ExitStack()
    s_sbuf, s_psum = mk(su_st)
    tmpA = s_sbuf("tmpA", [128, 2048], F32)
    tmpC = s_sbuf("tmpC", [128, 2048], BF16)
    tmpC2 = s_sbuf("tmpC2", [128, 2048], BF16)
    wld = [s_sbuf(f"wld{i}", [128, 2048], F32) for i in range(2)]
    wabs = s_sbuf("wabs", [128, 2048], F32)
    onesc32 = s_sbuf("onesc32", [128, 1], F32)
    p_small = s_psum("p_small", [96, 512], F32)
    p_row = s_psum("p_row", [1, 512], F32)

    with nc.Block(no_gpsimd_drain=True) as blk:
        @blk.sync
        def _(s):
            dma(s, ascale[:], acols_h[:], "qs")
            dma(s, bdt_c[:], bdt_h[:], "qs")
            dma(s, tmpA[0:96, 0:64], wx_h[:], "qs")
            dma(s, tmpA[0:64, 64:128], wdtT_h[:], "qs")
            dma(s, lhsT_f[:, 64:96], wxbcT_h[:], "qs")
            dma(s, bout_c[:], bout_h[:], "qs")
            dma(s, onesr[:], onesr_h[:], "qs")
            dma(s, onesc[:], onesc_h[:], "qs")
            dma(s, idm[:], idm_h[:], "qs")
            s.wait_ge(qs_h, qn["qs"])
        @blk.gpsimd
        def _(g):
            g.memset(cc[:, 0:1], EPS)
            g.memset(cc[:, 1:2], -0.5)
            g.memset(cc[:, 2:3], -1.0)
            g.memset(junk[:], 0.0)
            g.memset(stat[:], 0.0)
            omark(g.memset(rstdw[:], 0.0))
    with nc.Block(no_gpsimd_drain=True) as blk:
        @blk.scalar
        def _(sc):
            sc.wait_ge(eo_h, eon[0])
            sc.activation(tmpA[0:128, 128:256], ascale[:], AF.Exp)
            amark(sc.activation(junk[:], junk[:], AF.Copy))
        @blk.tensor
        def _(t):
            pmark(t.matmul(p_small[0:64, 0:64], tmpA[0:64, 0:64], tmpA[0:64, 64:128],
                           start=True, stop=True))
    with nc.Block(no_gpsimd_drain=True) as blk:
        @blk.vector
        def _(v):
            v.wait_ge(ea_h, ean[0])
            v.wait_ge(ep_h, epn[0])
            v.tensor_scalar_mul(ascale[:], tmpA[0:128, 128:256], -1.0)
            v.tensor_copy(lhsT_f[:, 0:64], p_small[0:64, 0:64])
    with nc.Block(no_gpsimd_drain=True) as blk:
        @blk.vector
        def _(v):
            v.tensor_copy(lhsT_b[:], lhsT_f[:])
            v.tensor_copy(onesc32[:], onesc[:])
            dmark(v.tensor_copy(idm_f[:], idm[:]))

    w_list = [(wgT_h, s_wgq, D, INNER, 0), (wuT_h, s_wuq, D, INNER, 1),
              (wdT_h, s_wdq, INNER, D, 2)]
    for (src, _d, R, Cc, idx) in w_list:
        CW = min(2048, Cc)
        cpr = Cc // CW
        ntile = (R // 128) * cpr
        if SIM_SMALL:
            ntile = 2
        njr = CW // 512
        for it in range(ntile + 1):
            with nc.Block(no_gpsimd_drain=True) as blk:
                @blk.sync
                def _(s):
                    if it < ntile:
                        r, c = it // cpr, it % cpr
                        dma(s, wld[it % 2][:, 0:CW],
                            src[r * 128:(r + 1) * 128, c * CW:(c + 1) * CW], "qs")
                    s.wait_ge(qs_h, qn["qs"])
                @blk.scalar
                def _(sc):
                    if it >= 1:
                        amark(sc.activation(wabs[:, 0:CW], wld[(it - 1) % 2][:, 0:CW],
                                            AF.Abs))
            if it >= 1:
                with nc.Block(no_gpsimd_drain=True) as blk:
                    @blk.tensor
                    def _(t):
                        t.wait_ge(ea_h, ean[0])
                        mm = None
                        for j in range(njr):
                            mm = t.matmul(p_row[0:1, 0:512], onesc32[:],
                                          wabs[:, j * 512:(j + 1) * 512],
                                          start=(it == 1 and j == 0),
                                          stop=(it == ntile and j == njr - 1))
                        if it == ntile:
                            pmark(mm)
        with nc.Block(no_gpsimd_drain=True) as blk:
            @blk.vector
            def _(v):
                v.wait_ge(ep_h, epn[0])
                v.reduce_sum(tmpA[0:1, 0:1], p_row[0:1, 0:512], axis=AX.X)
        with nc.Block(no_gpsimd_drain=True) as blk:
            @blk.vector
            def _(v):
                v.tensor_scalar(tmpA[0:1, 1:2], tmpA[0:1, 0:1],
                                1.0 / (R * Cc), EPS, ALU.mult, ALU.max)
        with nc.Block(no_gpsimd_drain=True) as blk:
            @blk.vector
            def _(v):
                dmark(v.reciprocal(tmpA[0:1, 2:3], tmpA[0:1, 1:2]))
        with nc.Block(no_gpsimd_drain=True) as blk:
            @blk.gpsimd
            def _(g):
                g.wait_ge(ed_h, edn[0])
                dma(g, s_stage[0:1, idx * 2:idx * 2 + 2], tmpA[0:1, 1:3], "qg")
                g.wait_ge(qg_h, qn["qg"])
    with nc.Block(no_gpsimd_drain=True) as blk:
        @blk.sync
        def _(s):
            for idx in range(3):
                dma(s, wq_cols[:, 4 + idx:5 + idx],
                    bass.AP(s_stage.tensor, idx * 2, [[0, 128], [1, 1]]), "qs")
                dma(s, wq_cols[:, idx:idx + 1],
                    bass.AP(s_stage.tensor, idx * 2 + 1, [[0, 128], [1, 1]]), "qs")
            s.wait_ge(qs_h, qn["qs"])

    for (src, dst, R, Cc, idx) in w_list:
        CW = min(2048, Cc)
        cpr = Cc // CW
        ntile = (R // 128) * cpr
        if SIM_SMALL:
            ntile = 2
        for it in range(ntile + 2):
            with nc.Block(no_gpsimd_drain=True) as blk:
                @blk.sync
                def _(s):
                    if it < ntile:
                        r, c = it // cpr, it % cpr
                        dma(s, wld[it % 2][:, 0:CW],
                            src[r * 128:(r + 1) * 128, c * CW:(c + 1) * CW], "qs")
                    s.wait_ge(qs_h, qn["qs"])
                @blk.vector
                def _(v):
                    if 1 <= it <= ntile:
                        j = it - 1
                        qb = tmpC if j % 2 == 0 else tmpC2
                        v.tensor_scalar_mul(tmpA[:, 0:CW], wld[j % 2][:, 0:CW],
                                            wq_cols[:, idx:idx + 1])
                        v.drain()
                        v.tensor_copy(tmpI[:, 0:CW], tmpA[:, 0:CW])
                        v.drain()
                        v.tensor_scalar(tmpI[:, 0:CW], tmpI[:, 0:CW], 1, -1,
                                        ALU.min, ALU.max)
                        v.drain()
                        dmark(v.tensor_copy(qb[:, 0:CW], tmpI[:, 0:CW]))
                @blk.gpsimd
                def _(g):
                    if it >= 2:
                        g.wait_ge(ed_h, edn[0])
                        j = it - 2
                        r, c = j // cpr, j % cpr
                        qb = tmpC if j % 2 == 0 else tmpC2
                        dma(g, dst[r * 128:(r + 1) * 128, c * CW:(c + 1) * CW],
                            qb[:, 0:CW], "qg")
                        g.wait_ge(qg_h, qn["qg"])
    su_st.close()

    p1_st = contextlib.ExitStack()
    p_sbuf, p_psum = mk(p1_st)
    xt = [p_sbuf(f"xt{i}", [128, NSUB * D], F32) for i in range(2)]
    sqb = p_sbuf("sqb", [128, 4 * D], F32)
    xn1T = p_sbuf("xn1T", [128, NSUB * D], BF16)
    xf = p_sbuf("xf", [64, NSUB * 2048], BF16)
    dt_p1 = [p_sbuf(f"dt_p1{i}", [64, 2048], F32) for i in range(2)]
    z_p1 = [p_sbuf(f"z_p1{i}", [64, 2048], F32) for i in range(2)]
    bc_p1 = [p_sbuf(f"bc_p1{i}", [32, 2048], BF16) for i in range(2)]
    sfp = [p_sbuf(f"sfp{i}", [64, 2048], F32) for i in range(3)]
    p_trA = p_psum("p_trA", [128, 2048], BF16)
    p_trB = p_psum("p_trB", [128, 2048], BF16)
    p_pj = p_psum("p_pj", [96, 2048], F32)

    x_flat = x_h.rearrange("b l d -> (b l) d")

    for g_i in range(NTs):
        with nc.Block(no_gpsimd_drain=True) as blk:
            @blk.sync
            def _(s):
                for ssi in range(NSUB):
                    t0 = g_i * TT + ssi * 128
                    dma(s, xt[g_i % 2][:, ssi * D:(ssi + 1) * D],
                        x_flat[t0:t0 + 128, :], "qs")
                s.wait_ge(qs_h, qn["qs"])
        xcur = xt[g_i % 2]
        b_i = g_i // 4
        tbase = (g_i % 4) * TT
        with nc.Block(no_gpsimd_drain=True) as blk:
            @blk.scalar
            def _(sc):
                amark(sc.activation(sqb[:], xcur[:], AF.Square))
            @blk.vector
            def _(v):
                v.reduce_sum(stat[:, 0:4],
                             xcur.ap().rearrange("p (s d) -> p s d", s=NSUB),
                             axis=AX.X)
        with nc.Block(no_gpsimd_drain=True) as blk:
            @blk.vector
            def _(v):
                v.wait_ge(ea_h, ean[0])
                v.reduce_sum(stat[:, 4:8],
                             sqb.ap().rearrange("p (s d) -> p s d", s=NSUB),
                             axis=AX.X)
        with nc.Block(no_gpsimd_drain=True) as blk:
            @blk.vector
            def _(v):
                v.tensor_scalar_mul(stat[:, 0:4], stat[:, 0:4], 1.0 / D)
                v.tensor_scalar_mul(stat[:, 4:8], stat[:, 4:8], 1.0 / D)
                v.drain()
                v.tensor_mul(stat[:, 8:12], stat[:, 0:4], stat[:, 0:4])
                v.drain()
                dmark(v.tensor_sub(stat[:, 8:12], stat[:, 4:8], stat[:, 8:12]))
        with nc.Block(no_gpsimd_drain=True) as blk:
            @blk.scalar
            def _(sc):
                sc.wait_ge(ed_h, edn[0])
                lnsrc = bass.AP(stat.ap().tensor, 8, [[40, 128], [0, 128], [1, 4]])
                sc.activation(junk.ap().rearrange("p (r c) -> p r c", c=4), lnsrc,
                              AF.Ln, bias=cc[:, 0:1])
                sc.drain()
                amark(sc.activation(rstdw[:], junk[:], AF.Exp, scale=cc[:, 1:2]))
        with nc.Block(no_gpsimd_drain=True) as blk:
            @blk.vector
            def _(v):
                v.wait_ge(ea_h, ean[0])
                li = None
                for ssi in range(NSUB):
                    li = v.tensor_scalar(xn1T[:, ssi * D:(ssi + 1) * D],
                                         xcur[:, ssi * D:(ssi + 1) * D],
                                         stat[:, ssi:ssi + 1], rstdw[:, ssi:ssi + 1],
                                         ALU.subtract, ALU.mult)
                dmark(li)
            @blk.gpsimd
            def _(g):
                dma(g, s_dbg[g_i, :, 0:16], stat[:, 0:16], "qg")
                g.wait_ge(qg_h, qn["qg"])
        for half in range(2):
            subs = (0, 1) if half == 0 else (2, 3)
            with nc.Block(no_gpsimd_drain=True) as blk:
                @blk.tensor
                def _(t):
                    t.wait_ge(ed_h, edn[0])
                    tr = None
                    for j, ssi in enumerate(subs):
                        for c in range(CH):
                            pt = p_trA if c < 4 else p_trB
                            col = (j * 4 + (c % 4)) * 128
                            tr = t.transpose(pt[:, col:col + 128],
                                             xn1T[:, ssi * D + c * 128:ssi * D + (c + 1) * 128],
                                             idm[:])
                    pmark(tr)
            with nc.Block(no_gpsimd_drain=True) as blk:
                @blk.scalar
                def _(sc):
                    sc.wait_ge(ep_h, epn[0])
                    for j, ssi in enumerate(subs):
                        for lo, pt in ((0, p_trA), (4, p_trB)):
                            src0 = pt.ap()[0:64, j * 512:(j + 1) * 512]\
                                .rearrange("p (c t) -> p c t", c=4)
                            src1 = pt.ap()[64:128, j * 512:(j + 1) * 512]\
                                .rearrange("p (c t) -> p c t", c=4)
                            dsts = xf.ap()[0:64, ssi * 2048:(ssi + 1) * 2048]\
                                .rearrange("p (h t) -> p h t", h=16)
                            sc.activation(dsts[:, 2 * lo:2 * lo + 7:2, :], src0, AF.Copy)
                            amark(sc.activation(dsts[:, 2 * lo + 1:2 * lo + 8:2, :],
                                                src1, AF.Copy))
        for ssi in range(NSUB):
            with nc.Block(no_gpsimd_drain=True) as blk:
                @blk.tensor
                def _(t):
                    t.wait_ge(ea_h, ean[0])
                    mm = None
                    for h_i in range(16):
                        mm = t.matmul(p_pj[:, h_i * 128:(h_i + 1) * 128], lhsT_b[:],
                                      xf[0:64, ssi * 2048 + h_i * 128:ssi * 2048 + (h_i + 1) * 128],
                                      start=True, stop=True)
                    pmark(mm)
            with nc.Block(no_gpsimd_drain=True) as blk:
                @blk.scalar
                def _(sc):
                    sc.wait_ge(ep_h, epn[0])
                    sc.activation(sfp[0][:], p_pj[0:64, :], AF.Abs, bias=bdt_c[:])
                @blk.vector
                def _(v):
                    v.wait_ge(ep_h, epn[0])
                    v.tensor_scalar(sfp[1][:], p_pj[0:64, :], bdt_c[:], 0.0,
                                    ALU.add, ALU.max)
                    dmark(v.tensor_copy(bc_p1[ssi % 2][:], p_pj[64:96, :]))
            with nc.Block(no_gpsimd_drain=True) as blk:
                @blk.scalar
                def _(sc):
                    sc.activation(sfp[2][:], sfp[0][:], AF.Exp, scale=cc[0:64, 2:3])
                    sc.drain()
                    amark(sc.activation(sfp[0][:], sfp[2][:], AF.Ln, bias=1.0))
            with nc.Block(no_gpsimd_drain=True) as blk:
                @blk.vector
                def _(v):
                    v.wait_ge(ea_h, ean[0])
                    v.tensor_add(dt_p1[ssi % 2][:], sfp[0][:], sfp[1][:])
                    v.drain()
                    dmark(v.tensor_mul(z_p1[ssi % 2][:], dt_p1[ssi % 2][:],
                                       xf[0:64, ssi * 2048:(ssi + 1) * 2048]))
            with nc.Block(no_gpsimd_drain=True) as blk:
                @blk.gpsimd
                def _(g):
                    g.wait_ge(ed_h, edn[0])
                    t0 = tbase + ssi * 128
                    dma(g, bass.AP(s_dt.tensor, b_i * (H * DH * L) + t0,
                                   [[L, 64], [DH * L, 16], [1, 128]]),
                        dt_p1[ssi % 2].ap().rearrange("p (h t) -> p h t", h=16), "qg")
                    dma(g, bass.AP(s_z.tensor, b_i * (H * DH * L) + t0,
                                   [[L, 64], [DH * L, 16], [1, 128]]),
                        z_p1[ssi % 2].ap().rearrange("p (h t) -> p h t", h=16), "qg")
                    dma(g, bass.AP(s_bc.tensor, b_i * (H * 2 * DS * L) + t0,
                                   [[L, 16], [2 * DS * L, 16], [1, 128]]),
                        bc_p1[ssi % 2].ap()[0:16, :].rearrange("p (h t) -> p h t", h=16),
                        "qg")
                    dma(g, bass.AP(s_bc.tensor, b_i * (H * 2 * DS * L) + DS * L + t0,
                                   [[L, 16], [2 * DS * L, 16], [1, 128]]),
                        bc_p1[ssi % 2].ap()[16:32, :].rearrange("p (h t) -> p h t", h=16),
                        "qg")
                    g.wait_ge(qg_h, qn["qg"])
    p1_st.close()

    p2_st = contextlib.ExitStack()
    p_sbuf, p_psum = mk(p2_st)
    dt2 = [p_sbuf(f"dt2_{i}", [128, L], F32) for i in range(2)]
    z2 = [p_sbuf(f"z2_{i}", [128, L], F32) for i in range(2)]
    Bb = [p_sbuf(f"Bb{i}", [128, L], BF16) for i in range(4)]
    Cb = [p_sbuf(f"Cb{i}", [128, L], BF16) for i in range(4)]
    a_s = [p_sbuf(f"a_s{i}", [128, L], F32) for i in range(4)]
    u_s = p_sbuf("u_s", [128, L], F32)
    h_s = p_sbuf("h_s", [128, L], F32)
    hc_s = p_sbuf("hc_s", [128, L], F32)
    y_s = [p_sbuf(f"y_s{i}", [128, L], F32) for i in range(2)]
    ybf = [p_sbuf(f"ybf{i}", [128, L], BF16) for i in range(2)]

    def bc_ap(b_i, hp, cb, n):
        off = b_i * (H * 2 * DS * L) + (2 * hp) * (2 * DS * L) + cb * (DS * L) + n * L
        return bass.AP(s_bc.tensor, off, [[2 * DS * L, 2], [0, 64], [1, L]])

    def dtz_ap(tns, b_i, hp):
        off = b_i * (H * DH * L) + (2 * hp) * (DH * L)
        return bass.AP(tns.tensor, off, [[DH * L, 2], [L, 64], [1, L]])

    NIT = NITs
    NPAIR = NIT * 8
    rec_b = {}
    rec_c = {}
    rec_dtz = {}
    rec_y = {}
    rec_a = {}

    def pair_of(j):
        return j // 8, j % 8

    with nc.Block(no_gpsimd_drain=True) as blk:
        @blk.sync
        def _(s):
            dma(s, dt2[0][:], dtz_ap(s_dt, 0, 0), "qs")
            dma(s, z2[0][:], dtz_ap(s_z, 0, 0), "qs")
            for n in (0, 1):
                dma(s, Bb[n % 4][:], bc_ap(0, 0, 0, n), "qs")
            rec_dtz[0] = qn["qs"]
            rec_b[0] = qn["qs"]
        @blk.gpsimd
        def _(g):
            for n in (0, 1):
                dma(g, Cb[n % 4][:], bc_ap(0, 0, 1, n), "qg")
            rec_c[0] = qn["qg"]

    for j in range(NPAIR + 1):
        it_a, pr_a = pair_of(j) if j < NPAIR else (None, None)
        with nc.Block(no_gpsimd_drain=True) as blk:
            @blk.scalar
            def _(sc):
                if j < NPAIR:
                    it, pr = it_a, pr_a
                    if pr == 0:
                        sc.wait_ge(qs_h, rec_dtz[it])
                    for n in (2 * pr, 2 * pr + 1):
                        col = (it % HP) * 16 + n
                        ai = sc.activation(a_s[n % 4][:], dt2[it % 2][:], AF.Exp,
                                           scale=ascale[:, col:col + 1])
                    amark(ai)
                    rec_a[j] = ean[0]
            @blk.vector
            def _(v):
                if j >= 1:
                    jj = j - 1
                    it, pr = pair_of(jj)
                    v.wait_ge(qs_h, rec_b[jj])
                    v.wait_ge(qg_h, rec_c[jj])
                    v.wait_ge(ea_h, rec_a[jj])
                    if pr == 0 and it >= 2 and (it - 2) in rec_y:
                        v.wait_ge(qg_h, rec_y[it - 2])
                    last = None
                    for n in (2 * pr, 2 * pr + 1):
                        v.tensor_mul(u_s[:], z2[it % 2][:], Bb[n % 4][:])
                        v.drain()
                        v.tensor_tensor_scan(h_s[:], a_s[n % 4][:], u_s[:],
                                             0.0, ALU.mult, ALU.add)
                        v.drain()
                        if n == 0:
                            last = v.tensor_mul(y_s[it % 2][:], h_s[:], Cb[n % 4][:])
                        elif n == 15:
                            v.tensor_mul(hc_s[:], h_s[:], Cb[n % 4][:])
                            v.drain()
                            last = v.tensor_add(ybf[it % 2][:], y_s[it % 2][:], hc_s[:])
                        else:
                            v.tensor_mul(hc_s[:], h_s[:], Cb[n % 4][:])
                            v.drain()
                            last = v.tensor_add(y_s[it % 2][:], y_s[it % 2][:], hc_s[:])
                        v.drain()
                    last.then_inc(ev_h, 1)
                    evn[0] += 1
            @blk.sync
            def _(s):
                if j < NPAIR and pr_a == 1 and it_a + 1 < NIT:
                    nb, nh = it_a + 1, 0
                    dma(s, dt2[(it_a + 1) % 2][:],
                        dtz_ap(s_dt, (it_a + 1) // HP, (it_a + 1) % HP), "qs")
                    dma(s, z2[(it_a + 1) % 2][:],
                        dtz_ap(s_z, (it_a + 1) // HP, (it_a + 1) % HP), "qs")
                    rec_dtz[it_a + 1] = qn["qs"]
                jn = j + 1
                if jn < NPAIR:
                    s.wait_ge(ev_h, j)
                    itn, prn = pair_of(jn)
                    for n in (2 * prn, 2 * prn + 1):
                        dma(s, Bb[n % 4][:], bc_ap(itn // HP, itn % HP, 0, n), "qs")
                    rec_b[jn] = qn["qs"]
            @blk.gpsimd
            def _(g):
                if j < NPAIR and pr_a == 1 and it_a >= 1:
                    pit = it_a - 1
                    pb, ph = pit // HP, pit % HP
                    g.wait_ge(ev_h, (pit + 1) * 8)
                    dma(g, bass.AP(s_y.tensor, ph * (128 * TOK) + pb * L,
                                   [[TOK, 128], [1, L]]), ybf[pit % 2][:], "qg")
                    rec_y[pit] = qn["qg"]
                jn = j + 1
                if jn < NPAIR:
                    g.wait_ge(ev_h, j)
                    itn, prn = pair_of(jn)
                    for n in (2 * prn, 2 * prn + 1):
                        dma(g, Cb[n % 4][:], bc_ap(itn // HP, itn % HP, 1, n), "qg")
                    rec_c[jn] = qn["qg"]
    with nc.Block(no_gpsimd_drain=True) as blk:
        @blk.gpsimd
        def _(g):
            pit = NIT - 1
            g.wait_ge(ev_h, NPAIR)
            dma(g, bass.AP(s_y.tensor, (pit % HP) * (128 * TOK) + (pit // HP) * L,
                           [[TOK, 128], [1, L]]), ybf[pit % 2][:], "qg")
            g.wait_ge(qg_h, qn["qg"])
    p2_st.close()

    a1_st = contextlib.ExitStack()
    p_sbuf, p_psum = mk(a1_st)
    tmpB = p_sbuf("tmpB", [128, 2048], F32)
    sqb = p_sbuf("sqb_a1", [128, 4 * D], F32)
    wout_sb = p_sbuf("wout_sb", [128, CH * D], BF16)
    ytl = [p_sbuf(f"ytl{i}", [128, CH * TT], BF16) for i in range(2)]
    xTt = [p_sbuf(f"xTt{i}", [128, NSUB * D], F32) for i in range(2)]
    aout_f = p_sbuf("aout_f", [128, CH * TT], BF16)
    aoutT = p_sbuf("aoutT", [128, NSUB * D], BF16)
    x2T = p_sbuf("x2T", [128, NSUB * D], F32)
    xcT = p_sbuf("xcT", [128, NSUB * D], BF16)
    xqT = p_sbuf("xqT", [128, NSUB * D], BF16)
    xq_f = p_sbuf("xq_f", [128, CH * TT], BF16)
    rgr = p_sbuf("rgr", [8, 128], F32)
    p_wo = [p_psum(f"p_wo{i}", [128, TT], F32) for i in range(4)]
    p3tr = p_psum("p3tr", [128, 2048], BF16)
    p3x = p_psum("p3x", [128, 128], F32)

    for k in range(CH):
        with nc.Block(no_gpsimd_drain=True) as blk:
            @blk.sync
            def _(s):
                dma(s, tmpB[:, 0:1024], woutT_h[k * 128:(k + 1) * 128, :], "qs")
                s.wait_ge(qs_h, qn["qs"])
        with nc.Block(no_gpsimd_drain=True) as blk:
            @blk.vector
            def _(v):
                dmark(v.tensor_copy(wout_sb[:, k * D:(k + 1) * D], tmpB[:, 0:1024]))

    rec_ed_wout = edn[0]
    rec_ytl = {}
    rec_xt = {}
    with nc.Block(no_gpsimd_drain=True) as blk:
        @blk.sync
        def _(s):
            for k in range(CH):
                dma(s, ytl[0][:, k * TT:(k + 1) * TT],
                    bass.AP(s_y.tensor, k * (128 * TOK), [[TOK, 128], [1, TT]]), "qs")
            rec_ytl[0] = qn["qs"]
        @blk.gpsimd
        def _(g):
            for ssi in range(NSUB):
                dma(g, xTt[0][:, ssi * D:(ssi + 1) * D],
                    x_flat[ssi * 128:(ssi + 1) * 128, :], "qg")
            rec_xt[0] = qn["qg"]

    for g_i in range(NTs):
        ycur = ytl[g_i % 2]
        xcurT = xTt[g_i % 2]
        for rnd in range(2):
            with nc.Block(no_gpsimd_drain=True) as blk:
                @blk.tensor
                def _(t):
                    if rnd == 0:
                        t.wait_ge(qs_h, rec_ytl[g_i])
                        t.wait_ge(ed_h, rec_ed_wout)
                    mm = None
                    for dmi in range(4):
                        dm = rnd * 4 + dmi
                        for k in range(CH):
                            mm = t.matmul(p_wo[dmi][:],
                                          wout_sb[:, k * D + dm * 128:k * D + (dm + 1) * 128],
                                          ycur[:, k * TT:(k + 1) * TT],
                                          start=(k == 0), stop=(k == CH - 1))
                    pmark(mm)
                @blk.sync
                def _(s):
                    if rnd == 0 and g_i + 1 < NT:
                        for k in range(CH):
                            dma(s, ytl[(g_i + 1) % 2][:, k * TT:(k + 1) * TT],
                                bass.AP(s_y.tensor, k * (128 * TOK) + (g_i + 1) * TT,
                                        [[TOK, 128], [1, TT]]), "qs")
                        rec_ytl[g_i + 1] = qn["qs"]
                @blk.gpsimd
                def _(g):
                    if rnd == 0 and g_i + 1 < NT:
                        for ssi in range(NSUB):
                            t0 = (g_i + 1) * TT + ssi * 128
                            dma(g, xTt[(g_i + 1) % 2][:, ssi * D:(ssi + 1) * D],
                                x_flat[t0:t0 + 128, :], "qg")
                        rec_xt[g_i + 1] = qn["qg"]
            with nc.Block(no_gpsimd_drain=True) as blk:
                @blk.scalar
                def _(sc):
                    sc.wait_ge(ep_h, epn[0])
                    last = None
                    for dmi in range(4):
                        dm = rnd * 4 + dmi
                        last = sc.activation(aout_f[:, dm * TT:(dm + 1) * TT],
                                             p_wo[dmi][:], AF.Identity,
                                             bias=bout_c[:, dm:dm + 1])
                    amark(last)
        for rnd in range(2):
            subs = (0, 1) if rnd == 0 else (2, 3)
            with nc.Block(no_gpsimd_drain=True) as blk:
                @blk.tensor
                def _(t):
                    t.wait_ge(ea_h, ean[0])
                    tr = None
                    for j, ssi in enumerate(subs):
                        for dm in range(CH):
                            col = (j * 8 + dm) * 128
                            tr = t.transpose(p3tr[:, col:col + 128],
                                             aout_f[:, dm * TT + ssi * 128:dm * TT + (ssi + 1) * 128],
                                             idm[:])
                    pmark(tr)
            with nc.Block(no_gpsimd_drain=True) as blk:
                @blk.scalar
                def _(sc):
                    sc.wait_ge(ep_h, epn[0])
                    last = None
                    for j, ssi in enumerate(subs):
                        last = sc.activation(aoutT[:, ssi * D:(ssi + 1) * D],
                                             p3tr[:, j * D:(j + 1) * D], AF.Copy)
                    amark(last)
        with nc.Block(no_gpsimd_drain=True) as blk:
            @blk.scalar
            def _(sc):
                amark(sc.activation(sqb[:], aoutT[:], AF.Square))
            @blk.vector
            def _(v):
                v.reduce_sum(stat[:, 0:4],
                             aoutT.ap().rearrange("p (s d) -> p s d", s=NSUB), axis=AX.X)
        with nc.Block(no_gpsimd_drain=True) as blk:
            @blk.vector
            def _(v):
                v.wait_ge(ea_h, ean[0])
                v.reduce_sum(stat[:, 4:8],
                             sqb.ap().rearrange("p (s d) -> p s d", s=NSUB), axis=AX.X)
        with nc.Block(no_gpsimd_drain=True) as blk:
            @blk.vector
            def _(v):
                v.tensor_scalar_mul(stat[:, 0:4], stat[:, 0:4], 1.0 / D)
                v.tensor_scalar_mul(stat[:, 4:8], stat[:, 4:8], 1.0 / D)
                v.drain()
                v.tensor_mul(stat[:, 8:12], stat[:, 0:4], stat[:, 0:4])
                v.drain()
                dmark(v.tensor_sub(stat[:, 8:12], stat[:, 4:8], stat[:, 8:12]))
        with nc.Block(no_gpsimd_drain=True) as blk:
            @blk.scalar
            def _(sc):
                sc.wait_ge(ed_h, edn[0])
                lnsrc = bass.AP(stat.ap().tensor, 8, [[40, 128], [0, 128], [1, 4]])
                sc.activation(junk.ap().rearrange("p (r c) -> p r c", c=4), lnsrc,
                              AF.Ln, bias=cc[:, 0:1])
                sc.drain()
                amark(sc.activation(rstdw[:], junk[:], AF.Exp, scale=cc[:, 1:2]))
        with nc.Block(no_gpsimd_drain=True) as blk:
            @blk.vector
            def _(v):
                v.wait_ge(qg_h, rec_xt[g_i])
                v.wait_ge(ea_h, ean[0])
                for ssi in range(NSUB):
                    v.tensor_scalar(aoutT[:, ssi * D:(ssi + 1) * D],
                                    aoutT[:, ssi * D:(ssi + 1) * D],
                                    stat[:, ssi:ssi + 1], rstdw[:, ssi:ssi + 1],
                                    ALU.subtract, ALU.mult)
                v.drain()
                dmark(v.tensor_add(x2T[:], xcurT[:], aoutT[:]))
        with nc.Block(no_gpsimd_drain=True) as blk:
            @blk.gpsimd
            def _(g):
                g.wait_ge(ed_h, edn[0])
                for ssi in range(NSUB):
                    dma(g, s_x2[g_i * NSUB + ssi, :, :], x2T[:, ssi * D:(ssi + 1) * D],
                        "qg")
                g.wait_ge(qg_h, qn["qg"])
            @blk.scalar
            def _(sc):
                sc.wait_ge(ed_h, edn[0])
                amark(sc.activation(sqb[:], x2T[:], AF.Square))
            @blk.vector
            def _(v):
                v.reduce_sum(stat[:, 0:4],
                             x2T.ap().rearrange("p (s d) -> p s d", s=NSUB), axis=AX.X)
        with nc.Block(no_gpsimd_drain=True) as blk:
            @blk.vector
            def _(v):
                v.wait_ge(ea_h, ean[0])
                v.reduce_sum(stat[:, 4:8],
                             sqb.ap().rearrange("p (s d) -> p s d", s=NSUB), axis=AX.X)
        with nc.Block(no_gpsimd_drain=True) as blk:
            @blk.vector
            def _(v):
                v.tensor_scalar_mul(stat[:, 0:4], stat[:, 0:4], 1.0 / D)
                v.tensor_scalar_mul(stat[:, 4:8], stat[:, 4:8], 1.0 / D)
                v.drain()
                v.tensor_mul(stat[:, 8:12], stat[:, 0:4], stat[:, 0:4])
                v.drain()
                dmark(v.tensor_sub(stat[:, 8:12], stat[:, 4:8], stat[:, 8:12]))
        with nc.Block(no_gpsimd_drain=True) as blk:
            @blk.scalar
            def _(sc):
                sc.wait_ge(ed_h, edn[0])
                lnsrc = bass.AP(stat.ap().tensor, 8, [[40, 128], [0, 128], [1, 4]])
                sc.activation(junk.ap().rearrange("p (r c) -> p r c", c=4), lnsrc,
                              AF.Ln, bias=cc[:, 0:1])
                sc.drain()
                amark(sc.activation(rstdw[:], junk[:], AF.Exp, scale=cc[:, 1:2]))
            @blk.vector
            def _(v):
                for ssi in range(NSUB):
                    v.tensor_single_scalar(xcT[:, ssi * D:(ssi + 1) * D],
                                           x2T[:, ssi * D:(ssi + 1) * D],
                                           stat[:, ssi:ssi + 1], ALU.subtract)
        with nc.Block(no_gpsimd_drain=True) as blk:
            @blk.vector
            def _(v):
                v.reduce_max(stat[:, 16:20],
                             xcT.ap().rearrange("p (s d) -> p s d", s=NSUB),
                             axis=AX.X, apply_absolute_value=True)
        with nc.Block(no_gpsimd_drain=True) as blk:
            @blk.vector
            def _(v):
                v.tensor_scalar_max(stat[:, 16:20], stat[:, 16:20], EPS)
        with nc.Block(no_gpsimd_drain=True) as blk:
            @blk.vector
            def _(v):
                v.reciprocal(stat[:, 20:24], stat[:, 16:20])
        with nc.Block(no_gpsimd_drain=True) as blk:
            @blk.vector
            def _(v):
                v.wait_ge(ea_h, ean[0])
                v.tensor_scalar_mul(stat[:, 20:24], stat[:, 20:24], 127.0)
                v.tensor_scalar_mul(stat[:, 16:20], stat[:, 16:20], 1.0 / 127.0)
                v.drain()
                v.wait_ge(ea_h, ean[0])
                v.tensor_mul(stat[:, 16:20], stat[:, 16:20], rstdw[:, 0:4])
                v.drain()
                v.tensor_scalar_mul(stat[:, 24:28], stat[:, 16:20], wq_cols[:, 4:5])
                v.tensor_scalar_mul(stat[:, 28:32], stat[:, 16:20], wq_cols[:, 5:6])
                for ssi in range(NSUB):
                    v.tensor_scalar_mul(sqb[:, ssi * D:(ssi + 1) * D],
                                        xcT[:, ssi * D:(ssi + 1) * D],
                                        stat[:, 20 + ssi:21 + ssi])
                v.drain()
                for hh in range(2):
                    v.tensor_copy(tmpI[:], sqb[:, hh * 2048:(hh + 1) * 2048])
                    v.drain()
                    li2 = v.tensor_copy(xqT[:, hh * 2048:(hh + 1) * 2048], tmpI[:])
                    if hh == 0:
                        v.drain()
                dmark(li2)
        with nc.Block(no_gpsimd_drain=True) as blk:
            @blk.tensor
            def _(t):
                t.wait_ge(ed_h, edn[0])
                pmark(t.transpose(p3x[0:8, :], stat[:, 24:32], idm_f[:]))
        with nc.Block(no_gpsimd_drain=True) as blk:
            @blk.vector
            def _(v):
                v.wait_ge(ep_h, epn[0])
                dmark(v.tensor_copy(rgr[0:8, :], p3x[0:8, :]))
        with nc.Block(no_gpsimd_drain=True) as blk:
            @blk.gpsimd
            def _(g):
                g.wait_ge(ed_h, edn[0])
                dma(g, s_rg[g_i, :, :], rgr[0:8, :], "qg")
                g.wait_ge(qg_h, qn["qg"])
        for rnd in range(2):
            cs = range(4) if rnd == 0 else range(4, 8)
            with nc.Block(no_gpsimd_drain=True) as blk:
                @blk.tensor
                def _(t):
                    t.wait_ge(ed_h, edn[0])
                    tr = None
                    for jc, c in enumerate(cs):
                        for ssi in range(NSUB):
                            col = (jc * 4 + ssi) * 128
                            tr = t.transpose(p3tr[:, col:col + 128],
                                             xqT[:, ssi * D + c * 128:ssi * D + (c + 1) * 128],
                                             idm[:])
                    pmark(tr)
            with nc.Block(no_gpsimd_drain=True) as blk:
                @blk.scalar
                def _(sc):
                    sc.wait_ge(ep_h, epn[0])
                    last = None
                    for jc, c in enumerate(cs):
                        last = sc.activation(xq_f[:, c * TT:(c + 1) * TT],
                                             p3tr[:, jc * TT:(jc + 1) * TT], AF.Copy)
                    amark(last)
        with nc.Block(no_gpsimd_drain=True) as blk:
            @blk.gpsimd
            def _(g):
                g.wait_ge(ea_h, ean[0])
                for c in range(CH):
                    dma(g, bass.AP(s_xq.tensor, c * (128 * TOK) + g_i * TT,
                                   [[TOK, 128], [1, TT]]),
                        xq_f[:, c * TT:(c + 1) * TT], "qg")
                g.wait_ge(qg_h, qn["qg"])
    a1_st.close()

    a2_st = contextlib.ExitStack()
    p_sbuf, p_psum = mk(a2_st)
    HTOK = TOK // 2
    NTH = 2 if SIM_SMALL else NT // 2
    xq_all = p_sbuf("xq_all", [128, CH * HTOK], BF16)
    rgb_all = p_sbuf("rgb_all", [128, HTOK], BF16)
    rub_all = p_sbuf("rub_all", [128, HTOK], BF16)
    tmpA2 = p_sbuf("tmpA2", [128, 1024], F32)
    wslc = [[p_sbuf(f"wslc{w}_{i}", [128, 1024], BF16) for i in range(2)]
            for w in range(2)]
    gat = [p_sbuf(f"gat{i}", [128, 1024], BF16) for i in range(2)]
    gat2 = [p_sbuf(f"gat2_{i}", [128, 1024], BF16) for i in range(2)]
    upv = [p_sbuf(f"upv{i}", [128, 1024], BF16) for i in range(4)]
    guv = [p_sbuf(f"guv{i}", [128, 1024], BF16) for i in range(2)]
    absgu = [p_sbuf(f"absgu{i}", [128, 1024], BF16) for i in range(2)]
    runmx = [p_sbuf(f"runmx{i}", [128, 1024], BF16) for i in range(NTH // 2)]
    trmx = p_sbuf("trmx", [128, 1024], BF16)
    red1 = p_sbuf("red1", [128, 32], F32)
    flmx = p_sbuf("flmx", [32, 128], F32)
    amfin = p_sbuf("amfin", [32, 32], F32)
    amfinT = p_sbuf("amfinT", [32, 32], F32)
    p_g = [p_psum(f"p_g{i}", [128, 1024], F32) for i in range(2)]
    p_u = [p_psum(f"p_u{i}", [128, 1024], F32) for i in range(2)]

    def wslc_ap(tns, m):
        return bass.AP(tns.tensor, m * 128,
                       [[INNER, 128], [128 * INNER, 8], [1, 128]])

    rec_w = {}
    rec_si = {}
    rec_pe = {}
    rec_ga = {}
    rec_gv = {}
    am_store_qg = {}
    rec_ab = {}
    for half in range(1 if SIM_SMALL else 2):
        tb = half * NTH
        with nc.Block(no_gpsimd_drain=True) as blk:
            @blk.sync
            def _(s):
                for c in range(CH):
                    for gg in range(NTH):
                        dma(s, xq_all[:, c * HTOK + gg * TT:c * HTOK + (gg + 1) * TT],
                            bass.AP(s_xq.tensor, c * (128 * TOK) + (tb + gg) * TT,
                                    [[TOK, 128], [1, TT]]), "qs")
                s.wait_ge(qs_h, qn["qs"])
        for gg in range(NTH):
            with nc.Block(no_gpsimd_drain=True) as blk:
                @blk.sync
                def _(s):
                    for ssi in range(NSUB):
                        dma(s, tmpA2[:, ssi * 128:(ssi + 1) * 128],
                            bass.AP(s_rg.tensor, ((tb + gg) * 8 + ssi) * 128,
                                    [[0, 128], [1, 128]]), "qs")
                        dma(s, tmpB2_ap(tmpA2, ssi),
                            bass.AP(s_rg.tensor, ((tb + gg) * 8 + 4 + ssi) * 128,
                                    [[0, 128], [1, 128]]), "qs")
                    s.wait_ge(qs_h, qn["qs"])
            with nc.Block(no_gpsimd_drain=True) as blk:
                @blk.vector
                def _(v):
                    v.tensor_copy(rgb_all[:, gg * TT:(gg + 1) * TT], tmpA2[:, 0:512])
                    v.tensor_copy(rub_all[:, gg * TT:(gg + 1) * TT], tmpA2[:, 512:1024])
        MCHs = 2 if SIM_SMALL else MCH
        NP = MCHs * (NTH // 2)
        def unit(J):
            return J // (NTH // 2), (J % (NTH // 2)) * 2
        with nc.Block(no_gpsimd_drain=True) as blk:
            @blk.sync
            def _(s):
                dma(s, wslc[0][0][:], wslc_ap(s_wgq, 0), "qs")
                dma(s, wslc[1][0][:], wslc_ap(s_wuq, 0), "qs")
                rec_w[(half, 0)] = qn["qs"]
        for J in range(NP + 8):
            with nc.Block(no_gpsimd_drain=True) as blk:
                @blk.sync
                def _(s):
                    if J < NP and J % (NTH // 2) == 0:
                        m = unit(J)[0]
                        if m + 1 < MCH:
                            dma(s, wslc[0][(m + 1) % 2][:], wslc_ap(s_wgq, m + 1), "qs")
                            dma(s, wslc[1][(m + 1) % 2][:], wslc_ap(s_wuq, m + 1), "qs")
                            rec_w[(half, m + 1)] = qn["qs"]
                @blk.tensor
                def _(t):
                    if J < NP:
                        m, gg = unit(J)
                        t.wait_ge(qs_h, rec_w[(half, m)])
                        mm = None
                        for pgp, wsl in ((p_g, wslc[0]), (p_u, wslc[1])):
                            for tt2 in range(2):
                                for k in range(CH):
                                    mm = t.matmul(pgp[J % 2][:, tt2 * TT:(tt2 + 1) * TT],
                                                  wsl[m % 2][:, k * 128:(k + 1) * 128],
                                                  xq_all[:, k * HTOK + (gg + tt2) * TT:
                                                         k * HTOK + (gg + tt2 + 1) * TT],
                                                  start=(k == 0), stop=(k == CH - 1))
                        pmark(mm)
                        rec_pe[J] = epn[0]
                @blk.vector
                def _(v):
                    li = None
                    if 1 <= J < NP + 1:
                        Jp = J - 1
                        v.wait_ge(ep_h, rec_pe[Jp])
                        li = v.tensor_mul(gat[Jp % 2][:], p_g[Jp % 2][:],
                                          rgb_all[:, rgslice(Jp, NTH)])
                        v.tensor_mul(upv[Jp % 4][:], p_u[Jp % 2][:],
                                     rub_all[:, rgslice(Jp, NTH)])
                        rec_ga[Jp] = edn[0] + 1
                    if 3 <= J < NP + 3:
                        Jp = J - 3
                        v.wait_ge(ea_h, rec_si[Jp])
                        li = v.tensor_mul(guv[Jp % 2][:], gat2[Jp % 2][:], upv[Jp % 4][:])
                        rec_gv[Jp] = edn[0] + 1
                    if li is not None:
                        dmark(li)
                    if 5 <= J < NP + 5:
                        Jp = J - 5
                        m, gg = unit(Jp)
                        gi = gg // 2
                        v.wait_ge(ea_h, rec_ab[Jp])
                        if m == 0:
                            v.tensor_copy(runmx[gi][:], absgu[Jp % 2][:])
                        else:
                            v.tensor_max(runmx[gi][:], runmx[gi][:],
                                         absgu[Jp % 2][:])
                @blk.scalar
                def _(sc):
                    if 2 <= J < NP + 2:
                        Jp = J - 2
                        sc.wait_ge(ed_h, rec_ga[Jp])
                        amark(sc.activation(gat2[Jp % 2][:], gat[Jp % 2][:], AF.Silu))
                        rec_si[Jp] = ean[0]
                    if 4 <= J < NP + 4:
                        Jp = J - 4
                        sc.wait_ge(ed_h, rec_gv[Jp])
                        amark(sc.activation(absgu[Jp % 2][:], guv[Jp % 2][:], AF.Abs))
                        rec_ab[Jp] = ean[0]
                @blk.gpsimd
                def _(g):
                    if 4 <= J < NP + 4:
                        Jp = J - 4
                        m, gg = unit(Jp)
                        g.wait_ge(ed_h, rec_gv[Jp])
                        dma(g, bass.AP(s_gu.tensor,
                                       m * (128 * TOK) + (half * NTH + gg) * TT,
                                       [[TOK, 128], [1, 2 * TT]]), guv[Jp % 2][:], "qg")
                        g.wait_ge(qg_h, qn["qg"])
        for gi in range(NTH // 2):
            fi = half * (NTH // 2) + gi
            with nc.Block(no_gpsimd_drain=True) as blk:
                red_mark = edn[0] + 1
                @blk.sync
                def _(s):
                    s.wait_ge(ed_h, red_mark)
                    for a in range(4):
                        dma(s, flmx[:, 32 * a:32 * (a + 1)],
                            red1[32 * a:32 * (a + 1), :], "qs")
                fl_tgt = qn["qs"]
                @blk.vector
                def _(v):
                    v.transpose(trmx[:], runmx[gi][:])
                    v.drain()
                    dmark(v.tensor_reduce(
                        red1[:], trmx.ap().rearrange("p (j v) -> p j v", v=32),
                        axis=AX.X, op=ALU.max))
                    v.wait_ge(qs_h, fl_tgt)
                    if fi >= 1:
                        v.wait_ge(qg_h, am_store_qg[fi - 1])
                    v.tensor_reduce(
                        amfin[:], flmx.ap().rearrange("p (a j) -> p j a", a=4),
                        axis=AX.X, op=ALU.max)
                    v.drain()
                    dmark(v.transpose(amfinT[:], amfin[:]))
                am_mark = edn[0]
                @blk.gpsimd
                def _(g):
                    g.wait_ge(ed_h, am_mark)
                    dma(g, bass.AP(s_am.tensor, (half * NTH + gi * 2) * TT,
                                   [[32, 32], [1, 32]]), amfinT[:], "qg")
                    am_store_qg[fi] = qn["qg"]
                    g.wait_ge(qg_h, qn["qg"])
    a2_st.close()

    b_st = contextlib.ExitStack()
    p_sbuf, p_psum = mk(b_st)
    wd_sb = p_sbuf("wd_sb", [128, MCH * D], BF16)
    gu_all = p_sbuf("gu_all", [128, MCH * TT], BF16)
    guq_all = p_sbuf("guq_all", [128, MCH * TT], BF16)
    rowmx = p_sbuf("rowmx", [1, TT], F32)
    rowmn = p_sbuf("rowmn", [1, TT], F32)
    rowsc = p_sbuf("rowsc", [1, TT], BF16)
    rowrd = p_sbuf("rowrd", [1, TT], BF16)
    scb = p_sbuf("scb", [128, TT], F32)
    rdb = p_sbuf("rdb", [128, TT], F32)
    ffn_f = p_sbuf("ffn_f", [128, CH * TT], BF16)
    ffnT = p_sbuf("ffnT", [128, NSUB * D], BF16)
    x2l = p_sbuf("x2l", [128, NSUB * D], F32)
    outT = p_sbuf("outT", [128, NSUB * D], F32)
    p_dn = [p_psum(f"p_dn{i}", [128, TT], F32) for i in range(4)]
    p3trb = p_psum("p3trb", [128, 2048], BF16)
    p_b2 = p_psum("p_b2", [128, 2 * TT], F32)

    out_flat = out_h.rearrange("b l d -> (b l) d")
    rec_addf = {}
    rec_amq = {}

    with nc.Block(no_gpsimd_drain=True) as blk:
        @blk.sync
        def _(s):
            for k in range(MCH):
                dma(s, wd_sb[:, k * D:(k + 1) * D],
                    bass.AP(s_wdq.tensor, k * 128 * D, [[D, 128], [1, D]]), "qs")
            s.wait_ge(qs_h, qn["qs"])

    for g_i in range(NTs):
        with nc.Block(no_gpsimd_drain=True) as blk:
            @blk.sync
            def _(s):
                for k in range(MCH):
                    dma(s, gu_all[:, k * TT:(k + 1) * TT],
                        bass.AP(s_gu.tensor, k * (128 * TOK) + g_i * TT,
                                [[TOK, 128], [1, TT]]), "qs")
                s.wait_ge(qs_h, qn["qs"])
            @blk.gpsimd
            def _(g):
                if g_i >= 1:
                    g.wait_ge(ed_h, rec_addf[g_i - 1])
                for ssi in range(NSUB):
                    dma(g, x2l[:, ssi * D:(ssi + 1) * D],
                        s_x2[g_i * NSUB + ssi, :, :], "qg")
                dma(g, rowmx[:], bass.AP(s_am.tensor, g_i * TT,
                                         [[TT, 1], [1, TT]]), "qg")
                rec_amq[g_i] = qn["qg"]
                g.wait_ge(qg_h, qn["qg"])
            @blk.vector
            def _(v):
                v.wait_ge(qg_h, rec_amq[g_i])
                v.tensor_scalar_max(rowmx[:], rowmx[:], EPS)
                v.drain()
                v.reciprocal(rowmn[:], rowmx[:])
                v.drain()
                v.tensor_scalar_mul(rowmn[:], rowmn[:], 127.0)
                v.drain()
                v.tensor_copy(rowsc[:], rowmn[:])
                v.tensor_scalar_mul(rowmx[:], rowmx[:], 1.0 / 127.0)
                v.drain()
                v.tensor_scalar_mul(rowmx[:], rowmx[:], wq_cols[0:1, 6:7])
                v.drain()
                dmark(v.tensor_copy(rowrd[:], rowmx[:]))
        with nc.Block(no_gpsimd_drain=True) as blk:
            @blk.tensor
            def _(t):
                t.wait_ge(ed_h, edn[0])
                t.matmul(p_b2[:, 0:TT], onesr[:], rowsc[0:1, :],
                         start=True, stop=True)
                pmark(t.matmul(p_b2[:, TT:2 * TT], onesr[:], rowrd[0:1, :],
                               start=True, stop=True))
            EO0 = eon[0]
            @blk.vector
            def _(v):
                v.wait_ge(ep_h, epn[0])
                v.tensor_copy(scb[:], p_b2[:, 0:TT])
                v.drain()
                dmark(v.tensor_copy(rdb[:], p_b2[:, TT:2 * TT]))
                ED0 = edn[0]
                for q in range(1, 18):
                    if q < 17:
                        v.wait_ge(eo_h, EO0 + min(q, 16))
                        for kk in range(2):
                            v.tensor_copy(tmpI[:, ((q - 1) % 2) * 1024 + kk * TT:
                                               ((q - 1) % 2) * 1024 + (kk + 1) * TT],
                                          outT[:, ((q - 1) % 2) * 2048 + kk * TT:
                                               ((q - 1) % 2) * 2048 + (kk + 1) * TT])
                    if q >= 2:
                        for kk in range(2):
                            k = (q - 2) * 2 + kk
                            v.tensor_copy(guq_all[:, k * TT:(k + 1) * TT],
                                          tmpI[:, ((q - 2) % 2) * 1024 + kk * TT:
                                               ((q - 2) % 2) * 1024 + (kk + 1) * TT])
                        dmarkd(v)
                    elif q == 1:
                        dmarkd(v)
            ED0 = edn[0] - 17
            @blk.gpsimd
            def _(g):
                for q in range(16):
                    g.wait_ge(ed_h, ED0 if q < 2 else ED0 + q - 1)
                    last = None
                    for kk in range(2):
                        k = q * 2 + kk
                        last = g.tensor_tensor(outT[:, (q % 2) * 2048 + kk * TT:
                                                    (q % 2) * 2048 + (kk + 1) * TT],
                                               gu_all[:, k * TT:(k + 1) * TT],
                                               scb[:], ALU.mult)
                    omark(last)
        with nc.Block(no_gpsimd_drain=True) as blk:
            EVD = edn[0]
            @blk.tensor
            def _(t):
                for rnd in range(2):
                    t.wait_ge(ed_h, edn[0] if rnd == 0 else EVD + 1)
                    mm = None
                    for dmi in range(4):
                        dm = rnd * 4 + dmi
                        for k in range(MCH):
                            mm = t.matmul(p_dn[dmi][:],
                                          wd_sb[:, k * D + dm * 128:k * D + (dm + 1) * 128],
                                          guq_all[:, k * TT:(k + 1) * TT],
                                          start=(k == 0), stop=(k == MCH - 1))
                    pmark(mm)
            EPD = epn[0] - 2
            @blk.vector
            def _(v):
                for rnd in range(2):
                    v.wait_ge(ep_h, EPD + 1 + rnd)
                    for dmi in range(4):
                        dm = rnd * 4 + dmi
                        v.tensor_mul(ffn_f[:, dm * TT:(dm + 1) * TT],
                                     p_dn[dmi][:], rdb[:])
                    dmarkd(v)
        with nc.Block(no_gpsimd_drain=True) as blk:
            EAT = ean[0]
            @blk.tensor
            def _(t):
                for rnd in range(2):
                    t.wait_ge(ed_h, edn[0] if rnd == 0 else EVD + 2)
                    if rnd == 1:
                        t.wait_ge(ea_h, EAT + 1)
                    subs = (0, 1) if rnd == 0 else (2, 3)
                    tr = None
                    for j, ssi in enumerate(subs):
                        for dm in range(CH):
                            col = (j * 8 + dm) * 128
                            tr = t.transpose(p3trb[:, col:col + 128],
                                             ffn_f[:, dm * TT + ssi * 128:dm * TT + (ssi + 1) * 128],
                                             idm[:])
                    pmark(tr)
            EPT = epn[0] - 2
            @blk.scalar
            def _(sc):
                for rnd in range(2):
                    sc.wait_ge(ep_h, EPT + 1 + rnd)
                    subs = (0, 1) if rnd == 0 else (2, 3)
                    last = None
                    for j, ssi in enumerate(subs):
                        last = sc.activation(ffnT[:, ssi * D:(ssi + 1) * D],
                                             p3trb[:, j * D:(j + 1) * D], AF.Copy)
                    amark(last)
        with nc.Block(no_gpsimd_drain=True) as blk:
            EA4 = ean[0]
            E4 = edn[0]
            @blk.scalar
            def _(sc):
                amark(sc.activation(gu_all.ap()[:, 0:4 * D], ffnT[:], AF.Square))
                sc.wait_ge(ed_h, E4 + 1)
                lnsrc = bass.AP(stat.ap().tensor, 8, [[40, 128], [0, 128], [1, 4]])
                sc.activation(junk.ap().rearrange("p (r c) -> p r c", c=4), lnsrc,
                              AF.Ln, bias=cc[:, 0:1])
                sc.drain()
                amark(sc.activation(rstdw[:], junk[:], AF.Exp, scale=cc[:, 1:2]))
            @blk.vector
            def _(v):
                v.wait_ge(ea_h, EA4)
                v.reduce_sum(stat[:, 0:4],
                             ffnT.ap().rearrange("p (s d) -> p s d", s=NSUB),
                             axis=AX.X)
                v.wait_ge(ea_h, EA4 + 1)
                v.reduce_sum(stat[:, 4:8],
                             gu_all.ap()[:, 0:4 * D].rearrange("p (s d) -> p s d", s=NSUB),
                             axis=AX.X)
                v.drain()
                v.tensor_scalar_mul(stat[:, 0:4], stat[:, 0:4], 1.0 / D)
                v.tensor_scalar_mul(stat[:, 4:8], stat[:, 4:8], 1.0 / D)
                v.drain()
                v.tensor_mul(stat[:, 8:12], stat[:, 0:4], stat[:, 0:4])
                v.drain()
                dmark(v.tensor_sub(stat[:, 8:12], stat[:, 4:8], stat[:, 8:12]))
                v.wait_ge(ea_h, EA4 + 2)
                for ssi in range(NSUB):
                    v.tensor_scalar(ffnT[:, ssi * D:(ssi + 1) * D],
                                    ffnT[:, ssi * D:(ssi + 1) * D],
                                    stat[:, ssi:ssi + 1], rstdw[:, ssi:ssi + 1],
                                    ALU.subtract, ALU.mult)
                v.drain()
                dmark(v.tensor_add(outT[:], x2l[:], ffnT[:]))
                rec_addf[g_i] = edn[0]
            @blk.gpsimd
            def _(g):
                g.wait_ge(ed_h, rec_addf[g_i])
                for ssi in range(NSUB):
                    t0 = g_i * TT + ssi * 128
                    dma(g, out_flat[t0:t0 + 128, :], outT[:, ssi * D:(ssi + 1) * D],
                        "qg")
                if dbg:
                    dma(g, s_dbg2[g_i, :, :], ffn_f[:], "qg")
                    dma(g, s_dbg3[g_i, :, :], ffnT[:], "qg")
                g.wait_ge(qg_h, qn["qg"])

    with nc.Block(no_gpsimd_drain=True) as blk:
        @blk.gpsimd
        def _(g):
            g.wait_ge(qg_h, qn["qg"])
            g.wait_ge(qs_h, qn["qs"])
        @blk.vector
        def _(v):
            v.wait_ge(qg_h, qn["qg"])
    b_st.close()

    g_st.close()
    return nc


def tmpB2_ap(tmpA2, ssi):
    return tmpA2[:, 512 + ssi * 128:512 + (ssi + 1) * 128]


def rgslice(Jp, NTH):
    gg = (Jp % (NTH // 2)) * 2
    return slice(gg * TT, (gg + 2) * TT)


def kernel(**inputs):
    x = np.ascontiguousarray(inputs["L0_attn_x"])
    A_log = np.asarray(inputs["L0_attn_A_log"])
    Wx = np.ascontiguousarray(inputs["L0_attn_Wx"]).astype(np.float32, copy=False)
    Wdt = np.asarray(inputs["L0_attn_Wdt"])
    bdt = np.asarray(inputs["L0_attn_bdt"])
    Wout = np.asarray(inputs["L0_attn_Wout"])
    bout = np.asarray(inputs["L0_attn_bout"])
    wg = np.asarray(inputs["L0_ffn_w_gate"])
    wu = np.asarray(inputs["L0_ffn_w_up"])
    wd = np.asarray(inputs["L0_ffn_w_down"])

    acols = np.empty((128, 128), dtype=np.float32)
    for hp in range(HP):
        for n in range(16):
            acols[0:64, hp * 16 + n] = A_log[2 * hp, n]
            acols[64:128, hp * 16 + n] = A_log[2 * hp + 1, n]
    in_common = {
        "acols": acols,
        "bdt": np.ascontiguousarray(bdt.reshape(64, 1), dtype=np.float32),
        "wx": Wx,
        "wdtT": np.ascontiguousarray(Wdt.T),
        "wxbcT": np.ascontiguousarray(Wx[64:96, :].T),
        "woutT": np.ascontiguousarray(Wout.T),
        "boutc": np.ascontiguousarray(bout.reshape(CH, 128).T),
        "wgT": np.ascontiguousarray(wg.T),
        "wuT": np.ascontiguousarray(wu.T),
        "wdT": np.ascontiguousarray(wd.T),
        "onesr": np.ones((1, 128), dtype=ml_dtypes.bfloat16),
        "onesc": np.ones((128, 1), dtype=ml_dtypes.bfloat16),
        "idm": np.eye(128, dtype=ml_dtypes.bfloat16),
    }
    nc = build_nc()
    in_maps = [dict(in_common, x=np.ascontiguousarray(x[c * B:(c + 1) * B]))
               for c in range(NCORES)]
    res = run_bass_kernel_spmd(nc, in_maps, core_ids=list(range(NCORES)))
    return np.concatenate([r["out"] for r in res.results], axis=0)


# version 09 (final)
# speedup vs baseline: 2.1829x; 1.0842x over previous
import contextlib
import numpy as np
import ml_dtypes
import concourse.bass as bass
import concourse.mybir as mybir
from concourse.bass_utils import run_bass_kernel_spmd

F32 = mybir.dt.float32
BF16 = mybir.dt.bfloat16
I32 = mybir.dt.int32
AF = mybir.ActivationFunctionType
ALU = mybir.AluOpType
AX = mybir.AxisListType

NCORES = 8
B = 4
L = 2048
D = 1024
H = 16
HP = 8
DH = 64
DS = 16
INNER = 4096
CH = D // 128
MCH = INNER // 128
TT = 512
NT = (B * L) // TT
NSUB = TT // 128
EPS = 1e-5
TOK = B * L


SIM_SMALL = False


def build_nc(dbg=False):
    NTs = 2 if SIM_SMALL else NT
    NITs = 2 if SIM_SMALL else B * HP
    nc = bass.Bass("TRN2", target_bir_lowering=False)

    x_h = nc.dram_tensor("x", [B, L, D], F32, kind="ExternalInput").ap()
    acols_h = nc.dram_tensor("acols", [128, 128], F32, kind="ExternalInput").ap()
    bdt_h = nc.dram_tensor("bdt", [64, 1], F32, kind="ExternalInput").ap()
    wx_h = nc.dram_tensor("wx", [96, 64], F32, kind="ExternalInput").ap()
    wdtT_h = nc.dram_tensor("wdtT", [64, 64], F32, kind="ExternalInput").ap()
    wxbcT_h = nc.dram_tensor("wxbcT", [64, 32], F32, kind="ExternalInput").ap()
    woutT_h = nc.dram_tensor("woutT", [D, D], F32, kind="ExternalInput").ap()
    bout_h = nc.dram_tensor("boutc", [128, CH], F32, kind="ExternalInput").ap()
    wgT_h = nc.dram_tensor("wgT", [D, INNER], F32, kind="ExternalInput").ap()
    wuT_h = nc.dram_tensor("wuT", [D, INNER], F32, kind="ExternalInput").ap()
    wdT_h = nc.dram_tensor("wdT", [INNER, D], F32, kind="ExternalInput").ap()
    onesr_h = nc.dram_tensor("onesr", [1, 128], BF16, kind="ExternalInput").ap()
    onesc_h = nc.dram_tensor("onesc", [128, 1], BF16, kind="ExternalInput").ap()
    idm_h = nc.dram_tensor("idm", [128, 128], BF16, kind="ExternalInput").ap()
    out_h = nc.dram_tensor("out", [B, L, D], F32, kind="ExternalOutput").ap()

    KD = "ExternalOutput" if dbg else "Internal"
    s_dt = nc.dram_tensor("s_dt", [B, H, DH, L], F32, kind=KD).ap()
    s_z = nc.dram_tensor("s_z", [B, H, DH, L], F32, kind=KD).ap()
    s_bc = nc.dram_tensor("s_bc", [B, H, 2, DS, L], BF16, kind=KD).ap()
    s_y = nc.dram_tensor("s_y", [CH, 128, TOK], BF16, kind=KD).ap()
    s_x2 = nc.dram_tensor("s_x2", [TOK // 128, 128, D], F32, kind=KD).ap()
    s_xq = nc.dram_tensor("s_xq", [CH, 128, TOK], BF16, kind=KD).ap()
    s_gu = nc.dram_tensor("s_gu", [MCH, 128, TOK], BF16, kind=KD).ap()
    s_wgq = nc.dram_tensor("s_wgq", [D, INNER], BF16, kind=KD).ap()
    s_wuq = nc.dram_tensor("s_wuq", [D, INNER], BF16, kind=KD).ap()
    s_wdq = nc.dram_tensor("s_wdq", [INNER, D], BF16, kind=KD).ap()
    s_stage = nc.dram_tensor("s_stage", [1, 16], F32, kind=KD).ap()
    s_rg = nc.dram_tensor("s_rg", [NT, 8, 128], F32, kind=KD).ap()
    s_am = nc.dram_tensor("s_am", [1, TOK], F32, kind=KD).ap()
    s_dbg = nc.dram_tensor("s_dbg", [NT, 128, 40], F32, kind=KD).ap()
    s_dbg2 = nc.dram_tensor("s_dbg2", [NT, 128, 4096], BF16, kind=KD).ap()
    s_dbg3 = nc.dram_tensor("s_dbg3", [NT, 128, 4096], BF16, kind=KD).ap()

    g_st = contextlib.ExitStack()

    def mk(stk):
        return (lambda n, s, d: stk.enter_context(nc.sbuf_tensor(n, s, d)),
                lambda n, s, d: stk.enter_context(nc.psum_tensor(n, s, d)))

    sbuf, psum = mk(g_st)

    ascale = sbuf("ascale", [128, 128], F32)
    bdt_c = sbuf("bdt_c", [64, 1], F32)
    lhsT_f = sbuf("lhsT_f", [64, 96], F32)
    lhsT_b = sbuf("lhsT_b", [64, 96], BF16)
    bout_c = sbuf("bout_c", [128, CH], F32)
    onesr = sbuf("onesr_s", [1, 128], BF16)
    onesc = sbuf("onesc_s", [128, 1], BF16)
    idm = sbuf("idm_s", [128, 128], BF16)
    idm_f = sbuf("idmf_s", [128, 128], F32)
    wq_cols = sbuf("wq_cols", [128, 8], F32)
    stat = sbuf("stat", [128, 40], F32)
    tmpI = sbuf("tmpI", [128, 2 * D], I32)
    cc = sbuf("cc", [128, 4], F32)
    junk = sbuf("junk", [128, 512], F32)
    rstdw = sbuf("rstdw", [128, 512], F32)

    qs_h = g_st.enter_context(nc.semaphore("qs"))
    qg_h = g_st.enter_context(nc.semaphore("qg"))
    ev_h = g_st.enter_context(nc.semaphore("ev"))
    ea_h = g_st.enter_context(nc.semaphore("ea"))
    eo_h = g_st.enter_context(nc.semaphore("eo"))
    ep_h = g_st.enter_context(nc.semaphore("ep"))
    ed_h = g_st.enter_context(nc.semaphore("ed"))
    qn = {"qs": 0, "qg": 0}
    evn = [0]
    ean = [0]
    eon = [0]
    epn = [0]
    edn = [0]

    def dmark(inst):
        inst.then_inc(ed_h, 1)
        edn[0] += 1

    def dmarkd(eng):
        eng.drain().then_inc(ed_h, 1)
        edn[0] += 1

    def pmark(inst):
        inst.then_inc(ep_h, 1)
        epn[0] += 1

    def amark(inst):
        inst.then_inc(ea_h, 1)
        ean[0] += 1

    def omark(inst):
        inst.then_inc(eo_h, 1)
        eon[0] += 1

    def dma(eng, dst, src, q):
        h = qs_h if q == "qs" else qg_h
        eng.wait_ge(h, qn[q])
        eng.dma_start(out=dst, in_=src).then_inc(h, 16)
        qn[q] += 16

    su_st = contextlib.ExitStack()
    s_sbuf, s_psum = mk(su_st)
    tmpA = s_sbuf("tmpA", [128, 2048], F32)
    tmpC = s_sbuf("tmpC", [128, 2048], BF16)
    tmpC2 = s_sbuf("tmpC2", [128, 2048], BF16)
    wld = [s_sbuf(f"wld{i}", [128, 2048], F32) for i in range(2)]
    wabs = s_sbuf("wabs", [128, 2048], F32)
    onesc32 = s_sbuf("onesc32", [128, 1], F32)
    p_small = s_psum("p_small", [96, 512], F32)
    p_row = s_psum("p_row", [1, 512], F32)

    with nc.Block(no_gpsimd_drain=True) as blk:
        @blk.sync
        def _(s):
            dma(s, ascale[:], acols_h[:], "qs")
            dma(s, bdt_c[:], bdt_h[:], "qs")
            dma(s, tmpA[0:96, 0:64], wx_h[:], "qs")
            dma(s, tmpA[0:64, 64:128], wdtT_h[:], "qs")
            dma(s, lhsT_f[:, 64:96], wxbcT_h[:], "qs")
            dma(s, bout_c[:], bout_h[:], "qs")
            dma(s, onesr[:], onesr_h[:], "qs")
            dma(s, onesc[:], onesc_h[:], "qs")
            dma(s, idm[:], idm_h[:], "qs")
            s.wait_ge(qs_h, qn["qs"])
        @blk.gpsimd
        def _(g):
            g.memset(cc[:, 0:1], EPS)
            g.memset(cc[:, 1:2], -0.5)
            g.memset(cc[:, 2:3], -1.0)
            g.memset(junk[:], 0.0)
            g.memset(stat[:], 0.0)
            omark(g.memset(rstdw[:], 0.0))
    with nc.Block(no_gpsimd_drain=True) as blk:
        @blk.scalar
        def _(sc):
            sc.wait_ge(eo_h, eon[0])
            sc.activation(tmpA[0:128, 128:256], ascale[:], AF.Exp)
            amark(sc.activation(junk[:], junk[:], AF.Copy))
        @blk.tensor
        def _(t):
            pmark(t.matmul(p_small[0:64, 0:64], tmpA[0:64, 0:64], tmpA[0:64, 64:128],
                           start=True, stop=True))
    with nc.Block(no_gpsimd_drain=True) as blk:
        @blk.vector
        def _(v):
            v.wait_ge(ea_h, ean[0])
            v.wait_ge(ep_h, epn[0])
            v.tensor_scalar_mul(ascale[:], tmpA[0:128, 128:256], -1.0)
            v.tensor_copy(lhsT_f[:, 0:64], p_small[0:64, 0:64])
    with nc.Block(no_gpsimd_drain=True) as blk:
        @blk.vector
        def _(v):
            v.tensor_copy(lhsT_b[:], lhsT_f[:])
            v.tensor_copy(onesc32[:], onesc[:])
            dmark(v.tensor_copy(idm_f[:], idm[:]))

    w_list = [(wgT_h, s_wgq, D, INNER, 0), (wuT_h, s_wuq, D, INNER, 1),
              (wdT_h, s_wdq, INNER, D, 2)]
    for (src, _d, R, Cc, idx) in w_list:
        CW = min(2048, Cc)
        cpr = Cc // CW
        ntile = (R // 128) * cpr
        if SIM_SMALL:
            ntile = 2
        njr = CW // 512
        for it in range(ntile + 1):
            with nc.Block(no_gpsimd_drain=True) as blk:
                @blk.sync
                def _(s):
                    if it < ntile:
                        r, c = it // cpr, it % cpr
                        dma(s, wld[it % 2][:, 0:CW],
                            src[r * 128:(r + 1) * 128, c * CW:(c + 1) * CW], "qs")
                    s.wait_ge(qs_h, qn["qs"])
                @blk.scalar
                def _(sc):
                    if it >= 1:
                        amark(sc.activation(wabs[:, 0:CW], wld[(it - 1) % 2][:, 0:CW],
                                            AF.Abs))
            if it >= 1:
                with nc.Block(no_gpsimd_drain=True) as blk:
                    @blk.tensor
                    def _(t):
                        t.wait_ge(ea_h, ean[0])
                        mm = None
                        for j in range(njr):
                            mm = t.matmul(p_row[0:1, 0:512], onesc32[:],
                                          wabs[:, j * 512:(j + 1) * 512],
                                          start=(it == 1 and j == 0),
                                          stop=(it == ntile and j == njr - 1))
                        if it == ntile:
                            pmark(mm)
        with nc.Block(no_gpsimd_drain=True) as blk:
            @blk.vector
            def _(v):
                v.wait_ge(ep_h, epn[0])
                v.reduce_sum(tmpA[0:1, 0:1], p_row[0:1, 0:512], axis=AX.X)
        with nc.Block(no_gpsimd_drain=True) as blk:
            @blk.vector
            def _(v):
                v.tensor_scalar(tmpA[0:1, 1:2], tmpA[0:1, 0:1],
                                1.0 / (R * Cc), EPS, ALU.mult, ALU.max)
        with nc.Block(no_gpsimd_drain=True) as blk:
            @blk.vector
            def _(v):
                dmark(v.reciprocal(tmpA[0:1, 2:3], tmpA[0:1, 1:2]))
        with nc.Block(no_gpsimd_drain=True) as blk:
            @blk.gpsimd
            def _(g):
                g.wait_ge(ed_h, edn[0])
                dma(g, s_stage[0:1, idx * 2:idx * 2 + 2], tmpA[0:1, 1:3], "qg")
                g.wait_ge(qg_h, qn["qg"])
    with nc.Block(no_gpsimd_drain=True) as blk:
        @blk.sync
        def _(s):
            for idx in range(3):
                dma(s, wq_cols[:, 4 + idx:5 + idx],
                    bass.AP(s_stage.tensor, idx * 2, [[0, 128], [1, 1]]), "qs")
                dma(s, wq_cols[:, idx:idx + 1],
                    bass.AP(s_stage.tensor, idx * 2 + 1, [[0, 128], [1, 1]]), "qs")
            s.wait_ge(qs_h, qn["qs"])

    for (src, dst, R, Cc, idx) in w_list:
        CW = min(2048, Cc)
        cpr = Cc // CW
        ntile = (R // 128) * cpr
        if SIM_SMALL:
            ntile = 2
        for it in range(ntile + 2):
            with nc.Block(no_gpsimd_drain=True) as blk:
                @blk.sync
                def _(s):
                    if it < ntile:
                        r, c = it // cpr, it % cpr
                        dma(s, wld[it % 2][:, 0:CW],
                            src[r * 128:(r + 1) * 128, c * CW:(c + 1) * CW], "qs")
                    s.wait_ge(qs_h, qn["qs"])
                @blk.vector
                def _(v):
                    if 1 <= it <= ntile:
                        j = it - 1
                        qb = tmpC if j % 2 == 0 else tmpC2
                        v.tensor_scalar_mul(tmpA[:, 0:CW], wld[j % 2][:, 0:CW],
                                            wq_cols[:, idx:idx + 1])
                        v.drain()
                        v.tensor_copy(tmpI[:, 0:CW], tmpA[:, 0:CW])
                        v.drain()
                        v.tensor_scalar(tmpI[:, 0:CW], tmpI[:, 0:CW], 1, -1,
                                        ALU.min, ALU.max)
                        v.drain()
                        dmark(v.tensor_copy(qb[:, 0:CW], tmpI[:, 0:CW]))
                @blk.gpsimd
                def _(g):
                    if it >= 2:
                        g.wait_ge(ed_h, edn[0])
                        j = it - 2
                        r, c = j // cpr, j % cpr
                        qb = tmpC if j % 2 == 0 else tmpC2
                        dma(g, dst[r * 128:(r + 1) * 128, c * CW:(c + 1) * CW],
                            qb[:, 0:CW], "qg")
                        g.wait_ge(qg_h, qn["qg"])
    su_st.close()

    p1_st = contextlib.ExitStack()
    p_sbuf, p_psum = mk(p1_st)
    xt = [p_sbuf(f"xt{i}", [128, NSUB * D], F32) for i in range(2)]
    sqb = p_sbuf("sqb", [128, 4 * D], F32)
    xn1T = p_sbuf("xn1T", [128, NSUB * D], BF16)
    xf = p_sbuf("xf", [64, NSUB * 2048], BF16)
    dt_p1 = [p_sbuf(f"dt_p1{i}", [64, 2048], F32) for i in range(2)]
    z_p1 = [p_sbuf(f"z_p1{i}", [64, 2048], F32) for i in range(2)]
    bc_p1 = [p_sbuf(f"bc_p1{i}", [32, 2048], BF16) for i in range(2)]
    sfp = [p_sbuf(f"sfp{i}", [64, 2048], F32) for i in range(3)]
    p_trA = p_psum("p_trA", [128, 2048], BF16)
    p_trB = p_psum("p_trB", [128, 2048], BF16)
    p_pj = p_psum("p_pj", [96, 2048], F32)

    x_flat = x_h.rearrange("b l d -> (b l) d")

    for g_i in range(NTs):
        with nc.Block(no_gpsimd_drain=True) as blk:
            @blk.sync
            def _(s):
                for ssi in range(NSUB):
                    t0 = g_i * TT + ssi * 128
                    dma(s, xt[g_i % 2][:, ssi * D:(ssi + 1) * D],
                        x_flat[t0:t0 + 128, :], "qs")
                s.wait_ge(qs_h, qn["qs"])
        xcur = xt[g_i % 2]
        b_i = g_i // 4
        tbase = (g_i % 4) * TT
        with nc.Block(no_gpsimd_drain=True) as blk:
            @blk.scalar
            def _(sc):
                amark(sc.activation(sqb[:], xcur[:], AF.Square))
            @blk.vector
            def _(v):
                v.reduce_sum(stat[:, 0:4],
                             xcur.ap().rearrange("p (s d) -> p s d", s=NSUB),
                             axis=AX.X)
        with nc.Block(no_gpsimd_drain=True) as blk:
            @blk.vector
            def _(v):
                v.wait_ge(ea_h, ean[0])
                v.reduce_sum(stat[:, 4:8],
                             sqb.ap().rearrange("p (s d) -> p s d", s=NSUB),
                             axis=AX.X)
        with nc.Block(no_gpsimd_drain=True) as blk:
            @blk.vector
            def _(v):
                v.tensor_scalar_mul(stat[:, 0:4], stat[:, 0:4], 1.0 / D)
                v.tensor_scalar_mul(stat[:, 4:8], stat[:, 4:8], 1.0 / D)
                v.drain()
                v.tensor_mul(stat[:, 8:12], stat[:, 0:4], stat[:, 0:4])
                v.drain()
                dmark(v.tensor_sub(stat[:, 8:12], stat[:, 4:8], stat[:, 8:12]))
        with nc.Block(no_gpsimd_drain=True) as blk:
            @blk.scalar
            def _(sc):
                sc.wait_ge(ed_h, edn[0])
                lnsrc = bass.AP(stat.ap().tensor, 8, [[40, 128], [0, 128], [1, 4]])
                sc.activation(junk.ap().rearrange("p (r c) -> p r c", c=4), lnsrc,
                              AF.Ln, bias=cc[:, 0:1])
                sc.drain()
                amark(sc.activation(rstdw[:], junk[:], AF.Exp, scale=cc[:, 1:2]))
        with nc.Block(no_gpsimd_drain=True) as blk:
            @blk.vector
            def _(v):
                v.wait_ge(ea_h, ean[0])
                li = None
                for ssi in range(NSUB):
                    li = v.tensor_scalar(xn1T[:, ssi * D:(ssi + 1) * D],
                                         xcur[:, ssi * D:(ssi + 1) * D],
                                         stat[:, ssi:ssi + 1], rstdw[:, ssi:ssi + 1],
                                         ALU.subtract, ALU.mult)
                dmark(li)
            @blk.gpsimd
            def _(g):
                dma(g, s_dbg[g_i, :, 0:16], stat[:, 0:16], "qg")
                g.wait_ge(qg_h, qn["qg"])
        for half in range(2):
            subs = (0, 1) if half == 0 else (2, 3)
            with nc.Block(no_gpsimd_drain=True) as blk:
                @blk.tensor
                def _(t):
                    t.wait_ge(ed_h, edn[0])
                    tr = None
                    for j, ssi in enumerate(subs):
                        for c in range(CH):
                            pt = p_trA if c < 4 else p_trB
                            col = (j * 4 + (c % 4)) * 128
                            tr = t.transpose(pt[:, col:col + 128],
                                             xn1T[:, ssi * D + c * 128:ssi * D + (c + 1) * 128],
                                             idm[:])
                    pmark(tr)
            with nc.Block(no_gpsimd_drain=True) as blk:
                @blk.scalar
                def _(sc):
                    sc.wait_ge(ep_h, epn[0])
                    for j, ssi in enumerate(subs):
                        for lo, pt in ((0, p_trA), (4, p_trB)):
                            src0 = pt.ap()[0:64, j * 512:(j + 1) * 512]\
                                .rearrange("p (c t) -> p c t", c=4)
                            src1 = pt.ap()[64:128, j * 512:(j + 1) * 512]\
                                .rearrange("p (c t) -> p c t", c=4)
                            dsts = xf.ap()[0:64, ssi * 2048:(ssi + 1) * 2048]\
                                .rearrange("p (h t) -> p h t", h=16)
                            sc.activation(dsts[:, 2 * lo:2 * lo + 7:2, :], src0, AF.Copy)
                            amark(sc.activation(dsts[:, 2 * lo + 1:2 * lo + 8:2, :],
                                                src1, AF.Copy))
        for ssi in range(NSUB):
            with nc.Block(no_gpsimd_drain=True) as blk:
                @blk.tensor
                def _(t):
                    t.wait_ge(ea_h, ean[0])
                    mm = None
                    for h_i in range(16):
                        mm = t.matmul(p_pj[:, h_i * 128:(h_i + 1) * 128], lhsT_b[:],
                                      xf[0:64, ssi * 2048 + h_i * 128:ssi * 2048 + (h_i + 1) * 128],
                                      start=True, stop=True)
                    pmark(mm)
            with nc.Block(no_gpsimd_drain=True) as blk:
                @blk.scalar
                def _(sc):
                    sc.wait_ge(ep_h, epn[0])
                    sc.activation(sfp[0][:], p_pj[0:64, :], AF.Abs, bias=bdt_c[:])
                @blk.vector
                def _(v):
                    v.wait_ge(ep_h, epn[0])
                    v.tensor_scalar(sfp[1][:], p_pj[0:64, :], bdt_c[:], 0.0,
                                    ALU.add, ALU.max)
                    dmark(v.tensor_copy(bc_p1[ssi % 2][:], p_pj[64:96, :]))
            with nc.Block(no_gpsimd_drain=True) as blk:
                @blk.scalar
                def _(sc):
                    sc.activation(sfp[2][:], sfp[0][:], AF.Exp, scale=cc[0:64, 2:3])
                    sc.drain()
                    amark(sc.activation(sfp[0][:], sfp[2][:], AF.Ln, bias=1.0))
            with nc.Block(no_gpsimd_drain=True) as blk:
                @blk.vector
                def _(v):
                    v.wait_ge(ea_h, ean[0])
                    v.tensor_add(dt_p1[ssi % 2][:], sfp[0][:], sfp[1][:])
                    v.drain()
                    dmark(v.tensor_mul(z_p1[ssi % 2][:], dt_p1[ssi % 2][:],
                                       xf[0:64, ssi * 2048:(ssi + 1) * 2048]))
            with nc.Block(no_gpsimd_drain=True) as blk:
                @blk.gpsimd
                def _(g):
                    g.wait_ge(ed_h, edn[0])
                    t0 = tbase + ssi * 128
                    dma(g, bass.AP(s_dt.tensor, b_i * (H * DH * L) + t0,
                                   [[L, 64], [DH * L, 16], [1, 128]]),
                        dt_p1[ssi % 2].ap().rearrange("p (h t) -> p h t", h=16), "qg")
                    dma(g, bass.AP(s_z.tensor, b_i * (H * DH * L) + t0,
                                   [[L, 64], [DH * L, 16], [1, 128]]),
                        z_p1[ssi % 2].ap().rearrange("p (h t) -> p h t", h=16), "qg")
                    dma(g, bass.AP(s_bc.tensor, b_i * (H * 2 * DS * L) + t0,
                                   [[L, 16], [2 * DS * L, 16], [1, 128]]),
                        bc_p1[ssi % 2].ap()[0:16, :].rearrange("p (h t) -> p h t", h=16),
                        "qg")
                    dma(g, bass.AP(s_bc.tensor, b_i * (H * 2 * DS * L) + DS * L + t0,
                                   [[L, 16], [2 * DS * L, 16], [1, 128]]),
                        bc_p1[ssi % 2].ap()[16:32, :].rearrange("p (h t) -> p h t", h=16),
                        "qg")
                    g.wait_ge(qg_h, qn["qg"])
    p1_st.close()

    p2_st = contextlib.ExitStack()
    p_sbuf, p_psum = mk(p2_st)
    dt2 = [p_sbuf(f"dt2_{i}", [128, L], F32) for i in range(2)]
    z2 = [p_sbuf(f"z2_{i}", [128, L], F32) for i in range(2)]
    Bb = [p_sbuf(f"Bb{i}", [128, L], BF16) for i in range(4)]
    Cb = [p_sbuf(f"Cb{i}", [128, L], BF16) for i in range(6)]
    a_s = [p_sbuf(f"a_s{i}", [128, L], F32) for i in range(4)]
    u_s = [(p_psum if i == 0 else p_sbuf)(f"u_s{i}", [128, L], F32)
           for i in range(4)]
    h_s = [p_sbuf(f"h_s{i}", [128, L], F32) for i in range(4)]
    hc_s = [p_sbuf(f"hc_s{i}", [128, L], F32) for i in range(2)]
    y_e = p_psum("y_e", [128, L], F32)
    y_o = p_sbuf("y_o", [128, L], F32)
    ybf = [p_sbuf(f"ybf{i}", [128, L], BF16) for i in range(2)]

    def bc_ap(b_i, hp, cb, n):
        off = b_i * (H * 2 * DS * L) + (2 * hp) * (2 * DS * L) + cb * (DS * L) + n * L
        return bass.AP(s_bc.tensor, off, [[2 * DS * L, 2], [0, 64], [1, L]])

    def dtz_ap(tns, b_i, hp):
        off = b_i * (H * DH * L) + (2 * hp) * (DH * L)
        return bass.AP(tns.tensor, off, [[DH * L, 2], [L, 64], [1, L]])

    NIT = NITs
    NPAIR = NIT * 8
    rec_b = {}
    rec_c = {}
    rec_dtz = {}
    rec_y = {}
    rec_a = {}
    rec_h = {}
    rec_hc2 = {}
    rec_yb = {}
    rec_ev = {}

    def pair_of(j):
        return j // 8, j % 8

    with nc.Block(no_gpsimd_drain=True) as blk:
        @blk.sync
        def _(s):
            dma(s, dt2[0][:], dtz_ap(s_dt, 0, 0), "qs")
            dma(s, z2[0][:], dtz_ap(s_z, 0, 0), "qs")
            for n in (0, 1):
                dma(s, Bb[n % 4][:], bc_ap(0, 0, 0, n), "qs")
            rec_dtz[0] = qn["qs"]
            rec_b[0] = qn["qs"]
        @blk.gpsimd
        def _(g):
            for n in (0, 1):
                dma(g, Cb[n % 4][:], bc_ap(0, 0, 1, n), "qg")
            rec_c[0] = qn["qg"]

    for j in range(NPAIR + 2):
        with nc.Block(no_gpsimd_drain=True) as blk:
            @blk.scalar
            def _(sc):
                if j < NPAIR:
                    it, pr = pair_of(j)
                    if pr == 0:
                        sc.wait_ge(qs_h, rec_dtz[it])
                    ai = None
                    for n in (2 * pr, 2 * pr + 1):
                        col = (it % HP) * 16 + n
                        ai = sc.activation(a_s[n % 4][:], dt2[it % 2][:], AF.Exp,
                                           scale=ascale[:, col:col + 1])
                    amark(ai)
                    rec_a[j] = ean[0]
            @blk.vector
            def _(v):
                last = None
                merge_only = True
                if j < NPAIR:
                    it, pr = pair_of(j)
                    v.wait_ge(qs_h, rec_b[j])
                    if pr == 0:
                        v.wait_ge(qs_h, rec_dtz[it])
                    for n in (2 * pr, 2 * pr + 1):
                        last = v.tensor_mul(u_s[n % 4][:], z2[it % 2][:],
                                            Bb[n % 4][:])
                        merge_only = False
                    last.then_inc(ev_h, 1)
                    evn[0] += 1
                if 1 <= j <= NPAIR:
                    p = j - 1
                    it, pr = pair_of(p)
                    v.wait_ge(ea_h, rec_a[p])
                    if (p - 2, 1) in rec_hc2:
                        v.wait_ge(eo_h, rec_hc2[(p - 2, 1)])
                    for n in (2 * pr, 2 * pr + 1):
                        last = v.tensor_tensor_scan(h_s[n % 4][:], a_s[n % 4][:],
                                                    u_s[n % 4][:], 0.0,
                                                    ALU.mult, ALU.add)
                        merge_only = False
                    dmark(last)
                    rec_h[p] = edn[0]
                if 2 <= j <= NPAIR + 1:
                    p = j - 2
                    it, pr = pair_of(p)
                    if pr == 0:
                        v.wait_ge(eo_h, eon[0] + 1)
                        last = v.tensor_copy(y_e[:], hc_s[0][:])
                        v.wait_ge(eo_h, eon[0] + 2)
                        last = v.tensor_copy(y_o[:], hc_s[1][:])
                        merge_only = False
                    else:
                        v.wait_ge(eo_h, eon[0] + 1)
                        last = v.tensor_add(y_e[:], y_e[:], hc_s[0][:])
                        v.wait_ge(eo_h, eon[0] + 2)
                        last = v.tensor_add(y_o[:], y_o[:], hc_s[1][:])
                        merge_only = False
                    if pr == 7:
                        v.drain()
                        if it >= 2:
                            v.wait_ge(qg_h, rec_y[it - 2])
                        last = v.tensor_add(ybf[it % 2][:], y_e[:], y_o[:])
                        dmark(last)
                        rec_yb[it] = edn[0]
                rec_ev[j] = evn[0]
            @blk.sync
            def _(s):
                if j < NPAIR:
                    it_a, pr_a = pair_of(j)
                    if pr_a == 1 and it_a + 1 < NIT:
                        dma(s, dt2[(it_a + 1) % 2][:],
                            dtz_ap(s_dt, (it_a + 1) // HP, (it_a + 1) % HP), "qs")
                        dma(s, z2[(it_a + 1) % 2][:],
                            dtz_ap(s_z, (it_a + 1) // HP, (it_a + 1) % HP), "qs")
                        rec_dtz[it_a + 1] = qn["qs"]
                jn = j + 1
                if jn < NPAIR:
                    if j >= 1:
                        s.wait_ge(ev_h, rec_ev[j - 1])
                    itn, prn = pair_of(jn)
                    for n in (2 * prn, 2 * prn + 1):
                        dma(s, Bb[n % 4][:], bc_ap(itn // HP, itn % HP, 0, n), "qs")
                    rec_b[jn] = qn["qs"]
            @blk.gpsimd
            def _(g):
                if 2 <= j <= NPAIR + 1:
                    p = j - 2
                    it, pr = pair_of(p)
                    g.wait_ge(ed_h, rec_h[p])
                    g.wait_ge(qg_h, rec_c[p])
                    for i, n in enumerate((2 * pr, 2 * pr + 1)):
                        omark(g.tensor_tensor(hc_s[i][:], h_s[n % 4][:],
                                              Cb[(2 * p + i) % 6][:], ALU.mult))
                        rec_hc2[(p, i)] = eon[0]
                jn = j + 1
                if jn < NPAIR:
                    if (jn - 3, 1) in rec_hc2:
                        g.wait_ge(eo_h, rec_hc2[(jn - 3, 1)])
                    itn, prn = pair_of(jn)
                    for i, n in enumerate((2 * prn, 2 * prn + 1)):
                        dma(g, Cb[(2 * jn + i) % 6][:],
                            bc_ap(itn // HP, itn % HP, 1, n), "qg")
                    rec_c[jn] = qn["qg"]
                if 2 <= j <= NPAIR + 1:
                    p = j - 2
                    it, pr = pair_of(p)
                    if pr == 7:
                        g.wait_ge(ed_h, rec_yb[it])
                        dma(g, bass.AP(s_y.tensor,
                                       (it % HP) * (128 * TOK) + (it // HP) * L,
                                       [[TOK, 128], [1, L]]), ybf[it % 2][:], "qg")
                        rec_y[it] = qn["qg"]
    with nc.Block(no_gpsimd_drain=True) as blk:
        @blk.gpsimd
        def _(g):
            g.wait_ge(qg_h, qn["qg"])
    p2_st.close()

    a1_st = contextlib.ExitStack()
    p_sbuf, p_psum = mk(a1_st)
    tmpB = p_sbuf("tmpB", [128, 2048], F32)
    sqb = p_sbuf("sqb_a1", [128, 4 * D], F32)
    wout_sb = p_sbuf("wout_sb", [128, CH * D], BF16)
    ytl = [p_sbuf(f"ytl{i}", [128, CH * TT], BF16) for i in range(2)]
    xTt = [p_sbuf(f"xTt{i}", [128, NSUB * D], F32) for i in range(2)]
    aout_f = p_sbuf("aout_f", [128, CH * TT], BF16)
    aoutT = p_sbuf("aoutT", [128, NSUB * D], BF16)
    x2T = p_sbuf("x2T", [128, NSUB * D], F32)
    xcT = p_sbuf("xcT", [128, NSUB * D], BF16)
    xqT = p_sbuf("xqT", [128, NSUB * D], BF16)
    xq_f = p_sbuf("xq_f", [128, CH * TT], BF16)
    rgr = p_sbuf("rgr", [8, 128], F32)
    p_wo = [p_psum(f"p_wo{i}", [128, TT], F32) for i in range(4)]
    p3tr = p_psum("p3tr", [128, 2048], BF16)
    p3x = p_psum("p3x", [128, 128], F32)

    for k in range(CH):
        with nc.Block(no_gpsimd_drain=True) as blk:
            @blk.sync
            def _(s):
                dma(s, tmpB[:, 0:1024], woutT_h[k * 128:(k + 1) * 128, :], "qs")
                s.wait_ge(qs_h, qn["qs"])
        with nc.Block(no_gpsimd_drain=True) as blk:
            @blk.vector
            def _(v):
                dmark(v.tensor_copy(wout_sb[:, k * D:(k + 1) * D], tmpB[:, 0:1024]))

    rec_ed_wout = edn[0]
    rec_ytl = {}
    rec_xt = {}
    with nc.Block(no_gpsimd_drain=True) as blk:
        @blk.sync
        def _(s):
            for k in range(CH):
                dma(s, ytl[0][:, k * TT:(k + 1) * TT],
                    bass.AP(s_y.tensor, k * (128 * TOK), [[TOK, 128], [1, TT]]), "qs")
            rec_ytl[0] = qn["qs"]
        @blk.gpsimd
        def _(g):
            for ssi in range(NSUB):
                dma(g, xTt[0][:, ssi * D:(ssi + 1) * D],
                    x_flat[ssi * 128:(ssi + 1) * 128, :], "qg")
            rec_xt[0] = qn["qg"]

    for g_i in range(NTs):
        ycur = ytl[g_i % 2]
        xcurT = xTt[g_i % 2]
        for rnd in range(2):
            with nc.Block(no_gpsimd_drain=True) as blk:
                @blk.tensor
                def _(t):
                    if rnd == 0:
                        t.wait_ge(qs_h, rec_ytl[g_i])
                        t.wait_ge(ed_h, rec_ed_wout)
                    mm = None
                    for dmi in range(4):
                        dm = rnd * 4 + dmi
                        for k in range(CH):
                            mm = t.matmul(p_wo[dmi][:],
                                          wout_sb[:, k * D + dm * 128:k * D + (dm + 1) * 128],
                                          ycur[:, k * TT:(k + 1) * TT],
                                          start=(k == 0), stop=(k == CH - 1))
                    pmark(mm)
                @blk.sync
                def _(s):
                    if rnd == 0 and g_i + 1 < NT:
                        for k in range(CH):
                            dma(s, ytl[(g_i + 1) % 2][:, k * TT:(k + 1) * TT],
                                bass.AP(s_y.tensor, k * (128 * TOK) + (g_i + 1) * TT,
                                        [[TOK, 128], [1, TT]]), "qs")
                        rec_ytl[g_i + 1] = qn["qs"]
                @blk.gpsimd
                def _(g):
                    if rnd == 0 and g_i + 1 < NT:
                        for ssi in range(NSUB):
                            t0 = (g_i + 1) * TT + ssi * 128
                            dma(g, xTt[(g_i + 1) % 2][:, ssi * D:(ssi + 1) * D],
                                x_flat[t0:t0 + 128, :], "qg")
                        rec_xt[g_i + 1] = qn["qg"]
            with nc.Block(no_gpsimd_drain=True) as blk:
                @blk.scalar
                def _(sc):
                    sc.wait_ge(ep_h, epn[0])
                    last = None
                    for dmi in range(4):
                        dm = rnd * 4 + dmi
                        last = sc.activation(aout_f[:, dm * TT:(dm + 1) * TT],
                                             p_wo[dmi][:], AF.Identity,
                                             bias=bout_c[:, dm:dm + 1])
                    amark(last)
        for rnd in range(2):
            subs = (0, 1) if rnd == 0 else (2, 3)
            with nc.Block(no_gpsimd_drain=True) as blk:
                @blk.tensor
                def _(t):
                    t.wait_ge(ea_h, ean[0])
                    tr = None
                    for j, ssi in enumerate(subs):
                        for dm in range(CH):
                            col = (j * 8 + dm) * 128
                            tr = t.transpose(p3tr[:, col:col + 128],
                                             aout_f[:, dm * TT + ssi * 128:dm * TT + (ssi + 1) * 128],
                                             idm[:])
                    pmark(tr)
            with nc.Block(no_gpsimd_drain=True) as blk:
                @blk.scalar
                def _(sc):
                    sc.wait_ge(ep_h, epn[0])
                    last = None
                    for j, ssi in enumerate(subs):
                        last = sc.activation(aoutT[:, ssi * D:(ssi + 1) * D],
                                             p3tr[:, j * D:(j + 1) * D], AF.Copy)
                    amark(last)
        with nc.Block(no_gpsimd_drain=True) as blk:
            @blk.scalar
            def _(sc):
                amark(sc.activation(sqb[:], aoutT[:], AF.Square))
            @blk.vector
            def _(v):
                v.reduce_sum(stat[:, 0:4],
                             aoutT.ap().rearrange("p (s d) -> p s d", s=NSUB), axis=AX.X)
        with nc.Block(no_gpsimd_drain=True) as blk:
            @blk.vector
            def _(v):
                v.wait_ge(ea_h, ean[0])
                v.reduce_sum(stat[:, 4:8],
                             sqb.ap().rearrange("p (s d) -> p s d", s=NSUB), axis=AX.X)
        with nc.Block(no_gpsimd_drain=True) as blk:
            @blk.vector
            def _(v):
                v.tensor_scalar_mul(stat[:, 0:4], stat[:, 0:4], 1.0 / D)
                v.tensor_scalar_mul(stat[:, 4:8], stat[:, 4:8], 1.0 / D)
                v.drain()
                v.tensor_mul(stat[:, 8:12], stat[:, 0:4], stat[:, 0:4])
                v.drain()
                dmark(v.tensor_sub(stat[:, 8:12], stat[:, 4:8], stat[:, 8:12]))
        with nc.Block(no_gpsimd_drain=True) as blk:
            @blk.scalar
            def _(sc):
                sc.wait_ge(ed_h, edn[0])
                lnsrc = bass.AP(stat.ap().tensor, 8, [[40, 128], [0, 128], [1, 4]])
                sc.activation(junk.ap().rearrange("p (r c) -> p r c", c=4), lnsrc,
                              AF.Ln, bias=cc[:, 0:1])
                sc.drain()
                amark(sc.activation(rstdw[:], junk[:], AF.Exp, scale=cc[:, 1:2]))
        with nc.Block(no_gpsimd_drain=True) as blk:
            @blk.vector
            def _(v):
                v.wait_ge(qg_h, rec_xt[g_i])
                v.wait_ge(ea_h, ean[0])
                for ssi in range(NSUB):
                    v.tensor_scalar(aoutT[:, ssi * D:(ssi + 1) * D],
                                    aoutT[:, ssi * D:(ssi + 1) * D],
                                    stat[:, ssi:ssi + 1], rstdw[:, ssi:ssi + 1],
                                    ALU.subtract, ALU.mult)
                v.drain()
                dmark(v.tensor_add(x2T[:], xcurT[:], aoutT[:]))
        with nc.Block(no_gpsimd_drain=True) as blk:
            @blk.gpsimd
            def _(g):
                g.wait_ge(ed_h, edn[0])
                for ssi in range(NSUB):
                    dma(g, s_x2[g_i * NSUB + ssi, :, :], x2T[:, ssi * D:(ssi + 1) * D],
                        "qg")
                g.wait_ge(qg_h, qn["qg"])
            @blk.scalar
            def _(sc):
                sc.wait_ge(ed_h, edn[0])
                amark(sc.activation(sqb[:], x2T[:], AF.Square))
            @blk.vector
            def _(v):
                v.reduce_sum(stat[:, 0:4],
                             x2T.ap().rearrange("p (s d) -> p s d", s=NSUB), axis=AX.X)
        with nc.Block(no_gpsimd_drain=True) as blk:
            @blk.vector
            def _(v):
                v.wait_ge(ea_h, ean[0])
                v.reduce_sum(stat[:, 4:8],
                             sqb.ap().rearrange("p (s d) -> p s d", s=NSUB), axis=AX.X)
        with nc.Block(no_gpsimd_drain=True) as blk:
            @blk.vector
            def _(v):
                v.tensor_scalar_mul(stat[:, 0:4], stat[:, 0:4], 1.0 / D)
                v.tensor_scalar_mul(stat[:, 4:8], stat[:, 4:8], 1.0 / D)
                v.drain()
                v.tensor_mul(stat[:, 8:12], stat[:, 0:4], stat[:, 0:4])
                v.drain()
                dmark(v.tensor_sub(stat[:, 8:12], stat[:, 4:8], stat[:, 8:12]))
        with nc.Block(no_gpsimd_drain=True) as blk:
            @blk.scalar
            def _(sc):
                sc.wait_ge(ed_h, edn[0])
                lnsrc = bass.AP(stat.ap().tensor, 8, [[40, 128], [0, 128], [1, 4]])
                sc.activation(junk.ap().rearrange("p (r c) -> p r c", c=4), lnsrc,
                              AF.Ln, bias=cc[:, 0:1])
                sc.drain()
                amark(sc.activation(rstdw[:], junk[:], AF.Exp, scale=cc[:, 1:2]))
            @blk.vector
            def _(v):
                for ssi in range(NSUB):
                    v.tensor_single_scalar(xcT[:, ssi * D:(ssi + 1) * D],
                                           x2T[:, ssi * D:(ssi + 1) * D],
                                           stat[:, ssi:ssi + 1], ALU.subtract)
        with nc.Block(no_gpsimd_drain=True) as blk:
            @blk.vector
            def _(v):
                v.reduce_max(stat[:, 16:20],
                             xcT.ap().rearrange("p (s d) -> p s d", s=NSUB),
                             axis=AX.X, apply_absolute_value=True)
        with nc.Block(no_gpsimd_drain=True) as blk:
            @blk.vector
            def _(v):
                v.tensor_scalar_max(stat[:, 16:20], stat[:, 16:20], EPS)
        with nc.Block(no_gpsimd_drain=True) as blk:
            @blk.vector
            def _(v):
                v.reciprocal(stat[:, 20:24], stat[:, 16:20])
        with nc.Block(no_gpsimd_drain=True) as blk:
            @blk.vector
            def _(v):
                v.wait_ge(ea_h, ean[0])
                v.tensor_scalar_mul(stat[:, 20:24], stat[:, 20:24], 127.0)
                v.tensor_scalar_mul(stat[:, 16:20], stat[:, 16:20], 1.0 / 127.0)
                v.drain()
                v.wait_ge(ea_h, ean[0])
                v.tensor_mul(stat[:, 16:20], stat[:, 16:20], rstdw[:, 0:4])
                v.drain()
                v.tensor_scalar_mul(stat[:, 24:28], stat[:, 16:20], wq_cols[:, 4:5])
                v.tensor_scalar_mul(stat[:, 28:32], stat[:, 16:20], wq_cols[:, 5:6])
                for ssi in range(NSUB):
                    v.tensor_scalar_mul(sqb[:, ssi * D:(ssi + 1) * D],
                                        xcT[:, ssi * D:(ssi + 1) * D],
                                        stat[:, 20 + ssi:21 + ssi])
                v.drain()
                for hh in range(2):
                    v.tensor_copy(tmpI[:], sqb[:, hh * 2048:(hh + 1) * 2048])
                    v.drain()
                    li2 = v.tensor_copy(xqT[:, hh * 2048:(hh + 1) * 2048], tmpI[:])
                    if hh == 0:
                        v.drain()
                dmark(li2)
        with nc.Block(no_gpsimd_drain=True) as blk:
            @blk.tensor
            def _(t):
                t.wait_ge(ed_h, edn[0])
                pmark(t.transpose(p3x[0:8, :], stat[:, 24:32], idm_f[:]))
        with nc.Block(no_gpsimd_drain=True) as blk:
            @blk.vector
            def _(v):
                v.wait_ge(ep_h, epn[0])
                dmark(v.tensor_copy(rgr[0:8, :], p3x[0:8, :]))
        with nc.Block(no_gpsimd_drain=True) as blk:
            @blk.gpsimd
            def _(g):
                g.wait_ge(ed_h, edn[0])
                dma(g, s_rg[g_i, :, :], rgr[0:8, :], "qg")
                g.wait_ge(qg_h, qn["qg"])
        for rnd in range(2):
            cs = range(4) if rnd == 0 else range(4, 8)
            with nc.Block(no_gpsimd_drain=True) as blk:
                @blk.tensor
                def _(t):
                    t.wait_ge(ed_h, edn[0])
                    tr = None
                    for jc, c in enumerate(cs):
                        for ssi in range(NSUB):
                            col = (jc * 4 + ssi) * 128
                            tr = t.transpose(p3tr[:, col:col + 128],
                                             xqT[:, ssi * D + c * 128:ssi * D + (c + 1) * 128],
                                             idm[:])
                    pmark(tr)
            with nc.Block(no_gpsimd_drain=True) as blk:
                @blk.scalar
                def _(sc):
                    sc.wait_ge(ep_h, epn[0])
                    last = None
                    for jc, c in enumerate(cs):
                        last = sc.activation(xq_f[:, c * TT:(c + 1) * TT],
                                             p3tr[:, jc * TT:(jc + 1) * TT], AF.Copy)
                    amark(last)
        with nc.Block(no_gpsimd_drain=True) as blk:
            @blk.gpsimd
            def _(g):
                g.wait_ge(ea_h, ean[0])
                for c in range(CH):
                    dma(g, bass.AP(s_xq.tensor, c * (128 * TOK) + g_i * TT,
                                   [[TOK, 128], [1, TT]]),
                        xq_f[:, c * TT:(c + 1) * TT], "qg")
                g.wait_ge(qg_h, qn["qg"])
    a1_st.close()

    a2_st = contextlib.ExitStack()
    p_sbuf, p_psum = mk(a2_st)
    HTOK = TOK // 2
    NTH = 2 if SIM_SMALL else NT // 2
    xq_all = p_sbuf("xq_all", [128, CH * HTOK], BF16)
    rgb_all = p_sbuf("rgb_all", [128, HTOK], BF16)
    rub_all = p_sbuf("rub_all", [128, HTOK], BF16)
    tmpA2 = p_sbuf("tmpA2", [128, 1024], F32)
    wslc = [[p_sbuf(f"wslc{w}_{i}", [128, 1024], BF16) for i in range(2)]
            for w in range(2)]
    gat = [p_sbuf(f"gat{i}", [128, 1024], BF16) for i in range(2)]
    gat2 = [p_sbuf(f"gat2_{i}", [128, 1024], BF16) for i in range(2)]
    upv = [p_sbuf(f"upv{i}", [128, 1024], BF16) for i in range(4)]
    guv = [p_sbuf(f"guv{i}", [128, 1024], BF16) for i in range(2)]
    absgu = [p_sbuf(f"absgu{i}", [128, 1024], BF16) for i in range(2)]
    runmx = [p_sbuf(f"runmx{i}", [128, 1024], BF16) for i in range(NTH // 2)]
    trmx = p_sbuf("trmx", [128, 1024], BF16)
    red1 = p_sbuf("red1", [128, 32], F32)
    flmx = p_sbuf("flmx", [32, 128], F32)
    amfin = p_sbuf("amfin", [32, 32], F32)
    amfinT = p_sbuf("amfinT", [32, 32], F32)
    p_g = [p_psum(f"p_g{i}", [128, 1024], F32) for i in range(2)]
    p_u = [p_psum(f"p_u{i}", [128, 1024], F32) for i in range(2)]

    def wslc_ap(tns, m):
        return bass.AP(tns.tensor, m * 128,
                       [[INNER, 128], [128 * INNER, 8], [1, 128]])

    rec_w = {}
    rec_si = {}
    rec_pe = {}
    rec_ga = {}
    rec_gv = {}
    am_store_qg = {}
    rec_ab = {}
    for half in range(1 if SIM_SMALL else 2):
        tb = half * NTH
        with nc.Block(no_gpsimd_drain=True) as blk:
            @blk.sync
            def _(s):
                for c in range(CH):
                    for gg in range(NTH):
                        dma(s, xq_all[:, c * HTOK + gg * TT:c * HTOK + (gg + 1) * TT],
                            bass.AP(s_xq.tensor, c * (128 * TOK) + (tb + gg) * TT,
                                    [[TOK, 128], [1, TT]]), "qs")
                s.wait_ge(qs_h, qn["qs"])
        for gg in range(NTH):
            with nc.Block(no_gpsimd_drain=True) as blk:
                @blk.sync
                def _(s):
                    for ssi in range(NSUB):
                        dma(s, tmpA2[:, ssi * 128:(ssi + 1) * 128],
                            bass.AP(s_rg.tensor, ((tb + gg) * 8 + ssi) * 128,
                                    [[0, 128], [1, 128]]), "qs")
                        dma(s, tmpB2_ap(tmpA2, ssi),
                            bass.AP(s_rg.tensor, ((tb + gg) * 8 + 4 + ssi) * 128,
                                    [[0, 128], [1, 128]]), "qs")
                    s.wait_ge(qs_h, qn["qs"])
            with nc.Block(no_gpsimd_drain=True) as blk:
                @blk.vector
                def _(v):
                    v.tensor_copy(rgb_all[:, gg * TT:(gg + 1) * TT], tmpA2[:, 0:512])
                    v.tensor_copy(rub_all[:, gg * TT:(gg + 1) * TT], tmpA2[:, 512:1024])
        MCHs = 2 if SIM_SMALL else MCH
        NP = MCHs * (NTH // 2)
        def unit(J):
            return J // (NTH // 2), (J % (NTH // 2)) * 2
        with nc.Block(no_gpsimd_drain=True) as blk:
            @blk.sync
            def _(s):
                dma(s, wslc[0][0][:], wslc_ap(s_wgq, 0), "qs")
                dma(s, wslc[1][0][:], wslc_ap(s_wuq, 0), "qs")
                rec_w[(half, 0)] = qn["qs"]
        for J in range(NP + 8):
            with nc.Block(no_gpsimd_drain=True) as blk:
                @blk.sync
                def _(s):
                    if J < NP and J % (NTH // 2) == 0:
                        m = unit(J)[0]
                        if m + 1 < MCH:
                            dma(s, wslc[0][(m + 1) % 2][:], wslc_ap(s_wgq, m + 1), "qs")
                            dma(s, wslc[1][(m + 1) % 2][:], wslc_ap(s_wuq, m + 1), "qs")
                            rec_w[(half, m + 1)] = qn["qs"]
                @blk.tensor
                def _(t):
                    if J < NP:
                        m, gg = unit(J)
                        t.wait_ge(qs_h, rec_w[(half, m)])
                        mm = None
                        for pgp, wsl in ((p_g, wslc[0]), (p_u, wslc[1])):
                            for tt2 in range(2):
                                for k in range(CH):
                                    mm = t.matmul(pgp[J % 2][:, tt2 * TT:(tt2 + 1) * TT],
                                                  wsl[m % 2][:, k * 128:(k + 1) * 128],
                                                  xq_all[:, k * HTOK + (gg + tt2) * TT:
                                                         k * HTOK + (gg + tt2 + 1) * TT],
                                                  start=(k == 0), stop=(k == CH - 1))
                        pmark(mm)
                        rec_pe[J] = epn[0]
                @blk.vector
                def _(v):
                    li = None
                    if 1 <= J < NP + 1:
                        Jp = J - 1
                        v.wait_ge(ep_h, rec_pe[Jp])
                        li = v.tensor_mul(gat[Jp % 2][:], p_g[Jp % 2][:],
                                          rgb_all[:, rgslice(Jp, NTH)])
                        v.tensor_mul(upv[Jp % 4][:], p_u[Jp % 2][:],
                                     rub_all[:, rgslice(Jp, NTH)])
                        rec_ga[Jp] = edn[0] + 1
                    if 3 <= J < NP + 3:
                        Jp = J - 3
                        v.wait_ge(ea_h, rec_si[Jp])
                        li = v.tensor_mul(guv[Jp % 2][:], gat2[Jp % 2][:], upv[Jp % 4][:])
                        rec_gv[Jp] = edn[0] + 1
                    if li is not None:
                        dmark(li)
                    if 5 <= J < NP + 5:
                        Jp = J - 5
                        m, gg = unit(Jp)
                        gi = gg // 2
                        v.wait_ge(ea_h, rec_ab[Jp])
                        if m == 0:
                            v.tensor_copy(runmx[gi][:], absgu[Jp % 2][:])
                        else:
                            v.tensor_max(runmx[gi][:], runmx[gi][:],
                                         absgu[Jp % 2][:])
                @blk.scalar
                def _(sc):
                    if 2 <= J < NP + 2:
                        Jp = J - 2
                        sc.wait_ge(ed_h, rec_ga[Jp])
                        amark(sc.activation(gat2[Jp % 2][:], gat[Jp % 2][:], AF.Silu))
                        rec_si[Jp] = ean[0]
                    if 4 <= J < NP + 4:
                        Jp = J - 4
                        sc.wait_ge(ed_h, rec_gv[Jp])
                        amark(sc.activation(absgu[Jp % 2][:], guv[Jp % 2][:], AF.Abs))
                        rec_ab[Jp] = ean[0]
                @blk.gpsimd
                def _(g):
                    if 4 <= J < NP + 4:
                        Jp = J - 4
                        m, gg = unit(Jp)
                        g.wait_ge(ed_h, rec_gv[Jp])
                        dma(g, bass.AP(s_gu.tensor,
                                       m * (128 * TOK) + (half * NTH + gg) * TT,
                                       [[TOK, 128], [1, 2 * TT]]), guv[Jp % 2][:], "qg")
                        g.wait_ge(qg_h, qn["qg"])
        for gi in range(NTH // 2):
            fi = half * (NTH // 2) + gi
            with nc.Block(no_gpsimd_drain=True) as blk:
                red_mark = edn[0] + 1
                @blk.sync
                def _(s):
                    s.wait_ge(ed_h, red_mark)
                    for a in range(4):
                        dma(s, flmx[:, 32 * a:32 * (a + 1)],
                            red1[32 * a:32 * (a + 1), :], "qs")
                fl_tgt = qn["qs"]
                @blk.vector
                def _(v):
                    v.transpose(trmx[:], runmx[gi][:])
                    v.drain()
                    dmark(v.tensor_reduce(
                        red1[:], trmx.ap().rearrange("p (j v) -> p j v", v=32),
                        axis=AX.X, op=ALU.max))
                    v.wait_ge(qs_h, fl_tgt)
                    if fi >= 1:
                        v.wait_ge(qg_h, am_store_qg[fi - 1])
                    v.tensor_reduce(
                        amfin[:], flmx.ap().rearrange("p (a j) -> p j a", a=4),
                        axis=AX.X, op=ALU.max)
                    v.drain()
                    dmark(v.transpose(amfinT[:], amfin[:]))
                am_mark = edn[0]
                @blk.gpsimd
                def _(g):
                    g.wait_ge(ed_h, am_mark)
                    dma(g, bass.AP(s_am.tensor, (half * NTH + gi * 2) * TT,
                                   [[32, 32], [1, 32]]), amfinT[:], "qg")
                    am_store_qg[fi] = qn["qg"]
                    g.wait_ge(qg_h, qn["qg"])
    a2_st.close()

    b_st = contextlib.ExitStack()
    p_sbuf, p_psum = mk(b_st)
    wd_sb = p_sbuf("wd_sb", [128, MCH * D], BF16)
    gu_all = p_sbuf("gu_all", [128, MCH * TT], BF16)
    guq_all = p_sbuf("guq_all", [128, MCH * TT], BF16)
    rowmx = p_sbuf("rowmx", [1, TT], F32)
    rowmn = p_sbuf("rowmn", [1, TT], F32)
    rowsc = p_sbuf("rowsc", [1, TT], BF16)
    rowrd = p_sbuf("rowrd", [1, TT], BF16)
    scb = p_sbuf("scb", [128, TT], F32)
    rdb = p_sbuf("rdb", [128, TT], F32)
    ffn_f = p_sbuf("ffn_f", [128, CH * TT], BF16)
    ffnT = p_sbuf("ffnT", [128, NSUB * D], BF16)
    x2l = p_sbuf("x2l", [128, NSUB * D], F32)
    outT = p_sbuf("outT", [128, NSUB * D], F32)
    p_dn = [p_psum(f"p_dn{i}", [128, TT], F32) for i in range(4)]
    p3trb = p_psum("p3trb", [128, 2048], BF16)
    p_b2 = p_psum("p_b2", [128, 2 * TT], F32)

    out_flat = out_h.rearrange("b l d -> (b l) d")
    rec_addf = {}
    rec_amq = {}

    with nc.Block(no_gpsimd_drain=True) as blk:
        @blk.sync
        def _(s):
            for k in range(MCH):
                dma(s, wd_sb[:, k * D:(k + 1) * D],
                    bass.AP(s_wdq.tensor, k * 128 * D, [[D, 128], [1, D]]), "qs")
            s.wait_ge(qs_h, qn["qs"])

    for g_i in range(NTs):
        with nc.Block(no_gpsimd_drain=True) as blk:
            @blk.sync
            def _(s):
                for k in range(MCH):
                    dma(s, gu_all[:, k * TT:(k + 1) * TT],
                        bass.AP(s_gu.tensor, k * (128 * TOK) + g_i * TT,
                                [[TOK, 128], [1, TT]]), "qs")
                s.wait_ge(qs_h, qn["qs"])
            @blk.gpsimd
            def _(g):
                if g_i >= 1:
                    g.wait_ge(ed_h, rec_addf[g_i - 1])
                for ssi in range(NSUB):
                    dma(g, x2l[:, ssi * D:(ssi + 1) * D],
                        s_x2[g_i * NSUB + ssi, :, :], "qg")
                dma(g, rowmx[:], bass.AP(s_am.tensor, g_i * TT,
                                         [[TT, 1], [1, TT]]), "qg")
                rec_amq[g_i] = qn["qg"]
                g.wait_ge(qg_h, qn["qg"])
            @blk.vector
            def _(v):
                v.wait_ge(qg_h, rec_amq[g_i])
                v.tensor_scalar_max(rowmx[:], rowmx[:], EPS)
                v.drain()
                v.reciprocal(rowmn[:], rowmx[:])
                v.drain()
                v.tensor_scalar_mul(rowmn[:], rowmn[:], 127.0)
                v.drain()
                v.tensor_copy(rowsc[:], rowmn[:])
                v.tensor_scalar_mul(rowmx[:], rowmx[:], 1.0 / 127.0)
                v.drain()
                v.tensor_scalar_mul(rowmx[:], rowmx[:], wq_cols[0:1, 6:7])
                v.drain()
                dmark(v.tensor_copy(rowrd[:], rowmx[:]))
        with nc.Block(no_gpsimd_drain=True) as blk:
            @blk.tensor
            def _(t):
                t.wait_ge(ed_h, edn[0])
                t.matmul(p_b2[:, 0:TT], onesr[:], rowsc[0:1, :],
                         start=True, stop=True)
                pmark(t.matmul(p_b2[:, TT:2 * TT], onesr[:], rowrd[0:1, :],
                               start=True, stop=True))
            EO0 = eon[0]
            @blk.vector
            def _(v):
                v.wait_ge(ep_h, epn[0])
                v.tensor_copy(scb[:], p_b2[:, 0:TT])
                v.drain()
                dmark(v.tensor_copy(rdb[:], p_b2[:, TT:2 * TT]))
                ED0 = edn[0]
                for q in range(1, 18):
                    if q < 17:
                        v.wait_ge(eo_h, EO0 + min(q, 16))
                        for kk in range(2):
                            v.tensor_copy(tmpI[:, ((q - 1) % 2) * 1024 + kk * TT:
                                               ((q - 1) % 2) * 1024 + (kk + 1) * TT],
                                          outT[:, ((q - 1) % 2) * 2048 + kk * TT:
                                               ((q - 1) % 2) * 2048 + (kk + 1) * TT])
                    if q >= 2:
                        for kk in range(2):
                            k = (q - 2) * 2 + kk
                            v.tensor_copy(guq_all[:, k * TT:(k + 1) * TT],
                                          tmpI[:, ((q - 2) % 2) * 1024 + kk * TT:
                                               ((q - 2) % 2) * 1024 + (kk + 1) * TT])
                        dmarkd(v)
                    elif q == 1:
                        dmarkd(v)
            ED0 = edn[0] - 17
            @blk.gpsimd
            def _(g):
                for q in range(16):
                    g.wait_ge(ed_h, ED0 if q < 2 else ED0 + q - 1)
                    last = None
                    for kk in range(2):
                        k = q * 2 + kk
                        last = g.tensor_tensor(outT[:, (q % 2) * 2048 + kk * TT:
                                                    (q % 2) * 2048 + (kk + 1) * TT],
                                               gu_all[:, k * TT:(k + 1) * TT],
                                               scb[:], ALU.mult)
                    omark(last)
        with nc.Block(no_gpsimd_drain=True) as blk:
            EVD = edn[0]
            @blk.tensor
            def _(t):
                for rnd in range(2):
                    t.wait_ge(ed_h, edn[0] if rnd == 0 else EVD + 1)
                    mm = None
                    for dmi in range(4):
                        dm = rnd * 4 + dmi
                        for k in range(MCH):
                            mm = t.matmul(p_dn[dmi][:],
                                          wd_sb[:, k * D + dm * 128:k * D + (dm + 1) * 128],
                                          guq_all[:, k * TT:(k + 1) * TT],
                                          start=(k == 0), stop=(k == MCH - 1))
                    pmark(mm)
            EPD = epn[0] - 2
            @blk.vector
            def _(v):
                for rnd in range(2):
                    v.wait_ge(ep_h, EPD + 1 + rnd)
                    for dmi in range(4):
                        dm = rnd * 4 + dmi
                        v.tensor_mul(ffn_f[:, dm * TT:(dm + 1) * TT],
                                     p_dn[dmi][:], rdb[:])
                    dmarkd(v)
        with nc.Block(no_gpsimd_drain=True) as blk:
            EAT = ean[0]
            @blk.tensor
            def _(t):
                for rnd in range(2):
                    t.wait_ge(ed_h, edn[0] if rnd == 0 else EVD + 2)
                    if rnd == 1:
                        t.wait_ge(ea_h, EAT + 1)
                    subs = (0, 1) if rnd == 0 else (2, 3)
                    tr = None
                    for j, ssi in enumerate(subs):
                        for dm in range(CH):
                            col = (j * 8 + dm) * 128
                            tr = t.transpose(p3trb[:, col:col + 128],
                                             ffn_f[:, dm * TT + ssi * 128:dm * TT + (ssi + 1) * 128],
                                             idm[:])
                    pmark(tr)
            EPT = epn[0] - 2
            @blk.scalar
            def _(sc):
                for rnd in range(2):
                    sc.wait_ge(ep_h, EPT + 1 + rnd)
                    subs = (0, 1) if rnd == 0 else (2, 3)
                    last = None
                    for j, ssi in enumerate(subs):
                        last = sc.activation(ffnT[:, ssi * D:(ssi + 1) * D],
                                             p3trb[:, j * D:(j + 1) * D], AF.Copy)
                    amark(last)
        with nc.Block(no_gpsimd_drain=True) as blk:
            EA4 = ean[0]
            E4 = edn[0]
            @blk.scalar
            def _(sc):
                amark(sc.activation(gu_all.ap()[:, 0:4 * D], ffnT[:], AF.Square))
                sc.wait_ge(ed_h, E4 + 1)
                lnsrc = bass.AP(stat.ap().tensor, 8, [[40, 128], [0, 128], [1, 4]])
                sc.activation(junk.ap().rearrange("p (r c) -> p r c", c=4), lnsrc,
                              AF.Ln, bias=cc[:, 0:1])
                sc.drain()
                amark(sc.activation(rstdw[:], junk[:], AF.Exp, scale=cc[:, 1:2]))
            @blk.vector
            def _(v):
                v.wait_ge(ea_h, EA4)
                v.reduce_sum(stat[:, 0:4],
                             ffnT.ap().rearrange("p (s d) -> p s d", s=NSUB),
                             axis=AX.X)
                v.wait_ge(ea_h, EA4 + 1)
                v.reduce_sum(stat[:, 4:8],
                             gu_all.ap()[:, 0:4 * D].rearrange("p (s d) -> p s d", s=NSUB),
                             axis=AX.X)
                v.drain()
                v.tensor_scalar_mul(stat[:, 0:4], stat[:, 0:4], 1.0 / D)
                v.tensor_scalar_mul(stat[:, 4:8], stat[:, 4:8], 1.0 / D)
                v.drain()
                v.tensor_mul(stat[:, 8:12], stat[:, 0:4], stat[:, 0:4])
                v.drain()
                dmark(v.tensor_sub(stat[:, 8:12], stat[:, 4:8], stat[:, 8:12]))
                v.wait_ge(ea_h, EA4 + 2)
                for ssi in range(NSUB):
                    v.tensor_scalar(ffnT[:, ssi * D:(ssi + 1) * D],
                                    ffnT[:, ssi * D:(ssi + 1) * D],
                                    stat[:, ssi:ssi + 1], rstdw[:, ssi:ssi + 1],
                                    ALU.subtract, ALU.mult)
                v.drain()
                dmark(v.tensor_add(outT[:], x2l[:], ffnT[:]))
                rec_addf[g_i] = edn[0]
            @blk.gpsimd
            def _(g):
                g.wait_ge(ed_h, rec_addf[g_i])
                for ssi in range(NSUB):
                    t0 = g_i * TT + ssi * 128
                    dma(g, out_flat[t0:t0 + 128, :], outT[:, ssi * D:(ssi + 1) * D],
                        "qg")
                if dbg:
                    dma(g, s_dbg2[g_i, :, :], ffn_f[:], "qg")
                    dma(g, s_dbg3[g_i, :, :], ffnT[:], "qg")
                g.wait_ge(qg_h, qn["qg"])

    with nc.Block(no_gpsimd_drain=True) as blk:
        @blk.gpsimd
        def _(g):
            g.wait_ge(qg_h, qn["qg"])
            g.wait_ge(qs_h, qn["qs"])
        @blk.vector
        def _(v):
            v.wait_ge(qg_h, qn["qg"])
    b_st.close()

    g_st.close()
    return nc


def tmpB2_ap(tmpA2, ssi):
    return tmpA2[:, 512 + ssi * 128:512 + (ssi + 1) * 128]


def rgslice(Jp, NTH):
    gg = (Jp % (NTH // 2)) * 2
    return slice(gg * TT, (gg + 2) * TT)


def kernel(**inputs):
    x = np.ascontiguousarray(inputs["L0_attn_x"])
    A_log = np.asarray(inputs["L0_attn_A_log"])
    Wx = np.ascontiguousarray(inputs["L0_attn_Wx"]).astype(np.float32, copy=False)
    Wdt = np.asarray(inputs["L0_attn_Wdt"])
    bdt = np.asarray(inputs["L0_attn_bdt"])
    Wout = np.asarray(inputs["L0_attn_Wout"])
    bout = np.asarray(inputs["L0_attn_bout"])
    wg = np.asarray(inputs["L0_ffn_w_gate"])
    wu = np.asarray(inputs["L0_ffn_w_up"])
    wd = np.asarray(inputs["L0_ffn_w_down"])

    acols = np.empty((128, 128), dtype=np.float32)
    for hp in range(HP):
        for n in range(16):
            acols[0:64, hp * 16 + n] = A_log[2 * hp, n]
            acols[64:128, hp * 16 + n] = A_log[2 * hp + 1, n]
    in_common = {
        "acols": acols,
        "bdt": np.ascontiguousarray(bdt.reshape(64, 1), dtype=np.float32),
        "wx": Wx,
        "wdtT": np.ascontiguousarray(Wdt.T),
        "wxbcT": np.ascontiguousarray(Wx[64:96, :].T),
        "woutT": np.ascontiguousarray(Wout.T),
        "boutc": np.ascontiguousarray(bout.reshape(CH, 128).T),
        "wgT": np.ascontiguousarray(wg.T),
        "wuT": np.ascontiguousarray(wu.T),
        "wdT": np.ascontiguousarray(wd.T),
        "onesr": np.ones((1, 128), dtype=ml_dtypes.bfloat16),
        "onesc": np.ones((128, 1), dtype=ml_dtypes.bfloat16),
        "idm": np.eye(128, dtype=ml_dtypes.bfloat16),
    }
    nc = build_nc()
    in_maps = [dict(in_common, x=np.ascontiguousarray(x[c * B:(c + 1) * B]))
               for c in range(NCORES)]
    res = run_bass_kernel_spmd(nc, in_maps, core_ids=list(range(NCORES)))
    return np.concatenate([r["out"] for r in res.results], axis=0)


# version 10
# speedup vs baseline: 2.1962x; 1.0061x over previous
import contextlib
import numpy as np
import ml_dtypes
import concourse.bass as bass
import concourse.mybir as mybir
from concourse.bass_utils import run_bass_kernel_spmd

F32 = mybir.dt.float32
BF16 = mybir.dt.bfloat16
I32 = mybir.dt.int32
AF = mybir.ActivationFunctionType
ALU = mybir.AluOpType
AX = mybir.AxisListType

NCORES = 8
B = 4
L = 2048
D = 1024
H = 16
HP = 8
DH = 64
DS = 16
INNER = 4096
CH = D // 128
MCH = INNER // 128
TT = 512
NT = (B * L) // TT
NSUB = TT // 128
EPS = 1e-5
TOK = B * L


SIM_SMALL = False


def build_nc(dbg=False):
    NTs = 2 if SIM_SMALL else NT
    NITs = 2 if SIM_SMALL else B * HP
    nc = bass.Bass("TRN2", target_bir_lowering=False)

    x_h = nc.dram_tensor("x", [B, L, D], F32, kind="ExternalInput").ap()
    acols_h = nc.dram_tensor("acols", [128, 128], F32, kind="ExternalInput").ap()
    bdt_h = nc.dram_tensor("bdt", [64, 1], F32, kind="ExternalInput").ap()
    wx_h = nc.dram_tensor("wx", [96, 64], F32, kind="ExternalInput").ap()
    wdtT_h = nc.dram_tensor("wdtT", [64, 64], F32, kind="ExternalInput").ap()
    wxbcT_h = nc.dram_tensor("wxbcT", [64, 32], F32, kind="ExternalInput").ap()
    woutT_h = nc.dram_tensor("woutT", [D, D], F32, kind="ExternalInput").ap()
    bout_h = nc.dram_tensor("boutc", [128, CH], F32, kind="ExternalInput").ap()
    wgT_h = nc.dram_tensor("wgT", [D, INNER], F32, kind="ExternalInput").ap()
    wuT_h = nc.dram_tensor("wuT", [D, INNER], F32, kind="ExternalInput").ap()
    wdT_h = nc.dram_tensor("wdT", [INNER, D], F32, kind="ExternalInput").ap()
    onesr_h = nc.dram_tensor("onesr", [1, 128], BF16, kind="ExternalInput").ap()
    onesc_h = nc.dram_tensor("onesc", [128, 1], BF16, kind="ExternalInput").ap()
    idm_h = nc.dram_tensor("idm", [128, 128], BF16, kind="ExternalInput").ap()
    out_h = nc.dram_tensor("out", [B, L, D], F32, kind="ExternalOutput").ap()

    KD = "ExternalOutput" if dbg else "Internal"
    s_dt = nc.dram_tensor("s_dt", [B, H, DH, L], F32, kind=KD).ap()
    s_z = nc.dram_tensor("s_z", [B, H, DH, L], F32, kind=KD).ap()
    s_bc = nc.dram_tensor("s_bc", [B, H, 2, DS, L], BF16, kind=KD).ap()
    s_y = nc.dram_tensor("s_y", [CH, 128, TOK], BF16, kind=KD).ap()
    s_x2 = nc.dram_tensor("s_x2", [TOK // 128, 128, D], F32, kind=KD).ap()
    s_xq = nc.dram_tensor("s_xq", [CH, 128, TOK], BF16, kind=KD).ap()
    s_gu = nc.dram_tensor("s_gu", [MCH, 128, TOK], BF16, kind=KD).ap()
    s_wgq = nc.dram_tensor("s_wgq", [D, INNER], BF16, kind=KD).ap()
    s_wuq = nc.dram_tensor("s_wuq", [D, INNER], BF16, kind=KD).ap()
    s_wdq = nc.dram_tensor("s_wdq", [INNER, D], BF16, kind=KD).ap()
    s_stage = nc.dram_tensor("s_stage", [1, 16], F32, kind=KD).ap()
    s_rg = nc.dram_tensor("s_rg", [NT, 8, 128], F32, kind=KD).ap()
    s_am = nc.dram_tensor("s_am", [1, TOK], F32, kind=KD).ap()
    s_dbg = nc.dram_tensor("s_dbg", [NT, 128, 40], F32, kind=KD).ap()
    s_dbg2 = nc.dram_tensor("s_dbg2", [NT, 128, 4096], BF16, kind=KD).ap()
    s_dbg3 = nc.dram_tensor("s_dbg3", [NT, 128, 4096], BF16, kind=KD).ap()

    g_st = contextlib.ExitStack()

    def mk(stk):
        return (lambda n, s, d: stk.enter_context(nc.sbuf_tensor(n, s, d)),
                lambda n, s, d: stk.enter_context(nc.psum_tensor(n, s, d)))

    sbuf, psum = mk(g_st)

    ascale = sbuf("ascale", [128, 128], F32)
    bdt_c = sbuf("bdt_c", [64, 1], F32)
    lhsT_f = sbuf("lhsT_f", [64, 96], F32)
    lhsT_b = sbuf("lhsT_b", [64, 96], BF16)
    bout_c = sbuf("bout_c", [128, CH], F32)
    onesr = sbuf("onesr_s", [1, 128], BF16)
    onesc = sbuf("onesc_s", [128, 1], BF16)
    idm = sbuf("idm_s", [128, 128], BF16)
    idm_f = sbuf("idmf_s", [128, 128], F32)
    wq_cols = sbuf("wq_cols", [128, 8], F32)
    stat = sbuf("stat", [128, 40], F32)
    tmpI = sbuf("tmpI", [128, 2 * D], I32)
    cc = sbuf("cc", [128, 4], F32)
    junk = sbuf("junk", [128, 512], F32)
    rstdw = sbuf("rstdw", [128, 512], F32)

    qs_h = g_st.enter_context(nc.semaphore("qs"))
    qg_h = g_st.enter_context(nc.semaphore("qg"))
    ev_h = g_st.enter_context(nc.semaphore("ev"))
    ea_h = g_st.enter_context(nc.semaphore("ea"))
    eo_h = g_st.enter_context(nc.semaphore("eo"))
    ep_h = g_st.enter_context(nc.semaphore("ep"))
    ed_h = g_st.enter_context(nc.semaphore("ed"))
    qn = {"qs": 0, "qg": 0}
    evn = [0]
    ean = [0]
    eon = [0]
    epn = [0]
    edn = [0]

    def dmark(inst):
        inst.then_inc(ed_h, 1)
        edn[0] += 1

    def dmarkd(eng):
        eng.drain().then_inc(ed_h, 1)
        edn[0] += 1

    def pmark(inst):
        inst.then_inc(ep_h, 1)
        epn[0] += 1

    def amark(inst):
        inst.then_inc(ea_h, 1)
        ean[0] += 1

    def omark(inst):
        inst.then_inc(eo_h, 1)
        eon[0] += 1

    def dma(eng, dst, src, q):
        h = qs_h if q == "qs" else qg_h
        eng.wait_ge(h, qn[q])
        eng.dma_start(out=dst, in_=src).then_inc(h, 16)
        qn[q] += 16

    su_st = contextlib.ExitStack()
    s_sbuf, s_psum = mk(su_st)
    tmpA = s_sbuf("tmpA", [128, 2048], F32)
    tmpC = s_sbuf("tmpC", [128, 2048], BF16)
    tmpC2 = s_sbuf("tmpC2", [128, 2048], BF16)
    wld = [s_sbuf(f"wld{i}", [128, 2048], F32) for i in range(2)]
    wabs = s_sbuf("wabs", [128, 2048], F32)
    onesc32 = s_sbuf("onesc32", [128, 1], F32)
    p_small = s_psum("p_small", [96, 512], F32)
    p_row = s_psum("p_row", [1, 512], F32)

    with nc.Block(no_gpsimd_drain=True) as blk:
        @blk.sync
        def _(s):
            dma(s, ascale[:], acols_h[:], "qs")
            dma(s, bdt_c[:], bdt_h[:], "qs")
            dma(s, tmpA[0:96, 0:64], wx_h[:], "qs")
            dma(s, tmpA[0:64, 64:128], wdtT_h[:], "qs")
            dma(s, lhsT_f[:, 64:96], wxbcT_h[:], "qs")
            dma(s, bout_c[:], bout_h[:], "qs")
            dma(s, onesr[:], onesr_h[:], "qs")
            dma(s, onesc[:], onesc_h[:], "qs")
            dma(s, idm[:], idm_h[:], "qs")
            s.wait_ge(qs_h, qn["qs"])
        @blk.gpsimd
        def _(g):
            g.memset(cc[:, 0:1], EPS)
            g.memset(cc[:, 1:2], -0.5)
            g.memset(cc[:, 2:3], -1.0)
            g.memset(junk[:], 0.0)
            g.memset(stat[:], 0.0)
            omark(g.memset(rstdw[:], 0.0))
    with nc.Block(no_gpsimd_drain=True) as blk:
        @blk.scalar
        def _(sc):
            sc.wait_ge(eo_h, eon[0])
            sc.activation(tmpA[0:128, 128:256], ascale[:], AF.Exp)
            amark(sc.activation(junk[:], junk[:], AF.Copy))
        @blk.tensor
        def _(t):
            pmark(t.matmul(p_small[0:64, 0:64], tmpA[0:64, 0:64], tmpA[0:64, 64:128],
                           start=True, stop=True))
    with nc.Block(no_gpsimd_drain=True) as blk:
        @blk.vector
        def _(v):
            v.wait_ge(ea_h, ean[0])
            v.wait_ge(ep_h, epn[0])
            v.tensor_scalar_mul(ascale[:], tmpA[0:128, 128:256], -1.0)
            v.tensor_copy(lhsT_f[:, 0:64], p_small[0:64, 0:64])
    with nc.Block(no_gpsimd_drain=True) as blk:
        @blk.vector
        def _(v):
            v.tensor_copy(lhsT_b[:], lhsT_f[:])
            v.tensor_copy(onesc32[:], onesc[:])
            dmark(v.tensor_copy(idm_f[:], idm[:]))

    w_list = [(wgT_h, s_wgq, D, INNER, 0), (wuT_h, s_wuq, D, INNER, 1),
              (wdT_h, s_wdq, INNER, D, 2)]
    for (src, _d, R, Cc, idx) in w_list:
        CW = min(2048, Cc)
        cpr = Cc // CW
        ntile = (R // 128) * cpr
        if SIM_SMALL:
            ntile = 2
        njr = CW // 512
        for it in range(ntile + 1):
            with nc.Block(no_gpsimd_drain=True) as blk:
                @blk.sync
                def _(s):
                    if it < ntile:
                        r, c = it // cpr, it % cpr
                        dma(s, wld[it % 2][:, 0:CW],
                            src[r * 128:(r + 1) * 128, c * CW:(c + 1) * CW], "qs")
                    s.wait_ge(qs_h, qn["qs"])
                @blk.scalar
                def _(sc):
                    if it >= 1:
                        amark(sc.activation(wabs[:, 0:CW], wld[(it - 1) % 2][:, 0:CW],
                                            AF.Abs))
            if it >= 1:
                with nc.Block(no_gpsimd_drain=True) as blk:
                    @blk.tensor
                    def _(t):
                        t.wait_ge(ea_h, ean[0])
                        mm = None
                        for j in range(njr):
                            mm = t.matmul(p_row[0:1, 0:512], onesc32[:],
                                          wabs[:, j * 512:(j + 1) * 512],
                                          start=(it == 1 and j == 0),
                                          stop=(it == ntile and j == njr - 1))
                        if it == ntile:
                            pmark(mm)
        with nc.Block(no_gpsimd_drain=True) as blk:
            @blk.vector
            def _(v):
                v.wait_ge(ep_h, epn[0])
                v.reduce_sum(tmpA[0:1, 0:1], p_row[0:1, 0:512], axis=AX.X)
        with nc.Block(no_gpsimd_drain=True) as blk:
            @blk.vector
            def _(v):
                v.tensor_scalar(tmpA[0:1, 1:2], tmpA[0:1, 0:1],
                                1.0 / (R * Cc), EPS, ALU.mult, ALU.max)
        with nc.Block(no_gpsimd_drain=True) as blk:
            @blk.vector
            def _(v):
                dmark(v.reciprocal(tmpA[0:1, 2:3], tmpA[0:1, 1:2]))
        with nc.Block(no_gpsimd_drain=True) as blk:
            @blk.gpsimd
            def _(g):
                g.wait_ge(ed_h, edn[0])
                dma(g, s_stage[0:1, idx * 2:idx * 2 + 2], tmpA[0:1, 1:3], "qg")
                g.wait_ge(qg_h, qn["qg"])
    with nc.Block(no_gpsimd_drain=True) as blk:
        @blk.sync
        def _(s):
            for idx in range(3):
                dma(s, wq_cols[:, 4 + idx:5 + idx],
                    bass.AP(s_stage.tensor, idx * 2, [[0, 128], [1, 1]]), "qs")
                dma(s, wq_cols[:, idx:idx + 1],
                    bass.AP(s_stage.tensor, idx * 2 + 1, [[0, 128], [1, 1]]), "qs")
            s.wait_ge(qs_h, qn["qs"])

    for (src, dst, R, Cc, idx) in w_list:
        CW = min(2048, Cc)
        cpr = Cc // CW
        ntile = (R // 128) * cpr
        if SIM_SMALL:
            ntile = 2
        for it in range(ntile + 2):
            with nc.Block(no_gpsimd_drain=True) as blk:
                @blk.sync
                def _(s):
                    if it < ntile:
                        r, c = it // cpr, it % cpr
                        dma(s, wld[it % 2][:, 0:CW],
                            src[r * 128:(r + 1) * 128, c * CW:(c + 1) * CW], "qs")
                    s.wait_ge(qs_h, qn["qs"])
                @blk.vector
                def _(v):
                    if 1 <= it <= ntile:
                        j = it - 1
                        qb = tmpC if j % 2 == 0 else tmpC2
                        v.tensor_scalar_mul(tmpA[:, 0:CW], wld[j % 2][:, 0:CW],
                                            wq_cols[:, idx:idx + 1])
                        v.drain()
                        v.tensor_copy(tmpI[:, 0:CW], tmpA[:, 0:CW])
                        v.drain()
                        v.tensor_scalar(tmpI[:, 0:CW], tmpI[:, 0:CW], 1, -1,
                                        ALU.min, ALU.max)
                        v.drain()
                        dmark(v.tensor_copy(qb[:, 0:CW], tmpI[:, 0:CW]))
                @blk.gpsimd
                def _(g):
                    if it >= 2:
                        g.wait_ge(ed_h, edn[0])
                        j = it - 2
                        r, c = j // cpr, j % cpr
                        qb = tmpC if j % 2 == 0 else tmpC2
                        dma(g, dst[r * 128:(r + 1) * 128, c * CW:(c + 1) * CW],
                            qb[:, 0:CW], "qg")
                        g.wait_ge(qg_h, qn["qg"])
    su_st.close()

    p1_st = contextlib.ExitStack()
    p_sbuf, p_psum = mk(p1_st)
    xt = [p_sbuf(f"xt{i}", [128, NSUB * D], F32) for i in range(2)]
    sqb = p_sbuf("sqb", [128, 4 * D], F32)
    xn1T = p_sbuf("xn1T", [128, NSUB * D], BF16)
    xf = p_sbuf("xf", [64, NSUB * 2048], BF16)
    dt_p1 = [p_sbuf(f"dt_p1{i}", [64, 2048], F32) for i in range(2)]
    z_p1 = [p_sbuf(f"z_p1{i}", [64, 2048], F32) for i in range(2)]
    bc_p1 = [p_sbuf(f"bc_p1{i}", [32, 2048], BF16) for i in range(2)]
    sfp = [p_sbuf(f"sfp{i}", [64, 2048], F32) for i in range(3)]
    p_trA = p_psum("p_trA", [128, 2048], BF16)
    p_trB = p_psum("p_trB", [128, 2048], BF16)
    p_pj = p_psum("p_pj", [96, 2048], F32)

    x_flat = x_h.rearrange("b l d -> (b l) d")

    for g_i in range(NTs):
        with nc.Block(no_gpsimd_drain=True) as blk:
            @blk.sync
            def _(s):
                for ssi in range(NSUB):
                    t0 = g_i * TT + ssi * 128
                    dma(s, xt[g_i % 2][:, ssi * D:(ssi + 1) * D],
                        x_flat[t0:t0 + 128, :], "qs")
                s.wait_ge(qs_h, qn["qs"])
        xcur = xt[g_i % 2]
        b_i = g_i // 4
        tbase = (g_i % 4) * TT
        with nc.Block(no_gpsimd_drain=True) as blk:
            @blk.scalar
            def _(sc):
                amark(sc.activation(sqb[:], xcur[:], AF.Square))
            @blk.vector
            def _(v):
                v.reduce_sum(stat[:, 0:4],
                             xcur.ap().rearrange("p (s d) -> p s d", s=NSUB),
                             axis=AX.X)
        with nc.Block(no_gpsimd_drain=True) as blk:
            @blk.vector
            def _(v):
                v.wait_ge(ea_h, ean[0])
                v.reduce_sum(stat[:, 4:8],
                             sqb.ap().rearrange("p (s d) -> p s d", s=NSUB),
                             axis=AX.X)
        with nc.Block(no_gpsimd_drain=True) as blk:
            @blk.vector
            def _(v):
                v.tensor_scalar_mul(stat[:, 0:4], stat[:, 0:4], 1.0 / D)
                v.tensor_scalar_mul(stat[:, 4:8], stat[:, 4:8], 1.0 / D)
                v.drain()
                v.tensor_mul(stat[:, 8:12], stat[:, 0:4], stat[:, 0:4])
                v.drain()
                dmark(v.tensor_sub(stat[:, 8:12], stat[:, 4:8], stat[:, 8:12]))
        with nc.Block(no_gpsimd_drain=True) as blk:
            @blk.scalar
            def _(sc):
                sc.wait_ge(ed_h, edn[0])
                lnsrc = bass.AP(stat.ap().tensor, 8, [[40, 128], [0, 128], [1, 4]])
                sc.activation(junk.ap().rearrange("p (r c) -> p r c", c=4), lnsrc,
                              AF.Ln, bias=cc[:, 0:1])
                sc.drain()
                amark(sc.activation(rstdw[:], junk[:], AF.Exp, scale=cc[:, 1:2]))
        with nc.Block(no_gpsimd_drain=True) as blk:
            @blk.vector
            def _(v):
                v.wait_ge(ea_h, ean[0])
                li = None
                for ssi in range(NSUB):
                    li = v.tensor_scalar(xn1T[:, ssi * D:(ssi + 1) * D],
                                         xcur[:, ssi * D:(ssi + 1) * D],
                                         stat[:, ssi:ssi + 1], rstdw[:, ssi:ssi + 1],
                                         ALU.subtract, ALU.mult)
                dmark(li)
            @blk.gpsimd
            def _(g):
                dma(g, s_dbg[g_i, :, 0:16], stat[:, 0:16], "qg")
                g.wait_ge(qg_h, qn["qg"])
        for half in range(2):
            subs = (0, 1) if half == 0 else (2, 3)
            with nc.Block(no_gpsimd_drain=True) as blk:
                @blk.tensor
                def _(t):
                    t.wait_ge(ed_h, edn[0])
                    tr = None
                    for j, ssi in enumerate(subs):
                        for c in range(CH):
                            pt = p_trA if c < 4 else p_trB
                            col = (j * 4 + (c % 4)) * 128
                            tr = t.transpose(pt[:, col:col + 128],
                                             xn1T[:, ssi * D + c * 128:ssi * D + (c + 1) * 128],
                                             idm[:])
                    pmark(tr)
            with nc.Block(no_gpsimd_drain=True) as blk:
                @blk.scalar
                def _(sc):
                    sc.wait_ge(ep_h, epn[0])
                    for j, ssi in enumerate(subs):
                        for lo, pt in ((0, p_trA), (4, p_trB)):
                            src0 = pt.ap()[0:64, j * 512:(j + 1) * 512]\
                                .rearrange("p (c t) -> p c t", c=4)
                            src1 = pt.ap()[64:128, j * 512:(j + 1) * 512]\
                                .rearrange("p (c t) -> p c t", c=4)
                            dsts = xf.ap()[0:64, ssi * 2048:(ssi + 1) * 2048]\
                                .rearrange("p (h t) -> p h t", h=16)
                            sc.activation(dsts[:, 2 * lo:2 * lo + 7:2, :], src0, AF.Copy)
                            amark(sc.activation(dsts[:, 2 * lo + 1:2 * lo + 8:2, :],
                                                src1, AF.Copy))
        for ssi in range(NSUB):
            with nc.Block(no_gpsimd_drain=True) as blk:
                @blk.tensor
                def _(t):
                    t.wait_ge(ea_h, ean[0])
                    mm = None
                    for h_i in range(16):
                        mm = t.matmul(p_pj[:, h_i * 128:(h_i + 1) * 128], lhsT_b[:],
                                      xf[0:64, ssi * 2048 + h_i * 128:ssi * 2048 + (h_i + 1) * 128],
                                      start=True, stop=True)
                    pmark(mm)
            with nc.Block(no_gpsimd_drain=True) as blk:
                @blk.scalar
                def _(sc):
                    sc.wait_ge(ep_h, epn[0])
                    sc.activation(sfp[0][:], p_pj[0:64, :], AF.Abs, bias=bdt_c[:])
                @blk.vector
                def _(v):
                    v.wait_ge(ep_h, epn[0])
                    v.tensor_scalar(sfp[1][:], p_pj[0:64, :], bdt_c[:], 0.0,
                                    ALU.add, ALU.max)
                    dmark(v.tensor_copy(bc_p1[ssi % 2][:], p_pj[64:96, :]))
            with nc.Block(no_gpsimd_drain=True) as blk:
                @blk.scalar
                def _(sc):
                    sc.activation(sfp[2][:], sfp[0][:], AF.Exp, scale=cc[0:64, 2:3])
                    sc.drain()
                    amark(sc.activation(sfp[0][:], sfp[2][:], AF.Ln, bias=1.0))
            with nc.Block(no_gpsimd_drain=True) as blk:
                @blk.vector
                def _(v):
                    v.wait_ge(ea_h, ean[0])
                    v.tensor_add(dt_p1[ssi % 2][:], sfp[0][:], sfp[1][:])
                    v.drain()
                    dmark(v.tensor_mul(z_p1[ssi % 2][:], dt_p1[ssi % 2][:],
                                       xf[0:64, ssi * 2048:(ssi + 1) * 2048]))
            with nc.Block(no_gpsimd_drain=True) as blk:
                @blk.gpsimd
                def _(g):
                    g.wait_ge(ed_h, edn[0])
                    t0 = tbase + ssi * 128
                    dma(g, bass.AP(s_dt.tensor, b_i * (H * DH * L) + t0,
                                   [[L, 64], [DH * L, 16], [1, 128]]),
                        dt_p1[ssi % 2].ap().rearrange("p (h t) -> p h t", h=16), "qg")
                    dma(g, bass.AP(s_z.tensor, b_i * (H * DH * L) + t0,
                                   [[L, 64], [DH * L, 16], [1, 128]]),
                        z_p1[ssi % 2].ap().rearrange("p (h t) -> p h t", h=16), "qg")
                    dma(g, bass.AP(s_bc.tensor, b_i * (H * 2 * DS * L) + t0,
                                   [[L, 16], [2 * DS * L, 16], [1, 128]]),
                        bc_p1[ssi % 2].ap()[0:16, :].rearrange("p (h t) -> p h t", h=16),
                        "qg")
                    dma(g, bass.AP(s_bc.tensor, b_i * (H * 2 * DS * L) + DS * L + t0,
                                   [[L, 16], [2 * DS * L, 16], [1, 128]]),
                        bc_p1[ssi % 2].ap()[16:32, :].rearrange("p (h t) -> p h t", h=16),
                        "qg")
                    g.wait_ge(qg_h, qn["qg"])
    p1_st.close()

    p2_st = contextlib.ExitStack()
    p_sbuf, p_psum = mk(p2_st)
    dt2 = [p_sbuf(f"dt2_{i}", [128, L], F32) for i in range(2)]
    z2 = [p_sbuf(f"z2_{i}", [128, L], F32) for i in range(2)]
    Bb = [p_sbuf(f"Bb{i}", [128, L], BF16) for i in range(4)]
    Cb = [p_sbuf(f"Cb{i}", [128, L], BF16) for i in range(6)]
    a_s = [p_sbuf(f"a_s{i}", [128, L], F32) for i in range(4)]
    u_s = [(p_psum if i == 0 else p_sbuf)(f"u_s{i}", [128, L], F32)
           for i in range(4)]
    h_s = [p_sbuf(f"h_s{i}", [128, L], F32) for i in range(4)]
    hc_s = [p_sbuf(f"hc_s{i}", [128, L], F32) for i in range(2)]
    y_e = p_psum("y_e", [128, L], F32)
    y_o = p_sbuf("y_o", [128, L], F32)
    ybf = [p_sbuf(f"ybf{i}", [128, L], BF16) for i in range(2)]

    def bc_ap(b_i, hp, cb, n):
        off = b_i * (H * 2 * DS * L) + (2 * hp) * (2 * DS * L) + cb * (DS * L) + n * L
        return bass.AP(s_bc.tensor, off, [[2 * DS * L, 2], [0, 64], [1, L]])

    def dtz_ap(tns, b_i, hp):
        off = b_i * (H * DH * L) + (2 * hp) * (DH * L)
        return bass.AP(tns.tensor, off, [[DH * L, 2], [L, 64], [1, L]])

    NIT = NITs
    NPAIR = NIT * 8
    rec_b = {}
    rec_c = {}
    rec_dtz = {}
    rec_y = {}
    rec_a = {}
    rec_h = {}
    rec_hc2 = {}
    rec_yb = {}
    rec_ev = {}

    def pair_of(j):
        return j // 8, j % 8

    with nc.Block(no_gpsimd_drain=True) as blk:
        @blk.sync
        def _(s):
            dma(s, dt2[0][:], dtz_ap(s_dt, 0, 0), "qs")
            dma(s, z2[0][:], dtz_ap(s_z, 0, 0), "qs")
            for n in (0, 1):
                dma(s, Bb[n % 4][:], bc_ap(0, 0, 0, n), "qs")
            rec_dtz[0] = qn["qs"]
            rec_b[0] = qn["qs"]
        @blk.gpsimd
        def _(g):
            for n in (0, 1):
                dma(g, Cb[n % 4][:], bc_ap(0, 0, 1, n), "qg")
            rec_c[0] = qn["qg"]

    for j in range(NPAIR + 2):
        with nc.Block(no_gpsimd_drain=True) as blk:
            @blk.scalar
            def _(sc):
                if j < NPAIR:
                    it, pr = pair_of(j)
                    if pr == 0:
                        sc.wait_ge(qs_h, rec_dtz[it])
                    ai = None
                    for n in (2 * pr, 2 * pr + 1):
                        col = (it % HP) * 16 + n
                        ai = sc.activation(a_s[n % 4][:], dt2[it % 2][:], AF.Exp,
                                           scale=ascale[:, col:col + 1])
                    amark(ai)
                    rec_a[j] = ean[0]
            @blk.vector
            def _(v):
                last = None
                merge_only = True
                if j < NPAIR:
                    it, pr = pair_of(j)
                    v.wait_ge(qs_h, rec_b[j])
                    if pr == 0:
                        v.wait_ge(qs_h, rec_dtz[it])
                    for n in (2 * pr, 2 * pr + 1):
                        last = v.tensor_mul(u_s[n % 4][:], z2[it % 2][:],
                                            Bb[n % 4][:])
                        merge_only = False
                    last.then_inc(ev_h, 1)
                    evn[0] += 1
                if 1 <= j <= NPAIR:
                    p = j - 1
                    it, pr = pair_of(p)
                    v.wait_ge(ea_h, rec_a[p])
                    if (p - 2, 1) in rec_hc2:
                        v.wait_ge(eo_h, rec_hc2[(p - 2, 1)])
                    for n in (2 * pr, 2 * pr + 1):
                        last = v.tensor_tensor_scan(h_s[n % 4][:], a_s[n % 4][:],
                                                    u_s[n % 4][:], 0.0,
                                                    ALU.mult, ALU.add)
                        merge_only = False
                    dmark(last)
                    rec_h[p] = edn[0]
                if 2 <= j <= NPAIR + 1:
                    p = j - 2
                    it, pr = pair_of(p)
                    if pr == 0:
                        v.wait_ge(eo_h, eon[0] + 1)
                        last = v.tensor_copy(y_e[:], hc_s[0][:])
                        v.wait_ge(eo_h, eon[0] + 2)
                        last = v.tensor_copy(y_o[:], hc_s[1][:])
                        merge_only = False
                    else:
                        v.wait_ge(eo_h, eon[0] + 1)
                        last = v.tensor_add(y_e[:], y_e[:], hc_s[0][:])
                        v.wait_ge(eo_h, eon[0] + 2)
                        last = v.tensor_add(y_o[:], y_o[:], hc_s[1][:])
                        merge_only = False
                    if pr == 7:
                        v.drain()
                        if it >= 2:
                            v.wait_ge(qg_h, rec_y[it - 2])
                        last = v.tensor_add(ybf[it % 2][:], y_e[:], y_o[:])
                        dmark(last)
                        rec_yb[it] = edn[0]
                rec_ev[j] = evn[0]
            @blk.sync
            def _(s):
                if j < NPAIR:
                    it_a, pr_a = pair_of(j)
                    if pr_a == 1 and it_a + 1 < NIT:
                        dma(s, dt2[(it_a + 1) % 2][:],
                            dtz_ap(s_dt, (it_a + 1) // HP, (it_a + 1) % HP), "qs")
                        dma(s, z2[(it_a + 1) % 2][:],
                            dtz_ap(s_z, (it_a + 1) // HP, (it_a + 1) % HP), "qs")
                        rec_dtz[it_a + 1] = qn["qs"]
                jn = j + 1
                if jn < NPAIR:
                    if j >= 1:
                        s.wait_ge(ev_h, rec_ev[j - 1])
                    itn, prn = pair_of(jn)
                    for n in (2 * prn, 2 * prn + 1):
                        dma(s, Bb[n % 4][:], bc_ap(itn // HP, itn % HP, 0, n), "qs")
                    rec_b[jn] = qn["qs"]
            @blk.gpsimd
            def _(g):
                if 2 <= j <= NPAIR + 1:
                    p = j - 2
                    it, pr = pair_of(p)
                    g.wait_ge(ed_h, rec_h[p])
                    g.wait_ge(qg_h, rec_c[p])
                    for i, n in enumerate((2 * pr, 2 * pr + 1)):
                        omark(g.tensor_tensor(hc_s[i][:], h_s[n % 4][:],
                                              Cb[(2 * p + i) % 6][:], ALU.mult))
                        rec_hc2[(p, i)] = eon[0]
                jn = j + 1
                if jn < NPAIR:
                    if (jn - 3, 1) in rec_hc2:
                        g.wait_ge(eo_h, rec_hc2[(jn - 3, 1)])
                    itn, prn = pair_of(jn)
                    for i, n in enumerate((2 * prn, 2 * prn + 1)):
                        dma(g, Cb[(2 * jn + i) % 6][:],
                            bc_ap(itn // HP, itn % HP, 1, n), "qg")
                    rec_c[jn] = qn["qg"]
                if 2 <= j <= NPAIR + 1:
                    p = j - 2
                    it, pr = pair_of(p)
                    if pr == 7:
                        g.wait_ge(ed_h, rec_yb[it])
                        dma(g, bass.AP(s_y.tensor,
                                       (it % HP) * (128 * TOK) + (it // HP) * L,
                                       [[TOK, 128], [1, L]]), ybf[it % 2][:], "qg")
                        rec_y[it] = qn["qg"]
    with nc.Block(no_gpsimd_drain=True) as blk:
        @blk.gpsimd
        def _(g):
            g.wait_ge(qg_h, qn["qg"])
    p2_st.close()

    a1_st = contextlib.ExitStack()
    p_sbuf, p_psum = mk(a1_st)
    tmpB = p_sbuf("tmpB", [128, 2048], F32)
    sqb = p_sbuf("sqb_a1", [128, 4 * D], F32)
    wout_sb = p_sbuf("wout_sb", [128, CH * D], BF16)
    ytl = [p_sbuf(f"ytl{i}", [128, CH * TT], BF16) for i in range(2)]
    xTt = [p_sbuf(f"xTt{i}", [128, NSUB * D], F32) for i in range(2)]
    aout_f = p_sbuf("aout_f", [128, CH * TT], BF16)
    aoutT = p_sbuf("aoutT", [128, NSUB * D], BF16)
    x2T = p_sbuf("x2T", [128, NSUB * D], F32)
    xcT = p_sbuf("xcT", [128, NSUB * D], BF16)
    xqT = p_sbuf("xqT", [128, NSUB * D], BF16)
    xq_f = p_sbuf("xq_f", [128, CH * TT], BF16)
    rgr = p_sbuf("rgr", [8, 128], F32)
    p_wo = [p_psum(f"p_wo{i}", [128, TT], F32) for i in range(4)]
    p3tr = p_psum("p3tr", [128, 2048], BF16)
    p3x = p_psum("p3x", [128, 128], F32)

    for k in range(CH):
        with nc.Block(no_gpsimd_drain=True) as blk:
            @blk.sync
            def _(s):
                dma(s, tmpB[:, 0:1024], woutT_h[k * 128:(k + 1) * 128, :], "qs")
                s.wait_ge(qs_h, qn["qs"])
        with nc.Block(no_gpsimd_drain=True) as blk:
            @blk.vector
            def _(v):
                dmark(v.tensor_copy(wout_sb[:, k * D:(k + 1) * D], tmpB[:, 0:1024]))

    rec_ed_wout = edn[0]
    rec_ytl = {}
    rec_xt = {}
    with nc.Block(no_gpsimd_drain=True) as blk:
        @blk.sync
        def _(s):
            for k in range(CH):
                dma(s, ytl[0][:, k * TT:(k + 1) * TT],
                    bass.AP(s_y.tensor, k * (128 * TOK), [[TOK, 128], [1, TT]]), "qs")
            rec_ytl[0] = qn["qs"]
        @blk.gpsimd
        def _(g):
            for ssi in range(NSUB):
                dma(g, xTt[0][:, ssi * D:(ssi + 1) * D],
                    x_flat[ssi * 128:(ssi + 1) * 128, :], "qg")
            rec_xt[0] = qn["qg"]

    for g_i in range(NTs):
        ycur = ytl[g_i % 2]
        xcurT = xTt[g_i % 2]
        for rnd in range(2):
            with nc.Block(no_gpsimd_drain=True) as blk:
                @blk.tensor
                def _(t):
                    if rnd == 0:
                        t.wait_ge(qs_h, rec_ytl[g_i])
                        t.wait_ge(ed_h, rec_ed_wout)
                    mm = None
                    for dmi in range(4):
                        dm = rnd * 4 + dmi
                        for k in range(CH):
                            mm = t.matmul(p_wo[dmi][:],
                                          wout_sb[:, k * D + dm * 128:k * D + (dm + 1) * 128],
                                          ycur[:, k * TT:(k + 1) * TT],
                                          start=(k == 0), stop=(k == CH - 1))
                    pmark(mm)
                @blk.sync
                def _(s):
                    if rnd == 0 and g_i + 1 < NT:
                        for k in range(CH):
                            dma(s, ytl[(g_i + 1) % 2][:, k * TT:(k + 1) * TT],
                                bass.AP(s_y.tensor, k * (128 * TOK) + (g_i + 1) * TT,
                                        [[TOK, 128], [1, TT]]), "qs")
                        rec_ytl[g_i + 1] = qn["qs"]
                @blk.gpsimd
                def _(g):
                    if rnd == 0 and g_i + 1 < NT:
                        for ssi in range(NSUB):
                            t0 = (g_i + 1) * TT + ssi * 128
                            dma(g, xTt[(g_i + 1) % 2][:, ssi * D:(ssi + 1) * D],
                                x_flat[t0:t0 + 128, :], "qg")
                        rec_xt[g_i + 1] = qn["qg"]
            with nc.Block(no_gpsimd_drain=True) as blk:
                @blk.scalar
                def _(sc):
                    sc.wait_ge(ep_h, epn[0])
                    last = None
                    for dmi in range(4):
                        dm = rnd * 4 + dmi
                        last = sc.activation(aout_f[:, dm * TT:(dm + 1) * TT],
                                             p_wo[dmi][:], AF.Identity,
                                             bias=bout_c[:, dm:dm + 1])
                    amark(last)
        for rnd in range(2):
            subs = (0, 1) if rnd == 0 else (2, 3)
            with nc.Block(no_gpsimd_drain=True) as blk:
                @blk.tensor
                def _(t):
                    t.wait_ge(ea_h, ean[0])
                    tr = None
                    for j, ssi in enumerate(subs):
                        for dm in range(CH):
                            col = (j * 8 + dm) * 128
                            tr = t.transpose(p3tr[:, col:col + 128],
                                             aout_f[:, dm * TT + ssi * 128:dm * TT + (ssi + 1) * 128],
                                             idm[:])
                    pmark(tr)
            with nc.Block(no_gpsimd_drain=True) as blk:
                @blk.scalar
                def _(sc):
                    sc.wait_ge(ep_h, epn[0])
                    last = None
                    for j, ssi in enumerate(subs):
                        last = sc.activation(aoutT[:, ssi * D:(ssi + 1) * D],
                                             p3tr[:, j * D:(j + 1) * D], AF.Copy)
                    amark(last)
        with nc.Block(no_gpsimd_drain=True) as blk:
            @blk.scalar
            def _(sc):
                amark(sc.activation(sqb[:], aoutT[:], AF.Square))
            @blk.vector
            def _(v):
                v.reduce_sum(stat[:, 0:4],
                             aoutT.ap().rearrange("p (s d) -> p s d", s=NSUB), axis=AX.X)
        with nc.Block(no_gpsimd_drain=True) as blk:
            @blk.vector
            def _(v):
                v.wait_ge(ea_h, ean[0])
                v.reduce_sum(stat[:, 4:8],
                             sqb.ap().rearrange("p (s d) -> p s d", s=NSUB), axis=AX.X)
        with nc.Block(no_gpsimd_drain=True) as blk:
            @blk.vector
            def _(v):
                v.tensor_scalar_mul(stat[:, 0:4], stat[:, 0:4], 1.0 / D)
                v.tensor_scalar_mul(stat[:, 4:8], stat[:, 4:8], 1.0 / D)
                v.drain()
                v.tensor_mul(stat[:, 8:12], stat[:, 0:4], stat[:, 0:4])
                v.drain()
                dmark(v.tensor_sub(stat[:, 8:12], stat[:, 4:8], stat[:, 8:12]))
        with nc.Block(no_gpsimd_drain=True) as blk:
            @blk.scalar
            def _(sc):
                sc.wait_ge(ed_h, edn[0])
                lnsrc = bass.AP(stat.ap().tensor, 8, [[40, 128], [0, 128], [1, 4]])
                sc.activation(junk.ap().rearrange("p (r c) -> p r c", c=4), lnsrc,
                              AF.Ln, bias=cc[:, 0:1])
                sc.drain()
                amark(sc.activation(rstdw[:], junk[:], AF.Exp, scale=cc[:, 1:2]))
        with nc.Block(no_gpsimd_drain=True) as blk:
            @blk.vector
            def _(v):
                v.wait_ge(qg_h, rec_xt[g_i])
                v.wait_ge(ea_h, ean[0])
                for ssi in range(NSUB):
                    v.tensor_scalar(aoutT[:, ssi * D:(ssi + 1) * D],
                                    aoutT[:, ssi * D:(ssi + 1) * D],
                                    stat[:, ssi:ssi + 1], rstdw[:, ssi:ssi + 1],
                                    ALU.subtract, ALU.mult)
                v.drain()
                dmark(v.tensor_add(x2T[:], xcurT[:], aoutT[:]))
        with nc.Block(no_gpsimd_drain=True) as blk:
            @blk.gpsimd
            def _(g):
                g.wait_ge(ed_h, edn[0])
                for ssi in range(NSUB):
                    dma(g, s_x2[g_i * NSUB + ssi, :, :], x2T[:, ssi * D:(ssi + 1) * D],
                        "qg")
                g.wait_ge(qg_h, qn["qg"])
            @blk.scalar
            def _(sc):
                sc.wait_ge(ed_h, edn[0])
                amark(sc.activation(sqb[:], x2T[:], AF.Square))
            @blk.vector
            def _(v):
                v.reduce_sum(stat[:, 0:4],
                             x2T.ap().rearrange("p (s d) -> p s d", s=NSUB), axis=AX.X)
        with nc.Block(no_gpsimd_drain=True) as blk:
            @blk.vector
            def _(v):
                v.wait_ge(ea_h, ean[0])
                v.reduce_sum(stat[:, 4:8],
                             sqb.ap().rearrange("p (s d) -> p s d", s=NSUB), axis=AX.X)
        with nc.Block(no_gpsimd_drain=True) as blk:
            @blk.vector
            def _(v):
                v.tensor_scalar_mul(stat[:, 0:4], stat[:, 0:4], 1.0 / D)
                v.tensor_scalar_mul(stat[:, 4:8], stat[:, 4:8], 1.0 / D)
                v.drain()
                v.tensor_mul(stat[:, 8:12], stat[:, 0:4], stat[:, 0:4])
                v.drain()
                dmark(v.tensor_sub(stat[:, 8:12], stat[:, 4:8], stat[:, 8:12]))
        with nc.Block(no_gpsimd_drain=True) as blk:
            @blk.scalar
            def _(sc):
                sc.wait_ge(ed_h, edn[0])
                lnsrc = bass.AP(stat.ap().tensor, 8, [[40, 128], [0, 128], [1, 4]])
                sc.activation(junk.ap().rearrange("p (r c) -> p r c", c=4), lnsrc,
                              AF.Ln, bias=cc[:, 0:1])
                sc.drain()
                amark(sc.activation(rstdw[:], junk[:], AF.Exp, scale=cc[:, 1:2]))
            @blk.vector
            def _(v):
                for ssi in range(NSUB):
                    v.tensor_single_scalar(xcT[:, ssi * D:(ssi + 1) * D],
                                           x2T[:, ssi * D:(ssi + 1) * D],
                                           stat[:, ssi:ssi + 1], ALU.subtract)
        with nc.Block(no_gpsimd_drain=True) as blk:
            @blk.vector
            def _(v):
                v.reduce_max(stat[:, 16:20],
                             xcT.ap().rearrange("p (s d) -> p s d", s=NSUB),
                             axis=AX.X, apply_absolute_value=True)
        with nc.Block(no_gpsimd_drain=True) as blk:
            @blk.vector
            def _(v):
                v.tensor_scalar_max(stat[:, 16:20], stat[:, 16:20], EPS)
        with nc.Block(no_gpsimd_drain=True) as blk:
            @blk.vector
            def _(v):
                v.reciprocal(stat[:, 20:24], stat[:, 16:20])
        with nc.Block(no_gpsimd_drain=True) as blk:
            @blk.vector
            def _(v):
                v.wait_ge(ea_h, ean[0])
                v.tensor_scalar_mul(stat[:, 20:24], stat[:, 20:24], 127.0)
                v.tensor_scalar_mul(stat[:, 16:20], stat[:, 16:20], 1.0 / 127.0)
                v.drain()
                v.wait_ge(ea_h, ean[0])
                v.tensor_mul(stat[:, 16:20], stat[:, 16:20], rstdw[:, 0:4])
                v.drain()
                v.tensor_scalar_mul(stat[:, 24:28], stat[:, 16:20], wq_cols[:, 4:5])
                v.tensor_scalar_mul(stat[:, 28:32], stat[:, 16:20], wq_cols[:, 5:6])
                for ssi in range(NSUB):
                    v.tensor_scalar_mul(sqb[:, ssi * D:(ssi + 1) * D],
                                        xcT[:, ssi * D:(ssi + 1) * D],
                                        stat[:, 20 + ssi:21 + ssi])
                v.drain()
                for hh in range(2):
                    v.tensor_copy(tmpI[:], sqb[:, hh * 2048:(hh + 1) * 2048])
                    v.drain()
                    li2 = v.tensor_copy(xqT[:, hh * 2048:(hh + 1) * 2048], tmpI[:])
                    if hh == 0:
                        v.drain()
                dmark(li2)
        with nc.Block(no_gpsimd_drain=True) as blk:
            @blk.tensor
            def _(t):
                t.wait_ge(ed_h, edn[0])
                pmark(t.transpose(p3x[0:8, :], stat[:, 24:32], idm_f[:]))
        with nc.Block(no_gpsimd_drain=True) as blk:
            @blk.vector
            def _(v):
                v.wait_ge(ep_h, epn[0])
                dmark(v.tensor_copy(rgr[0:8, :], p3x[0:8, :]))
        with nc.Block(no_gpsimd_drain=True) as blk:
            @blk.gpsimd
            def _(g):
                g.wait_ge(ed_h, edn[0])
                dma(g, s_rg[g_i, :, :], rgr[0:8, :], "qg")
                g.wait_ge(qg_h, qn["qg"])
        for rnd in range(2):
            cs = range(4) if rnd == 0 else range(4, 8)
            with nc.Block(no_gpsimd_drain=True) as blk:
                @blk.tensor
                def _(t):
                    t.wait_ge(ed_h, edn[0])
                    tr = None
                    for jc, c in enumerate(cs):
                        for ssi in range(NSUB):
                            col = (jc * 4 + ssi) * 128
                            tr = t.transpose(p3tr[:, col:col + 128],
                                             xqT[:, ssi * D + c * 128:ssi * D + (c + 1) * 128],
                                             idm[:])
                    pmark(tr)
            with nc.Block(no_gpsimd_drain=True) as blk:
                @blk.scalar
                def _(sc):
                    sc.wait_ge(ep_h, epn[0])
                    last = None
                    for jc, c in enumerate(cs):
                        last = sc.activation(xq_f[:, c * TT:(c + 1) * TT],
                                             p3tr[:, jc * TT:(jc + 1) * TT], AF.Copy)
                    amark(last)
        with nc.Block(no_gpsimd_drain=True) as blk:
            @blk.gpsimd
            def _(g):
                g.wait_ge(ea_h, ean[0])
                for c in range(CH):
                    dma(g, bass.AP(s_xq.tensor, c * (128 * TOK) + g_i * TT,
                                   [[TOK, 128], [1, TT]]),
                        xq_f[:, c * TT:(c + 1) * TT], "qg")
                g.wait_ge(qg_h, qn["qg"])
    a1_st.close()

    a2_st = contextlib.ExitStack()
    p_sbuf, p_psum = mk(a2_st)
    HTOK = TOK // 2
    NTH = 2 if SIM_SMALL else NT // 2
    xq_all = p_sbuf("xq_all", [128, CH * HTOK], BF16)
    rgb_all = p_sbuf("rgb_all", [128, HTOK], BF16)
    rub_all = p_sbuf("rub_all", [128, HTOK], BF16)
    tmpA2 = p_sbuf("tmpA2", [128, 1024], F32)
    wslc = [[p_sbuf(f"wslc{w}_{i}", [128, 1024], BF16) for i in range(2)]
            for w in range(2)]
    gat = [p_sbuf(f"gat{i}", [128, 1024], BF16) for i in range(2)]
    gat2 = [p_sbuf(f"gat2_{i}", [128, 1024], BF16) for i in range(2)]
    upv = [p_sbuf(f"upv{i}", [128, 1024], BF16) for i in range(4)]
    guv = [p_sbuf(f"guv{i}", [128, 1024], BF16) for i in range(2)]
    absgu = [p_sbuf(f"absgu{i}", [128, 1024], BF16) for i in range(2)]
    runmx = [p_sbuf(f"runmx{i}", [128, 1024], BF16) for i in range(NTH // 2)]
    trmx = p_sbuf("trmx", [128, 1024], BF16)
    red1 = p_sbuf("red1", [128, 32], F32)
    flmx = p_sbuf("flmx", [32, 128], F32)
    amfin = p_sbuf("amfin", [32, 32], F32)
    amfinT = p_sbuf("amfinT", [32, 32], F32)
    p_g = [p_psum(f"p_g{i}", [128, 1024], F32) for i in range(2)]
    p_u = [p_psum(f"p_u{i}", [128, 1024], F32) for i in range(2)]

    def wslc_ap(tns, m):
        return bass.AP(tns.tensor, m * 128,
                       [[INNER, 128], [128 * INNER, 8], [1, 128]])

    rec_w = {}
    rec_si = {}
    rec_pe = {}
    rec_ga = {}
    rec_gv = {}
    am_store_qg = {}
    rec_ab = {}
    for half in range(1 if SIM_SMALL else 2):
        tb = half * NTH
        with nc.Block(no_gpsimd_drain=True) as blk:
            @blk.sync
            def _(s):
                for c in range(CH):
                    for gg in range(NTH):
                        dma(s, xq_all[:, c * HTOK + gg * TT:c * HTOK + (gg + 1) * TT],
                            bass.AP(s_xq.tensor, c * (128 * TOK) + (tb + gg) * TT,
                                    [[TOK, 128], [1, TT]]), "qs")
                s.wait_ge(qs_h, qn["qs"])
        for gg in range(NTH):
            with nc.Block(no_gpsimd_drain=True) as blk:
                @blk.sync
                def _(s):
                    for ssi in range(NSUB):
                        dma(s, tmpA2[:, ssi * 128:(ssi + 1) * 128],
                            bass.AP(s_rg.tensor, ((tb + gg) * 8 + ssi) * 128,
                                    [[0, 128], [1, 128]]), "qs")
                        dma(s, tmpB2_ap(tmpA2, ssi),
                            bass.AP(s_rg.tensor, ((tb + gg) * 8 + 4 + ssi) * 128,
                                    [[0, 128], [1, 128]]), "qs")
                    s.wait_ge(qs_h, qn["qs"])
            with nc.Block(no_gpsimd_drain=True) as blk:
                @blk.vector
                def _(v):
                    v.tensor_copy(rgb_all[:, gg * TT:(gg + 1) * TT], tmpA2[:, 0:512])
                    v.tensor_copy(rub_all[:, gg * TT:(gg + 1) * TT], tmpA2[:, 512:1024])
        MCHs = 2 if SIM_SMALL else MCH
        NP = MCHs * (NTH // 2)
        def unit(J):
            return J // (NTH // 2), (J % (NTH // 2)) * 2
        with nc.Block(no_gpsimd_drain=True) as blk:
            @blk.sync
            def _(s):
                dma(s, wslc[0][0][:], wslc_ap(s_wgq, 0), "qs")
                dma(s, wslc[1][0][:], wslc_ap(s_wuq, 0), "qs")
                rec_w[(half, 0)] = qn["qs"]
        for J in range(NP + 8):
            with nc.Block(no_gpsimd_drain=True) as blk:
                @blk.sync
                def _(s):
                    if J < NP and J % (NTH // 2) == 0:
                        m = unit(J)[0]
                        if m + 1 < MCH:
                            dma(s, wslc[0][(m + 1) % 2][:], wslc_ap(s_wgq, m + 1), "qs")
                            dma(s, wslc[1][(m + 1) % 2][:], wslc_ap(s_wuq, m + 1), "qs")
                            rec_w[(half, m + 1)] = qn["qs"]
                @blk.tensor
                def _(t):
                    if J < NP:
                        m, gg = unit(J)
                        t.wait_ge(qs_h, rec_w[(half, m)])
                        mm = None
                        for pgp, wsl in ((p_g, wslc[0]), (p_u, wslc[1])):
                            for tt2 in range(2):
                                for k in range(CH):
                                    mm = t.matmul(pgp[J % 2][:, tt2 * TT:(tt2 + 1) * TT],
                                                  wsl[m % 2][:, k * 128:(k + 1) * 128],
                                                  xq_all[:, k * HTOK + (gg + tt2) * TT:
                                                         k * HTOK + (gg + tt2 + 1) * TT],
                                                  start=(k == 0), stop=(k == CH - 1))
                        pmark(mm)
                        rec_pe[J] = epn[0]
                @blk.vector
                def _(v):
                    li = None
                    if 1 <= J < NP + 1:
                        Jp = J - 1
                        v.wait_ge(ep_h, rec_pe[Jp])
                        li = v.tensor_mul(gat[Jp % 2][:], p_g[Jp % 2][:],
                                          rgb_all[:, rgslice(Jp, NTH)])
                        v.tensor_mul(upv[Jp % 4][:], p_u[Jp % 2][:],
                                     rub_all[:, rgslice(Jp, NTH)])
                        rec_ga[Jp] = edn[0] + 1
                    if 3 <= J < NP + 3:
                        Jp = J - 3
                        v.wait_ge(ea_h, rec_si[Jp])
                        li = v.tensor_mul(guv[Jp % 2][:], gat2[Jp % 2][:], upv[Jp % 4][:])
                        rec_gv[Jp] = edn[0] + 1
                    if li is not None:
                        dmark(li)
                    if 5 <= J < NP + 5:
                        Jp = J - 5
                        m, gg = unit(Jp)
                        gi = gg // 2
                        v.wait_ge(ea_h, rec_ab[Jp])
                        if m == 0:
                            v.tensor_copy(runmx[gi][:], absgu[Jp % 2][:])
                        else:
                            v.tensor_max(runmx[gi][:], runmx[gi][:],
                                         absgu[Jp % 2][:])
                @blk.scalar
                def _(sc):
                    if 2 <= J < NP + 2:
                        Jp = J - 2
                        sc.wait_ge(ed_h, rec_ga[Jp])
                        amark(sc.activation(gat2[Jp % 2][:], gat[Jp % 2][:], AF.Silu))
                        rec_si[Jp] = ean[0]
                    if 4 <= J < NP + 4:
                        Jp = J - 4
                        sc.wait_ge(ed_h, rec_gv[Jp])
                        amark(sc.activation(absgu[Jp % 2][:], guv[Jp % 2][:], AF.Abs))
                        rec_ab[Jp] = ean[0]
                @blk.gpsimd
                def _(g):
                    if 4 <= J < NP + 4:
                        Jp = J - 4
                        m, gg = unit(Jp)
                        g.wait_ge(ed_h, rec_gv[Jp])
                        dma(g, bass.AP(s_gu.tensor,
                                       m * (128 * TOK) + (half * NTH + gg) * TT,
                                       [[TOK, 128], [1, 2 * TT]]), guv[Jp % 2][:], "qg")
                        g.wait_ge(qg_h, qn["qg"])
        for gi in range(NTH // 2):
            fi = half * (NTH // 2) + gi
            with nc.Block(no_gpsimd_drain=True) as blk:
                red_mark = edn[0] + 1
                @blk.sync
                def _(s):
                    s.wait_ge(ed_h, red_mark)
                    for a in range(4):
                        dma(s, flmx[:, 32 * a:32 * (a + 1)],
                            red1[32 * a:32 * (a + 1), :], "qs")
                fl_tgt = qn["qs"]
                @blk.vector
                def _(v):
                    v.transpose(trmx[:], runmx[gi][:])
                    v.drain()
                    dmark(v.tensor_reduce(
                        red1[:], trmx.ap().rearrange("p (j v) -> p j v", v=32),
                        axis=AX.X, op=ALU.max))
                    v.wait_ge(qs_h, fl_tgt)
                    if fi >= 1:
                        v.wait_ge(qg_h, am_store_qg[fi - 1])
                    v.tensor_reduce(
                        amfin[:], flmx.ap().rearrange("p (a j) -> p j a", a=4),
                        axis=AX.X, op=ALU.max)
                    v.drain()
                    dmark(v.transpose(amfinT[:], amfin[:]))
                am_mark = edn[0]
                @blk.gpsimd
                def _(g):
                    g.wait_ge(ed_h, am_mark)
                    dma(g, bass.AP(s_am.tensor, (half * NTH + gi * 2) * TT,
                                   [[32, 32], [1, 32]]), amfinT[:], "qg")
                    am_store_qg[fi] = qn["qg"]
                    g.wait_ge(qg_h, qn["qg"])
    a2_st.close()

    b_st = contextlib.ExitStack()
    p_sbuf, p_psum = mk(b_st)
    wd_sb = p_sbuf("wd_sb", [128, MCH * D], BF16)
    gu_all = p_sbuf("gu_all", [128, MCH * TT], BF16)
    guq_all = p_sbuf("guq_all", [128, MCH * TT], BF16)
    rowmx = p_sbuf("rowmx", [1, TT], F32)
    rowmn = p_sbuf("rowmn", [1, TT], F32)
    rowsc = p_sbuf("rowsc", [1, TT], BF16)
    rowrd = p_sbuf("rowrd", [1, TT], BF16)
    scb = p_sbuf("scb", [128, TT], F32)
    rdb = p_sbuf("rdb", [128, TT], F32)
    ffn_f = p_sbuf("ffn_f", [128, CH * TT], BF16)
    ffnT = p_sbuf("ffnT", [128, NSUB * D], BF16)
    x2l = p_sbuf("x2l", [128, NSUB * D], F32)
    outT = p_sbuf("outT", [128, NSUB * D], F32)
    p_dn = [p_psum(f"p_dn{i}", [128, TT], F32) for i in range(4)]
    p3trb = p_psum("p3trb", [128, 2048], BF16)
    p_b2 = p_psum("p_b2", [128, 2 * TT], F32)

    out_flat = out_h.rearrange("b l d -> (b l) d")
    rec_addf = {}
    rec_amq = {}
    rec_guk = {}

    with nc.Block(no_gpsimd_drain=True) as blk:
        @blk.sync
        def _(s):
            for k in range(MCH):
                dma(s, wd_sb[:, k * D:(k + 1) * D],
                    bass.AP(s_wdq.tensor, k * 128 * D, [[D, 128], [1, D]]), "qs")
            s.wait_ge(qs_h, qn["qs"])

    for g_i in range(NTs):
        with nc.Block(no_gpsimd_drain=True) as blk:
            @blk.sync
            def _(s):
                for k in range(MCH):
                    dma(s, gu_all[:, k * TT:(k + 1) * TT],
                        bass.AP(s_gu.tensor, k * (128 * TOK) + g_i * TT,
                                [[TOK, 128], [1, TT]]), "qs")
                    if k % 2 == 1:
                        rec_guk[(g_i, k // 2)] = qn["qs"]
            @blk.gpsimd
            def _(g):
                if g_i >= 1:
                    g.wait_ge(ed_h, rec_addf[g_i - 1])
                for ssi in range(NSUB):
                    dma(g, x2l[:, ssi * D:(ssi + 1) * D],
                        s_x2[g_i * NSUB + ssi, :, :], "qg")
                dma(g, rowmx[:], bass.AP(s_am.tensor, g_i * TT,
                                         [[TT, 1], [1, TT]]), "qg")
                rec_amq[g_i] = qn["qg"]
                g.wait_ge(qg_h, qn["qg"])
            @blk.vector
            def _(v):
                v.wait_ge(qg_h, rec_amq[g_i])
                v.tensor_scalar_max(rowmx[:], rowmx[:], EPS)
                v.drain()
                v.reciprocal(rowmn[:], rowmx[:])
                v.drain()
                v.tensor_scalar_mul(rowmn[:], rowmn[:], 127.0)
                v.drain()
                v.tensor_copy(rowsc[:], rowmn[:])
                v.tensor_scalar_mul(rowmx[:], rowmx[:], 1.0 / 127.0)
                v.drain()
                v.tensor_scalar_mul(rowmx[:], rowmx[:], wq_cols[0:1, 6:7])
                v.drain()
                dmark(v.tensor_copy(rowrd[:], rowmx[:]))
        with nc.Block(no_gpsimd_drain=True) as blk:
            @blk.tensor
            def _(t):
                t.wait_ge(ed_h, edn[0])
                t.matmul(p_b2[:, 0:TT], onesr[:], rowsc[0:1, :],
                         start=True, stop=True)
                pmark(t.matmul(p_b2[:, TT:2 * TT], onesr[:], rowrd[0:1, :],
                               start=True, stop=True))
            EO0 = eon[0]
            @blk.vector
            def _(v):
                v.wait_ge(ep_h, epn[0])
                v.tensor_copy(scb[:], p_b2[:, 0:TT])
                v.drain()
                dmark(v.tensor_copy(rdb[:], p_b2[:, TT:2 * TT]))
                ED0 = edn[0]
                for q in range(1, 18):
                    if q < 17:
                        v.wait_ge(eo_h, EO0 + min(q, 16))
                        for kk in range(2):
                            v.tensor_copy(tmpI[:, ((q - 1) % 2) * 1024 + kk * TT:
                                               ((q - 1) % 2) * 1024 + (kk + 1) * TT],
                                          outT[:, ((q - 1) % 2) * 2048 + kk * TT:
                                               ((q - 1) % 2) * 2048 + (kk + 1) * TT])
                    if q >= 2:
                        for kk in range(2):
                            k = (q - 2) * 2 + kk
                            v.tensor_copy(guq_all[:, k * TT:(k + 1) * TT],
                                          tmpI[:, ((q - 2) % 2) * 1024 + kk * TT:
                                               ((q - 2) % 2) * 1024 + (kk + 1) * TT])
                        dmarkd(v)
                    elif q == 1:
                        dmarkd(v)
            ED0 = edn[0] - 17
            @blk.gpsimd
            def _(g):
                for q in range(16):
                    g.wait_ge(qs_h, rec_guk[(g_i, q)])
                    g.wait_ge(ed_h, ED0 if q < 2 else ED0 + q - 1)
                    last = None
                    for kk in range(2):
                        k = q * 2 + kk
                        last = g.tensor_tensor(outT[:, (q % 2) * 2048 + kk * TT:
                                                    (q % 2) * 2048 + (kk + 1) * TT],
                                               gu_all[:, k * TT:(k + 1) * TT],
                                               scb[:], ALU.mult)
                    omark(last)
        with nc.Block(no_gpsimd_drain=True) as blk:
            EVD = edn[0]
            @blk.tensor
            def _(t):
                for rnd in range(2):
                    t.wait_ge(ed_h, edn[0] if rnd == 0 else EVD + 1)
                    mm = None
                    for dmi in range(4):
                        dm = rnd * 4 + dmi
                        for k in range(MCH):
                            mm = t.matmul(p_dn[dmi][:],
                                          wd_sb[:, k * D + dm * 128:k * D + (dm + 1) * 128],
                                          guq_all[:, k * TT:(k + 1) * TT],
                                          start=(k == 0), stop=(k == MCH - 1))
                    pmark(mm)
            EPD = epn[0] - 2
            @blk.vector
            def _(v):
                for rnd in range(2):
                    v.wait_ge(ep_h, EPD + 1 + rnd)
                    for dmi in range(4):
                        dm = rnd * 4 + dmi
                        v.tensor_mul(ffn_f[:, dm * TT:(dm + 1) * TT],
                                     p_dn[dmi][:], rdb[:])
                    dmarkd(v)
        with nc.Block(no_gpsimd_drain=True) as blk:
            EAT = ean[0]
            @blk.tensor
            def _(t):
                for rnd in range(2):
                    t.wait_ge(ed_h, edn[0] if rnd == 0 else EVD + 2)
                    if rnd == 1:
                        t.wait_ge(ea_h, EAT + 1)
                    subs = (0, 1) if rnd == 0 else (2, 3)
                    tr = None
                    for j, ssi in enumerate(subs):
                        for dm in range(CH):
                            col = (j * 8 + dm) * 128
                            tr = t.transpose(p3trb[:, col:col + 128],
                                             ffn_f[:, dm * TT + ssi * 128:dm * TT + (ssi + 1) * 128],
                                             idm[:])
                    pmark(tr)
            EPT = epn[0] - 2
            @blk.scalar
            def _(sc):
                for rnd in range(2):
                    sc.wait_ge(ep_h, EPT + 1 + rnd)
                    subs = (0, 1) if rnd == 0 else (2, 3)
                    last = None
                    for j, ssi in enumerate(subs):
                        last = sc.activation(ffnT[:, ssi * D:(ssi + 1) * D],
                                             p3trb[:, j * D:(j + 1) * D], AF.Copy)
                    amark(last)
        with nc.Block(no_gpsimd_drain=True) as blk:
            EA4 = ean[0]
            E4 = edn[0]
            @blk.scalar
            def _(sc):
                amark(sc.activation(gu_all.ap()[:, 0:4 * D], ffnT[:], AF.Square))
                sc.wait_ge(ed_h, E4 + 1)
                lnsrc = bass.AP(stat.ap().tensor, 8, [[40, 128], [0, 128], [1, 4]])
                sc.activation(junk.ap().rearrange("p (r c) -> p r c", c=4), lnsrc,
                              AF.Ln, bias=cc[:, 0:1])
                sc.drain()
                amark(sc.activation(rstdw[:], junk[:], AF.Exp, scale=cc[:, 1:2]))
            @blk.vector
            def _(v):
                v.wait_ge(ea_h, EA4)
                v.reduce_sum(stat[:, 0:4],
                             ffnT.ap().rearrange("p (s d) -> p s d", s=NSUB),
                             axis=AX.X)
                v.wait_ge(ea_h, EA4 + 1)
                v.reduce_sum(stat[:, 4:8],
                             gu_all.ap()[:, 0:4 * D].rearrange("p (s d) -> p s d", s=NSUB),
                             axis=AX.X)
                v.drain()
                v.tensor_scalar_mul(stat[:, 0:4], stat[:, 0:4], 1.0 / D)
                v.tensor_scalar_mul(stat[:, 4:8], stat[:, 4:8], 1.0 / D)
                v.drain()
                v.tensor_mul(stat[:, 8:12], stat[:, 0:4], stat[:, 0:4])
                v.drain()
                dmark(v.tensor_sub(stat[:, 8:12], stat[:, 4:8], stat[:, 8:12]))
                v.wait_ge(ea_h, EA4 + 2)
                for ssi in range(NSUB):
                    v.tensor_scalar(ffnT[:, ssi * D:(ssi + 1) * D],
                                    ffnT[:, ssi * D:(ssi + 1) * D],
                                    stat[:, ssi:ssi + 1], rstdw[:, ssi:ssi + 1],
                                    ALU.subtract, ALU.mult)
                v.drain()
                dmark(v.tensor_add(outT[:], x2l[:], ffnT[:]))
                rec_addf[g_i] = edn[0]
            @blk.gpsimd
            def _(g):
                g.wait_ge(ed_h, rec_addf[g_i])
                for ssi in range(NSUB):
                    t0 = g_i * TT + ssi * 128
                    dma(g, out_flat[t0:t0 + 128, :], outT[:, ssi * D:(ssi + 1) * D],
                        "qg")
                if dbg:
                    dma(g, s_dbg2[g_i, :, :], ffn_f[:], "qg")
                    dma(g, s_dbg3[g_i, :, :], ffnT[:], "qg")
                g.wait_ge(qg_h, qn["qg"])

    with nc.Block(no_gpsimd_drain=True) as blk:
        @blk.gpsimd
        def _(g):
            g.wait_ge(qg_h, qn["qg"])
            g.wait_ge(qs_h, qn["qs"])
        @blk.vector
        def _(v):
            v.wait_ge(qg_h, qn["qg"])
    b_st.close()

    g_st.close()
    return nc


def tmpB2_ap(tmpA2, ssi):
    return tmpA2[:, 512 + ssi * 128:512 + (ssi + 1) * 128]


def rgslice(Jp, NTH):
    gg = (Jp % (NTH // 2)) * 2
    return slice(gg * TT, (gg + 2) * TT)


def kernel(**inputs):
    x = np.ascontiguousarray(inputs["L0_attn_x"])
    A_log = np.asarray(inputs["L0_attn_A_log"])
    Wx = np.ascontiguousarray(inputs["L0_attn_Wx"]).astype(np.float32, copy=False)
    Wdt = np.asarray(inputs["L0_attn_Wdt"])
    bdt = np.asarray(inputs["L0_attn_bdt"])
    Wout = np.asarray(inputs["L0_attn_Wout"])
    bout = np.asarray(inputs["L0_attn_bout"])
    wg = np.asarray(inputs["L0_ffn_w_gate"])
    wu = np.asarray(inputs["L0_ffn_w_up"])
    wd = np.asarray(inputs["L0_ffn_w_down"])

    acols = np.empty((128, 128), dtype=np.float32)
    for hp in range(HP):
        for n in range(16):
            acols[0:64, hp * 16 + n] = A_log[2 * hp, n]
            acols[64:128, hp * 16 + n] = A_log[2 * hp + 1, n]
    in_common = {
        "acols": acols,
        "bdt": np.ascontiguousarray(bdt.reshape(64, 1), dtype=np.float32),
        "wx": Wx,
        "wdtT": np.ascontiguousarray(Wdt.T),
        "wxbcT": np.ascontiguousarray(Wx[64:96, :].T),
        "woutT": np.ascontiguousarray(Wout.T),
        "boutc": np.ascontiguousarray(bout.reshape(CH, 128).T),
        "wgT": np.ascontiguousarray(wg.T),
        "wuT": np.ascontiguousarray(wu.T),
        "wdT": np.ascontiguousarray(wd.T),
        "onesr": np.ones((1, 128), dtype=ml_dtypes.bfloat16),
        "onesc": np.ones((128, 1), dtype=ml_dtypes.bfloat16),
        "idm": np.eye(128, dtype=ml_dtypes.bfloat16),
    }
    nc = build_nc()
    in_maps = [dict(in_common, x=np.ascontiguousarray(x[c * B:(c + 1) * B]))
               for c in range(NCORES)]
    res = run_bass_kernel_spmd(nc, in_maps, core_ids=list(range(NCORES)))
    return np.concatenate([r["out"] for r in res.results], axis=0)
